# version 14
# baseline (speedup 1.0000x reference)
import numpy as np
import concourse.bass as bass
import concourse.mybir as mybir
from concourse.bass_utils import run_bass_kernel_spmd

F32 = mybir.dt.float32
BF16 = mybir.dt.bfloat16
U8 = mybir.dt.uint8
ALU = mybir.AluOpType
ACT = mybir.ActivationFunctionType
AX = mybir.AxisListType

NEG = -1.0e30
EPS = 1e-5


def cfg_full():
    return dict(B=4, S=4096, H=2048, NH=16, I=1408, E=8, WINDOW=256, OWN=2048,
                NCORES=8)


def derive(c):
    d = dict(c)
    d["KC"] = c["H"] // 128
    d["PAD"] = c["OWN"]
    d["CTX"] = 2 * c["OWN"]
    d["NCH"] = d["CTX"] // 16
    d["PADC"] = d["PAD"] // 16
    d["QT"] = c["OWN"] // 128
    d["WT"] = (c["OWN"] + c["WINDOW"]) // 128
    d["ICH"] = c["I"] // 128
    d["NE"] = c["E"] + 1
    d["CT"] = d["CTX"] // 128
    d["NG"] = d["CTX"] // 512
    d["NPASS"] = c["OWN"] // 512
    d["JG"] = c["H"] // 512
    d["CVW"] = (d["NCH"] + 127) // 128
    d["SCALE"] = 1.0 / np.sqrt(128.0)
    return d


class Sem:
    def __init__(self, h):
        self.h = h
        self.n = 0


class Build:
    def __init__(self, nc):
        self.nc = nc
        self.P = {"t": [], "v": [], "s": [], "g": [], "y": []}

    def emit(self, stream, meth, *args, waits=(), inc=None, inck=1, **kw):
        waits = tuple(w for w in waits if w is not None)

        def f(eng, meth=meth, args=args, kw=kw, waits=waits, inc=inc, inck=inck):
            for h, v in waits:
                if v > 0:
                    eng.wait_ge(h, v)
            ins = getattr(eng, meth)(*args, **kw)
            if inc is not None:
                ins.then_inc(inc, inck)
        self.P[stream].append(f)

    def dma(self, stream, dst, src, sem, waits=()):
        sem.n += 16
        self.emit(stream, "dma_start", dst, src, waits=waits, inc=sem.h, inck=16)
        return sem.n

    def wait(self, stream, h, v):
        def f(eng, h=h, v=v):
            if v > 0:
                eng.wait_ge(h, v)
        self.P[stream].append(f)




def build_program(cfg, debug=False):
    d = derive(cfg)
    H, NH, I, E = cfg["H"], cfg["NH"], cfg["I"], cfg["E"]
    OWN, WINDOW = cfg["OWN"], cfg["WINDOW"]
    KC, CTX, PAD, NCH, PADC = d["KC"], d["CTX"], d["PAD"], d["NCH"], d["PADC"]
    QT, WT, ICH, NE, CT, NG = d["QT"], d["WT"], d["ICH"], d["NE"], d["CT"], d["NG"]
    NPASS, JG, CVW = d["NPASS"], d["JG"], d["CVW"]
    SCALE = d["SCALE"]
    NSQ = H // 512

    nc = bass.Bass("TRN2", target_bir_lowering=False, detect_race_conditions=False)

    def din(name, shape):
        return nc.dram_tensor(name, list(shape), F32, kind="ExternalInput").ap()

    x_ctx = din("x_ctx", (CTX, H))
    wqT = din("wqT", (H, H))
    wkT = din("wkT", (H, H))
    wvT = din("wvT", (H, H))
    woT = din("woT", (H, H))
    ws1 = din("ws1", (H, I))
    ws2 = din("ws2", (I, H))
    w1 = din("w1", (E, H, I))
    w2 = din("w2", (E, I, H))
    wrt = din("wrt", (H, E))
    kcos = din("kcos", (128, CTX))
    ks3 = din("ks3", (128, CTX))
    qcos = din("qcos", (128, OWN))
    qs3 = din("qs3", (128, OWN))
    cmask = din("cmask", (QT, 128, NCH))
    wmask = din("wmask", (QT, 128, 384))
    identin = din("identin", (128, 128))
    pswapin = din("pswapin", (128, 128))

    y = nc.dram_tensor("y", [OWN, H], F32, kind="ExternalOutput").ap()

    def dint(name, shape, dt=BF16):
        kind = "ExternalOutput" if debug else "Internal"
        return nc.dram_tensor(name, list(shape), dt, kind=kind).ap()

    lnx_h = dint("lnx_h", (CT, KC, 128, 128))
    kwin_h = dint("kwin_h", (NH, 128, OWN + WINDOW))
    vwin_h = dint("vwin_h", (NH, 128, WT * 128))
    q_hbm = dint("q_hbm", (NH, 128, OWN))
    lnx2_h = dint("lnx2_h", (QT, KC, 128, 128))
    r1_h = dint("r1_h", (OWN, H), F32)
    w1bf_h = dint("w1bf_h", (NE, H, I))
    w2bf_h = dint("w2bf_h", (NE, I, H))
    wobf_h = dint("wobf_h", (H, H))
    wrtbf_h = dint("wrtbf_h", (H, E))

    HEAP_BASE = 16512
    NB32 = 51200
    NB16 = 161628
    assert HEAP_BASE + NB32 + NB16 <= 229340
    h32_t = nc.alloc_sbuf_tensor_at("heap32", [128, NB32 // 4], F32,
                                    offset=HEAP_BASE)
    h16_t = nc.alloc_sbuf_tensor_at("heap16", [128, NB16 // 2], BF16,
                                    offset=HEAP_BASE + NB32)
    h32 = h32_t.ap()
    h16 = h16_t.ap()

    off = {F32: 0, BF16: 0}
    himax = {F32: 0, BF16: 0}

    def alloc(cols, dt):
        esz = 4 if dt == F32 else 2
        o = off[dt]
        assert o % 32 == 0
        ap = (h32 if dt == F32 else h16)[:, o // esz:o // esz + cols]
        off[dt] = o + ((cols * esz + 31) // 32) * 32
        himax[dt] = max(himax[dt], off[dt])
        assert off[F32] <= NB32 and off[BF16] <= NB16, \
            f"heap overflow f32={off[F32]} bf16={off[BF16]}"
        return ap

    g_ident = alloc(128, BF16)
    g_pswap = alloc(128, BF16)
    g_ck = alloc(NH * NCH, BF16)
    g_cv = alloc(CVW * NH * 128, BF16)
    g_comb = alloc(QT * 8, F32)
    F32_PH = ((off[F32] + 255) // 256) * 256
    O_OFF = ((off[BF16] + 255) // 256) * 256
    off[BF16] = O_OFF
    g_o = alloc(NH * OWN, BF16)
    PH_BASE = ((off[BF16] + 255) // 256) * 256

    def phase_reset(reuse_o=False):
        off[BF16] = O_OFF if reuse_o else PH_BASE
        off[F32] = F32_PH

    pb_ctx = []
    pb = []
    for i in range(8):
        cm = nc.psum_tensor(f"pbank{i}", [128, 512], F32)
        pb.append(cm.__enter__())
        pb_ctx.append(cm)
    pbf = [p.ap() for p in pb]

    bld = Build(nc)
    semctx = []

    def S(name):
        cm = nc.semaphore(name)
        h = cm.__enter__()
        semctx.append(cm)
        return Sem(h)

    dw = S("dw")
    dm2 = S("dm2")
    sag = [S(f"sag{i}") for i in range(8)]
    dl0, dl1 = S("dl0"), S("dl1")
    gsim = S("gsim")
    dy0, dy1 = S("dy0"), S("dy1")
    ds1 = S("ds1")
    ds2 = S("ds2")
    st0, st1 = S("st0"), S("st1")
    sf0, sf1 = S("sf0"), S("sf1")
    sy = S("sy")
    ten, vec, scl = S("ten"), S("vec"), S("scl")
    phz = S("phz")
    dyp = [dy0, dy1]
    dlp = [dl0, dl1]
    stp = [st0, st1]
    sfp = [sf0, sf1]

    def TEN():
        ten.n += 1
        return ten.n

    def VEC():
        vec.n += 1
        return vec.n

    def SCL():
        scl.n += 1
        return scl.n

    def barrier():
        for stnm, sem in (("t", ten), ("v", vec), ("s", scl)):
            bld.emit(stnm, "nop", waits=[(sem.h, sem.n)], inc=phz.h)
        for stnm in ("g", "y"):
            bld.emit(stnm, "nop", inc=phz.h)
        phz.n += 5
        target = phz.n
        for stnm in ("t", "v", "s", "g", "y"):
            bld.wait(stnm, phz.h, target)

    class TrPipe:

        def __init__(self, banks, evac_stream="s"):
            self.banks = banks
            self.ev = {b: 0 for b in banks}
            self.i = 0
            self.evs = evac_stream

        def go(self, src_ap, dst_ap, w0=128, rhs=None, extra_waits=(),
               evac_waits=()):
            b = self.banks[self.i % 2]
            self.i += 1
            wv = list(extra_waits)
            if self.ev[b]:
                wv.append(((scl.h if self.evs == "s" else vec.h), self.ev[b]))
            bld.emit("t", "matmul", pbf[b][0:w0, 0:128], src_ap,
                     rhs if rhs is not None else g_ident,
                     start=True, stop=True, waits=wv, inc=ten.h)
            trd = TEN()
            bld.emit(self.evs, "copy" if self.evs == "s" else "tensor_copy",
                     dst_ap, pbf[b][0:w0, 0:128],
                     waits=[(ten.h, trd)] + list(evac_waits),
                     inc=(scl.h if self.evs == "s" else vec.h))
            self.ev[b] = SCL() if self.evs == "s" else VEC()
            return self.ev[b]

    bld.dma("g", g_ident, identin, dw)
    bld.dma("g", g_pswap, pswapin, dw)
    CONST_DONE = dw.n

    phase_reset(reuse_o=True)
    w_sb = alloc(KC * H, BF16)
    lnx_g = [alloc(KC * 512, BF16) for _ in range(2)]
    mark = off[BF16]
    rc_t = alloc(CTX, BF16)
    rs_t = alloc(CTX, BF16)
    end_kq = off[BF16]
    off[BF16] = mark
    c_vft = [alloc(512, BF16) for _ in range(2)]
    c_vtk = [alloc(512, BF16) for _ in range(2)]
    c_cvft = alloc(NH * NCH, BF16)
    off[BF16] = max(off[BF16], end_kq)
    b_v1 = [alloc(512, BF16) for _ in range(2)]
    b_t1 = [alloc(512, BF16) for _ in range(2)]
    b_kr = [alloc(512, BF16) for _ in range(2)]
    b_pool = alloc(64, F32)

    def lnx_load(gp, gsl, waits):
        v3 = lnx_g[gp].rearrange("p (kc g c) -> p kc g c", g=4, c=128)
        for j in range(4):
            n = bld.dma("y", v3[:, :, j, :],
                        lnx_h[4 * gsl + j].transpose([1, 0, 2]),
                        dlp[gp], waits=(waits if j == 0 else ()))
        return n

    def qkv_pass(kind, preload=None):
        wsrc = {"k": wkT, "v": wvT, "q": wqT}[kind]
        wld_m = None
        if preload is None:
            wld_m = []
            wv3 = w_sb.rearrange("p (kc c) -> p kc c", c=H)
            sv3 = wsrc.rearrange("(kc p) c -> p kc c", p=128)
            for m in range(NH):
                bld.dma("g", wv3[:, :, 128 * m:128 * (m + 1)],
                        sv3[:, :, 128 * m:128 * (m + 1)], dw)
                wld_m.append(dw.n)
            if kind == "q":
                bld.dma("g", rc_t[:, 0:OWN], qcos, dw)
                bld.dma("g", rs_t[:, 0:OWN], qs3, dw)
            wld = dw.n
        else:
            wld = preload
        tscale = 0
        if kind == "q":
            bld.emit("v", "tensor_scalar", rc_t[:, 0:OWN], rc_t[:, 0:OWN],
                     float(SCALE), None, ALU.mult, waits=[(dw.h, wld)])
            bld.emit("v", "tensor_scalar", rs_t[:, 0:OWN], rs_t[:, 0:OWN],
                     float(SCALE), None, ALU.mult, inc=vec.h)
            tscale = VEC()
        glist = list(range(NG)) if kind != "q" else list(range(NG // 2, NG))
        NI = len(glist) * NH
        ld_n = [0, 0]
        ps_free = {}
        psw_free = {}
        kr_store = {}
        r1_done = {}
        r2_done = {}
        ev_done = {}
        pl_done = {}
        grp_mm = {}
        for gi in range(min(2, len(glist))):
            ld_n[gi % 2] = lnx_load(gi % 2, glist[gi],
                                    [(sag[glist[gi] % 8].h,
                                      64 * (glist[gi] // 8 + 1))]
                                    if kind == "k" else ())

        def st_proj(i):
            gi, m = divmod(i, NH)
            gp = gi % 2
            gsl = glist[gi]
            pk = i % 2
            wv = []
            if m == 0:
                wv.append((dlp[gp].h, ld_n[gp]))
            if gi == 0:
                if wld_m is not None:
                    wv.append((dw.h, wld_m[m] if kind != "q" else wld))
                elif m == 0:
                    wv.append((dw.h, wld))
            if i - 2 in ps_free:
                wv.append(((vec.h if kind != "v" else scl.h), ps_free[i - 2]))
            for kcI in range(KC):
                bld.emit("t", "matmul", pbf[pk][:, 0:512],
                         w_sb[:, H * kcI + 128 * m: H * kcI + 128 * (m + 1)],
                         lnx_g[gp][:, 512 * kcI:512 * (kcI + 1)],
                         start=(kcI == 0), stop=(kcI == KC - 1),
                         waits=(wv if kcI == 0 else ()),
                         inc=(ten.h if kcI == KC - 1 else None))
            grp_mm[i] = TEN()
            if m == NH - 1 and gi + 2 < len(glist):
                gn = glist[gi + 2]
                ld_n[gp] = lnx_load(gp, gn,
                                    [(ten.h, grp_mm[i])] +
                                    ([(sag[gn % 8].h, 64 * (gn // 8 + 1))]
                                     if kind == "k" else []))

        def st_rope1(i):
            gi, m = divmod(i, NH)
            gsl = glist[gi]
            pk = i % 2
            tb0 = 512 * gsl if kind == "k" else 512 * (gsl - NG // 2)
            wv = [(ten.h, grp_mm[i])]
            if tscale and i == 0:
                wv.append((vec.h, tscale))
            if i - 2 in r2_done:
                wv.append((vec.h, r2_done[i - 2]))
            bld.emit("v", "tensor_tensor", b_v1[pk], pbf[pk][:, 0:512],
                     rs_t[:, tb0:tb0 + 512], ALU.mult, waits=wv)
            bld.emit("v", "tensor_tensor", b_t1[pk], pbf[pk][:, 0:512],
                     rc_t[:, tb0:tb0 + 512], ALU.mult, inc=vec.h)
            r1_done[i] = VEC()
            ps_free[i] = r1_done[i]

        def st_swap(i):
            pk = i % 2
            wv = [(vec.h, r1_done[i])]
            if i - 2 in r2_done:
                wv.append((vec.h, r2_done[i - 2]))
            bld.emit("t", "matmul", pbf[4 + pk][:, 0:512], g_pswap,
                     b_v1[pk], start=True, stop=True, waits=wv, inc=ten.h)
            psw_free[i] = TEN()

        def st_rope2(i):
            gi, m = divmod(i, NH)
            gsl = glist[gi]
            pk = i % 2
            wv = [(ten.h, psw_free[i])]
            if i - 2 in kr_store:
                wv.append((stp[pk].h, kr_store[i - 2]))
            bld.emit("v", "tensor_tensor", b_kr[pk], b_t1[pk],
                     pbf[4 + pk][:, 0:512], ALU.add, waits=wv, inc=vec.h)
            v3 = VEC()
            if kind == "k":
                bld.emit("v", "reduce_sum", b_pool[:, 0:32],
                         b_kr[pk].rearrange("p (a b) -> p a b", b=16),
                         axis=AX.X, waits=[(vec.h, vec.n)], inc=vec.h)
                VEC()
                bld.emit("v", "tensor_scalar",
                         g_ck[:, NCH * m + 32 * gsl:NCH * m + 32 * (gsl + 1)],
                         b_pool[:, 0:32], 1.0 / 16.0, None, ALU.mult,
                         waits=[(vec.h, vec.n)], inc=vec.h)
                v3 = VEC()
            r2_done[i] = v3
            if kind == "k":
                lo_tok = 512 * gsl
                lo = max(0, (PAD - WINDOW) - lo_tok)
                if lo < 512:
                    u0 = (lo_tok + lo - (PAD - WINDOW)) // 128
                    kr_store[i] = bld.dma(
                        "g", kwin_h[m][:, 128 * u0:128 * u0 + (512 - lo)],
                        b_kr[pk][:, lo:512], stp[pk], waits=[(vec.h, v3)])
                else:
                    kr_store[i] = 0
            else:
                gq = gsl - NG // 2
                kr_store[i] = bld.dma("g", q_hbm[m][:, 512 * gq:512 * (gq + 1)],
                                      b_kr[pk], stp[pk], waits=[(vec.h, v3)])

        tpipeV = TrPipe((6, 7), "v")

        def st_vev(i):
            pk = i % 2
            ew = [(ten.h, grp_mm[i])]
            if i - 2 in pl_done:
                ew.append((vec.h, pl_done[i - 2]))
                if ev_t_free.get(i - 2):
                    ew.append((ten.h, ev_t_free[i - 2]))
            bld.emit("s", "copy", c_vft[pk], pbf[pk][:, 0:512],
                     waits=ew, inc=scl.h)
            ev_done[i] = SCL()
            ps_free[i] = ev_done[i]

        ev_t_free = {}

        def st_vpool(i):
            gi, m = divmod(i, NH)
            gsl = glist[gi]
            pk = i % 2
            bld.emit("v", "reduce_sum", b_pool[:, 0:32],
                     c_vft[pk].rearrange("p (a b) -> p a b", b=16),
                     axis=AX.X, waits=[(scl.h, ev_done[i]), (vec.h, vec.n)],
                     inc=vec.h)
            VEC()
            bld.emit("v", "tensor_scalar",
                     c_cvft[:, NCH * m + 32 * gsl:NCH * m + 32 * (gsl + 1)],
                     b_pool[:, 0:32], 1.0 / 16.0, None, ALU.mult,
                     waits=[(vec.h, vec.n)], inc=vec.h)
            pl_done[i] = VEC()
            lo_tok = 512 * gsl
            lo = max(0, (PAD - WINDOW) - lo_tok)
            if lo < 512:
                evw = [(stp[pk].h, kr_store[i - 2])] if kr_store.get(i - 2) else []
                for j in range(lo // 128, 4):
                    tpipeV.go(c_vft[pk][:, 128 * j:128 * (j + 1)],
                              c_vtk[pk][:, 128 * j:128 * (j + 1)],
                              extra_waits=([(scl.h, ev_done[i])]
                                           if j == lo // 128 else ()),
                              evac_waits=(evw if j == lo // 128 else ()))
                u0 = (lo_tok + lo - (PAD - WINDOW)) // 128
                kr_store[i] = bld.dma(
                    "g", vwin_h[m][:, 128 * u0:128 * u0 + (512 - lo)],
                    c_vtk[pk][:, lo:512], stp[pk], waits=[(vec.h, vec.n)])
                ev_t_free[i] = ten.n
            else:
                kr_store[i] = 0
                ev_t_free[i] = 0

        if kind == "v":
            for step in range(NI + 2):
                if step < NI:
                    st_proj(step)
                if 1 <= step <= NI:
                    st_vev(step - 1)
                if step >= 2:
                    st_vpool(step - 2)
                yield step
        else:
            for step in range(NI + 2):
                if step < NI:
                    st_proj(step)
                if 1 <= step <= NI:
                    st_rope1(step - 1)
                    st_swap(step - 1)
                if step >= 2:
                    st_rope2(step - 2)
                yield step


    a_x = [alloc(H, F32) for _ in range(2)]
    a_stat = alloc(48, F32)
    a_ag = alloc(4, F32)
    a_eps = alloc(2, F32)
    a_istd = alloc(4, F32)
    a_nmu = alloc(2, F32)
    ATOP = ((NB16 - 16384) // 32) * 32
    assert off[BF16] <= ATOP, f"A-top overlap {off[BF16]} > {ATOP}"
    a_ln = [h16[:, (ATOP + 4096 * i) // 2:(ATOP + 4096 * i) // 2 + H]
            for i in range(2)]
    a_tr = [h16[:, (ATOP + 8192 + 4096 * i) // 2:(ATOP + 8192 + 4096 * i) // 2 + H]
            for i in range(2)]

    bld.dma("g", w_sb.rearrange("p (kc c) -> p kc c", c=H),
            wkT.rearrange("(kc p) c -> p kc c", p=128), dw)
    bld.dma("g", rc_t, kcos, dw)
    bld.dma("g", rs_t, ks3, dw)
    WLD_K = dw.n

    a_xfree = [0, 0]
    a_trdone = [0, 0]
    a_stdone = [0, 0]
    a_vln = {}
    a_vagg = {}
    a_srec = {}
    tpipeA = TrPipe((6, 7), "s")

    def a_s1(t):
        p = t % 2
        ldv = bld.dma("y", a_x[p], x_ctx[128 * t:128 * (t + 1), :], dyp[p],
                      waits=([(vec.h, a_xfree[p])] if t >= 2 else ()))
        for cchunk in range(NSQ):
            bld.emit("v", "bn_stats", a_stat[:, 12 * p + 6 * cchunk // 1:
                                             12 * p + 6 * cchunk + 6]
                     if False else a_stat[:, 24 * p + 6 * cchunk:24 * p + 6 * cchunk + 6],
                     a_x[p][:, 512 * cchunk:512 * (cchunk + 1)],
                     waits=([(dyp[p].h, ldv)] if cchunk == 0 else ()),
                     inc=(vec.h if cchunk == NSQ - 1 else None))
        VEC()
        bld.emit("v", "bn_aggr", a_ag[:, 2 * p:2 * p + 2],
                 a_stat[:, 24 * p:24 * p + 6 * NSQ],
                 waits=[(vec.h, vec.n)], inc=vec.h)
        VEC()
        bld.emit("v", "tensor_scalar", a_eps[:, p:p + 1], a_ag[:, 2 * p + 1:2 * p + 2],
                 EPS, None, ALU.add, waits=[(vec.h, vec.n)], inc=vec.h)
        a_vagg[t] = VEC()

    def a_s2(t):
        p = t % 2
        bld.emit("s", "sqrt", a_istd[:, 2 + p:3 + p], a_eps[:, p:p + 1],
                 waits=[(vec.h, a_vagg[t])], inc=scl.h)
        srec = SCL()
        bld.emit("v", "reciprocal", a_istd[:, p:p + 1], a_istd[:, 2 + p:3 + p],
                 waits=[(scl.h, srec)], inc=vec.h)
        VEC()
        bld.emit("v", "tensor_scalar", a_nmu[:, p:p + 1], a_ag[:, 2 * p:2 * p + 1],
                 a_istd[:, p:p + 1], -1.0, ALU.mult, ALU.mult,
                 waits=[(vec.h, vec.n)], inc=vec.h)
        VEC()
        bld.emit("v", "tensor_scalar", a_ln[p], a_x[p], a_istd[:, p:p + 1],
                 a_nmu[:, p:p + 1], ALU.mult, ALU.add,
                 waits=[(vec.h, vec.n)] +
                       ([(ten.h, a_trdone[p])] if t >= 2 else []), inc=vec.h)
        a_vln[t] = VEC()
        a_xfree[p] = a_vln[t]

    def a_s3(t):
        p = t % 2
        for kcI in range(KC):
            ew = [(sag[((t - 2) // 4) % 8].h, a_stdone[p])] \
                if (kcI == 0 and t >= 2) else []
            xw = [(vec.h, a_vln[t])] if kcI == 0 else []
            if t == 0 and kcI == 0:
                xw.append((dw.h, CONST_DONE))
            tpipeA.go(a_ln[p][:, 128 * kcI:128 * (kcI + 1)],
                      a_tr[p][:, 128 * kcI:128 * (kcI + 1)],
                      extra_waits=xw, evac_waits=ew)
        a_trdone[p] = ten.n
        evd = scl.n
        sg = sag[(t // 4) % 8]
        sg.n += 16
        bld.emit("g", "dma_start", lnx_h[t].transpose([1, 0, 2]),
                 a_tr[p].rearrange("p (kc c) -> p kc c", c=128),
                 waits=[(scl.h, evd)], inc=sg.h, inck=16)
        a_stdone[p] = sg.n

    kgen = qkv_pass("k", WLD_K)
    for step in range(CT + 2):
        if step < CT:
            a_s1(step)
        if 1 <= step <= CT:
            a_s2(step - 1)
        if step >= 2:
            a_s3(step - 2)
    for _ in kgen:
        pass
    for sgx in sag:
        bld.wait("g", sgx.h, sgx.n)
    bld.wait("g", st0.h, st0.n)
    bld.wait("g", st1.h, st1.n)
    barrier()
    for _ in qkv_pass("v"):
        pass
    tpipeCV = TrPipe((6, 7), "v")
    first_cv = True
    for m in range(NH):
        for cc in range(CVW):
            w0 = min(128, NCH - 128 * cc)
            xw = [(vec.h, vec.n)] if first_cv else []
            if first_cv:
                xw.append((ten.h, ten.n))
            first_cv = False
            tpipeCV.go(c_cvft[:, NCH * m + 128 * cc:NCH * m + 128 * cc + w0],
                       g_cv[0:w0,
                            (NH * 128) * cc + 128 * m:(NH * 128) * cc + 128 * (m + 1)],
                       w0=w0, extra_waits=xw)
    bld.wait("g", st0.h, st0.n)
    bld.wait("g", st1.h, st1.n)
    barrier()
    for _ in qkv_pass("q"):
        pass
    bld.wait("g", st0.h, st0.n)
    bld.wait("g", st1.h, st1.n)
    barrier()
    B_ST0, B_ST1 = st0.n, st1.n

    phase_reset()
    e_cm = alloc(QT * NCH, BF16)
    e_wm = alloc(QT * 384, BF16)
    e_k = [alloc(OWN + WINDOW, BF16) for _ in range(2)]
    e_v = [alloc(WT * 128, BF16) for _ in range(2)]
    e_q = [alloc(OWN, BF16) for _ in range(2)]
    e_pc = [alloc(NCH, BF16) for _ in range(4)]
    e_pw = [alloc(384, BF16) for _ in range(4)]
    NPT = CVW + 3
    e_pt = [alloc(NPT * 128, BF16) for _ in range(2)]
    e_D = [alloc(128, BF16) for _ in range(4)]
    e_nm = alloc(16, F32)

    bld.dma("g", e_cm.rearrange("p (t c) -> p t c", c=NCH),
            cmask.transpose([1, 0, 2]), dm2)
    mld = bld.dma("g", e_wm.rearrange("p (t c) -> p t c", c=384),
                  wmask.transpose([1, 0, 2]), dm2)
    bld.dma("g", wobf_h, woT, dw)
    PRE_WO = dw.n
    bld.dma("g", wrtbf_h, wrt, dw)
    PRE_WRT = dw.n
    for e in range(NE):
        bld.dma("g", w1bf_h[e], ws1 if e == 0 else w1[e - 1], dw)
        bld.dma("g", w2bf_h[e], ws2 if e == 0 else w2[e - 1], dw)
    PRECAST_DONE = dw.n

    kv_ld = {}
    for m in range(min(2, NH)):
        p = m % 2
        bld.dma("y", e_k[p], kwin_h[m],
                dyp[p], waits=([(stp[0].h, B_ST0), (stp[1].h, B_ST1)]
                               if m == 0 else ()))
        bld.dma("y", e_v[p], vwin_h[m], dyp[p])
        kv_ld[m] = bld.dma("y", e_q[p], q_hbm[m], dyp[p])

    N_IT = NH * QT
    sc_free = {}
    sw_free = {}
    o_free = {}
    pc_free = {}
    av_done = {}
    nmd_d = {}
    scd_d = {}
    dbl_d = {}
    tr_ev = {6: 0, 7: 0}
    tr_ev_s = {6: "v", 7: "v"}
    tr_i = [0]

    def e_scores(i):
        m, t = divmod(i, QT)
        p = m % 2
        tp = i % 2
        wc = PADC + 8 * (t + 1)
        wv = [(dyp[p].h, kv_ld[m])] if t == 0 else []
        if i == 0:
            wv.append((dm2.h, mld))
        if i - 2 in sc_free:
            wv.append((scl.h, sc_free[i - 2]))
        bld.emit("t", "matmul", pbf[tp][:, 0:wc], g_ident,
                 e_cm[:, NCH * t:NCH * t + wc], start=True, stop=False,
                 waits=wv)
        bld.emit("t", "matmul", pbf[tp][:, 0:wc],
                 e_q[p][:, 128 * t:128 * (t + 1)],
                 g_ck[:, NCH * m:NCH * m + wc], start=False, stop=True,
                 inc=ten.h)
        scd_d[i] = TEN()
        wv = [(scl.h, sw_free[i - 2])] if i - 2 in sw_free else []
        bld.emit("t", "matmul", pbf[2 + tp][:, 0:384], g_ident,
                 e_wm[:, 384 * t:384 * (t + 1)], start=True, stop=False,
                 waits=wv)
        bld.emit("t", "matmul", pbf[2 + tp][:, 0:384],
                 e_q[p][:, 128 * t:128 * (t + 1)],
                 e_k[p][:, 128 * t:128 * t + 384], start=False, stop=True,
                 inc=ten.h)
        sc_free[i] = st = TEN()
        nmd_d[i] = st

    def e_sm1(i):
        m, t = divmod(i, QT)
        tp = i % 2
        nb = 8 * tp
        wc = PADC + 8 * (t + 1)
        swd = sc_free[i]
        wv = [(ten.h, scd_d[i])]
        if i - 2 in sw_free:
            wv.append((scl.h, sw_free[i - 2]))
        bld.emit("v", "reduce_max", e_nm[:, nb:nb + 1], pbf[tp][:, 0:wc],
                 axis=AX.X, negate=True, waits=wv)
        bld.emit("v", "reduce_max", e_nm[:, nb + 1:nb + 2],
                 pbf[2 + tp][:, 0:384], axis=AX.X, negate=True,
                 waits=[(ten.h, swd)], inc=vec.h)
        VEC()
        bld.emit("v", "tensor_tensor", e_nm[:, nb + 2:nb + 3],
                 e_nm[:, nb:nb + 1], e_nm[:, nb + 1:nb + 2], ALU.min,
                 waits=[(vec.h, vec.n)], inc=vec.h)
        nmd_d[i] = VEC()

    def e_sm2(i):
        m, t = divmod(i, QT)
        tp = i % 2
        nb = 8 * tp
        wc = PADC + 8 * (t + 1)
        p4 = i % 4
        ew = [(vec.h, nmd_d[i])]
        if i - 4 in pc_free:
            ew.append((ten.h, pc_free[i - 4]))
        bld.emit("s", "activation", e_pc[p4][:, 0:wc], pbf[tp][:, 0:wc],
                 ACT.Exp, bias=e_nm[:, nb + 2:nb + 3], scale=1.0,
                 accum_out=e_nm[:, nb + 3:nb + 4], waits=ew, inc=scl.h)
        sc_free[i] = SCL()
        bld.emit("s", "activation", e_pw[p4][:, 0:384], pbf[2 + tp][:, 0:384],
                 ACT.Exp, bias=e_nm[:, nb + 2:nb + 3], scale=1.0,
                 accum_out=e_nm[:, nb + 4:nb + 5], inc=scl.h)
        sw_free[i] = SCL()

    def e_sm3(i):
        tp = i % 2
        nb = 8 * tp
        bld.emit("v", "tensor_tensor", e_nm[:, nb + 5:nb + 6],
                 e_nm[:, nb + 3:nb + 4], e_nm[:, nb + 4:nb + 5], ALU.add,
                 waits=[(scl.h, sw_free[i])], inc=vec.h)
        VEC()
        bld.emit("v", "reciprocal", e_nm[:, nb + 6:nb + 7],
                 e_nm[:, nb + 5:nb + 6], waits=[(vec.h, vec.n)], inc=vec.h)
        VEC()
        dw2 = [(vec.h, vec.n)]
        if i - 4 in pc_free:
            dw2.append((ten.h, pc_free[i - 4]))
        bld.emit("v", "tensor_scalar", e_D[i % 4], g_ident,
                 e_nm[:, nb + 6:nb + 7], None, ALU.mult, waits=dw2,
                 inc=vec.h)
        dbl_d[i] = VEC()

    def e_tr(i):
        m, t = divmod(i, QT)
        tp = i % 2
        wc = PADC + 8 * (t + 1)
        nwc = (wc + 127) // 128
        p4 = i % 4
        for j in range(nwc + 3):
            if j < nwc:
                w0 = min(128, wc - 128 * j)
                src = e_pc[p4][:, 128 * j:128 * j + w0]
            else:
                w0 = 128
                src = e_pw[p4][:, 128 * (j - nwc):128 * (j - nwc + 1)]
            b = (6, 7)[tr_i[0] % 2]
            tr_i[0] += 1
            wv = []
            if j == 0:
                wv = [(vec.h, dbl_d[i]), (scl.h, sw_free[i])]
            if tr_ev[b]:
                wv.append(((vec.h if tr_ev_s[b] == "v" else scl.h), tr_ev[b]))
            bld.emit("t", "matmul", pbf[b][0:w0, 0:128], src, e_D[p4],
                     start=True, stop=True, waits=wv, inc=ten.h)
            trd = TEN()
            ew = [(ten.h, trd)]
            if j == 0 and i - 2 in av_done:
                ew.append((ten.h, av_done[i - 2]))
            dst = e_pt[tp][0:w0, 128 * j:128 * (j + 1)]
            if j % 2 == 0:
                bld.emit("v", "tensor_copy", dst, pbf[b][0:w0, 0:128],
                         waits=ew, inc=vec.h)
                tr_ev[b] = VEC()
                tr_ev_s[b] = "v"
            else:
                bld.emit("s", "copy", dst, pbf[b][0:w0, 0:128],
                         waits=ew, inc=scl.h)
                tr_ev[b] = SCL()
                tr_ev_s[b] = "s"
        pc_free[i] = ten.n

    def e_av(i):
        m, t = divmod(i, QT)
        p = m % 2
        tp = i % 2
        wc = PADC + 8 * (t + 1)
        nwc = (wc + 127) // 128
        wv = [(vec.h, tr_ev[6] if tr_ev_s[6] == "v" else 0),
              (vec.h, tr_ev[7] if tr_ev_s[7] == "v" else 0),
              (scl.h, tr_ev[6] if tr_ev_s[6] == "s" else 0),
              (scl.h, tr_ev[7] if tr_ev_s[7] == "s" else 0)]
        wv = [w for w in wv if w[1]]
        if i - 2 in o_free:
            wv.append((scl.h, o_free[i - 2]))
        first = True
        for j in range(nwc):
            w0 = min(128, wc - 128 * j)
            bld.emit("t", "matmul", pbf[4 + tp][:, 0:128],
                     g_cv[0:w0, (NH * 128) * j + 128 * m:(NH * 128) * j + 128 * (m + 1)],
                     e_pt[tp][0:w0, 128 * j:128 * (j + 1)],
                     start=first, stop=False, waits=(wv if first else ()))
            first = False
        for j in range(3):
            last = (j == 2)
            bld.emit("t", "matmul", pbf[4 + tp][:, 0:128],
                     e_v[p][:, 128 * (t + j):128 * (t + j + 1)],
                     e_pt[tp][:, 128 * (nwc + j):128 * (nwc + j + 1)],
                     start=False, stop=last, inc=(ten.h if last else None))
        av_done[i] = TEN()
        bld.emit("s", "copy", g_o[:, OWN * m + 128 * t:OWN * m + 128 * (t + 1)],
                 pbf[4 + tp][:, 0:128], waits=[(ten.h, av_done[i])],
                 inc=scl.h)
        o_free[i] = SCL()
        if t == QT - 1 and m + 2 < NH:
            bld.dma("y", e_k[p], kwin_h[m + 2], dyp[p],
                    waits=[(ten.h, ten.n)])
            bld.dma("y", e_v[p], vwin_h[m + 2], dyp[p])
            kv_ld[m + 2] = bld.dma("y", e_q[p], q_hbm[m + 2], dyp[p])

    for step in range(N_IT + 3):
        if step < N_IT:
            e_scores(step)
        if 1 <= step <= N_IT:
            e_sm1(step - 1)
            e_sm2(step - 1)
        if 2 <= step <= N_IT + 1:
            e_sm3(step - 2)
        if step >= 3:
            e_tr(step - 3)
            e_av(step - 3)
    barrier()

    phase_reset()
    f_wo = alloc(KC * H, BF16)
    f_xr = [alloc(H, F32) for _ in range(2)]
    f_ln2 = alloc(H, BF16)
    f_ln = [f_ln2, f_ln2]
    f_tr = [alloc(KC * 128, BF16) for _ in range(2)]
    f_wr = alloc(KC * 8, BF16)
    f_stat = alloc(6 * NSQ, F32)
    f_ag = alloc(4, F32)
    f_eps = alloc(2, F32)
    f_istd = alloc(2, F32)
    f_nmu = alloc(2, F32)
    f_lg = alloc(8, F32)
    f_m8 = alloc(8, F32)
    f_nm1 = alloc(2, F32)
    f_den = alloc(2, F32)
    f_sel = alloc(8, F32)

    wold = bld.dma("s", f_wo.rearrange("p (kc c) -> p kc c", c=H),
                   wobf_h.rearrange("(kc p) c -> p kc c", p=128), ds1,
                   waits=[(dw.h, PRE_WO)])
    wrld = bld.dma("s", f_wr.rearrange("p (kc c) -> p kc c", c=8),
                   wrtbf_h.rearrange("(kc p) c -> p kc c", p=128), ds2,
                   waits=[(dw.h, PRE_WRT)])
    x_ld = [0, 0]
    for t in range(min(2, QT)):
        x_ld[t % 2] = bld.dma("y", f_xr[t % 2],
                              x_ctx[PAD + 128 * t:PAD + 128 * (t + 1), :],
                              dyp[t % 2])
    f_psfree = {}
    f_store = [0, 0]
    f_trlast = 0
    f_r1free = [0, 0]
    rt_evd = 0
    tpipeF = TrPipe((6, 7), "s")
    for t in range(QT):
        p = t % 2
        for jg in range(JG):
            bk = jg % 2
            wv = []
            if t == 0 and jg == 0:
                wv.append((ds1.h, wold))
            if bk in f_psfree:
                wv.append((vec.h, f_psfree[bk]))
            for kcI in range(KC):
                bld.emit("t", "matmul", pbf[bk][:, 0:512],
                         g_o[:, OWN * kcI + 128 * t:OWN * kcI + 128 * (t + 1)],
                         f_wo[:, H * kcI + 512 * jg:H * kcI + 512 * (jg + 1)],
                         start=(kcI == 0), stop=(kcI == KC - 1),
                         waits=(wv if kcI == 0 else ()),
                         inc=(ten.h if kcI == KC - 1 else None))
            opd = TEN()
            wv = [(ten.h, opd), (dyp[p].h, x_ld[p])]
            if jg == 0 and t >= 2:
                wv.append((sfp[p].h, f_store[p]))
                wv.append((vec.h, f_r1free[p]))
            bld.emit("v", "tensor_tensor", f_xr[p][:, 512 * jg:512 * (jg + 1)],
                     pbf[bk][:, 0:512], f_xr[p][:, 512 * jg:512 * (jg + 1)],
                     ALU.add, waits=wv, inc=vec.h)
            f_psfree[bk] = VEC()
        r1d = vec.n
        bld.dma("y", r1_h[128 * t:128 * (t + 1), :], f_xr[p], sfp[p],
                waits=[(vec.h, r1d)])
        f_store[p] = sfp[p].n
        for cchunk in range(NSQ):
            bld.emit("v", "bn_stats", f_stat[:, 6 * cchunk:6 * cchunk + 6],
                     f_xr[p][:, 512 * cchunk:512 * (cchunk + 1)],
                     waits=([(vec.h, vec.n)] if cchunk == 0 else ()),
                     inc=(vec.h if cchunk == NSQ - 1 else None))
        VEC()
        bld.emit("v", "bn_aggr", f_ag[:, 0:2], f_stat[:, 0:6 * NSQ],
                 waits=[(vec.h, vec.n)], inc=vec.h)
        VEC()
        bld.emit("v", "tensor_scalar", f_eps[:, 0:1], f_ag[:, 1:2], EPS, None,
                 ALU.add, waits=[(vec.h, vec.n)], inc=vec.h)
        vrec = VEC()
        bld.emit("s", "activation", f_istd[:, 1:2], f_eps[:, 0:1], ACT.Ln,
                 waits=[(vec.h, vrec)], inc=scl.h)
        SCL()
        bld.emit("s", "activation", f_istd[:, 0:1], f_istd[:, 1:2], ACT.Exp,
                 scale=-0.5, waits=[(scl.h, scl.n)], inc=scl.h)
        srec = SCL()
        bld.emit("v", "tensor_scalar", f_nmu[:, 0:1], f_ag[:, 0:1],
                 f_istd[:, 0:1], -1.0, ALU.mult, ALU.mult,
                 waits=[(scl.h, srec)], inc=vec.h)
        VEC()
        bld.emit("v", "tensor_scalar", f_ln[p], f_xr[p], f_istd[:, 0:1],
                 f_nmu[:, 0:1], ALU.mult, ALU.add,
                 waits=[(vec.h, vec.n)] +
                       ([(ten.h, f_trlast)] if t >= 1 else []), inc=vec.h)
        lnd = VEC()
        f_r1free[p] = lnd
        if t + 2 < QT:
            x_ld[p] = bld.dma("y", f_xr[p],
                              x_ctx[PAD + 128 * (t + 2):PAD + 128 * (t + 3), :],
                              dyp[p],
                              waits=[(vec.h, lnd), (sfp[p].h, f_store[p])])
        for kcI in range(KC):
            ew = []
            if kcI == 0 and t >= 2:
                ew.append((sfp[p].h, f_store[p]))
            tpipeF.go(f_ln[p][:, 128 * kcI:128 * (kcI + 1)],
                      f_tr[p][:, 128 * kcI:128 * (kcI + 1)],
                      extra_waits=([(vec.h, lnd)] if kcI == 0 else ()),
                      evac_waits=ew)
        f_trlast = ten.n
        evd = scl.n
        wv = [(scl.h, evd), (ds2.h, wrld)]
        if rt_evd:
            wv.append((vec.h, rt_evd))
        for kcI in range(KC):
            bld.emit("t", "matmul", pbf[4][:, 0:8],
                     f_tr[p][:, 128 * kcI:128 * (kcI + 1)],
                     f_wr[:, 8 * kcI:8 * (kcI + 1)],
                     start=(kcI == 0), stop=(kcI == KC - 1),
                     waits=(wv if kcI == 0 else ()),
                     inc=(ten.h if kcI == KC - 1 else None))
        rtd = TEN()
        bld.emit("v", "tensor_copy", f_lg[:, 0:8], pbf[4][:, 0:8],
                 waits=[(ten.h, rtd)], inc=vec.h)
        rt_evd = VEC()
        bld.emit("v", "max", f_m8[:, 0:8], f_lg[:, 0:8],
                 waits=[(vec.h, vec.n)], inc=vec.h)
        VEC()
        bld.emit("v", "tensor_scalar", f_nm1[:, 0:1], f_m8[:, 0:1], -1.0, None,
                 ALU.mult, waits=[(vec.h, vec.n)])
        bld.emit("v", "tensor_scalar", f_sel[:, 0:8], f_lg[:, 0:8],
                 f_m8[:, 1:2], None, ALU.is_ge, inc=vec.h)
        seld = VEC()
        bld.emit("s", "activation", f_den[:, 0:1], f_m8[:, 1:2], ACT.Exp,
                 bias=f_nm1[:, 0:1], scale=1.0, waits=[(vec.h, seld)])
        bld.emit("s", "activation", f_lg[:, 0:8], f_lg[:, 0:8], ACT.Exp,
                 bias=f_nm1[:, 0:1], scale=1.0, inc=scl.h)
        expd = SCL()
        bld.emit("v", "tensor_scalar", f_den[:, 1:2], f_den[:, 0:1], 1.0, None,
                 ALU.add, waits=[(scl.h, expd)], inc=vec.h)
        VEC()
        bld.emit("v", "reciprocal", f_den[:, 0:1], f_den[:, 1:2],
                 waits=[(vec.h, vec.n)], inc=vec.h)
        VEC()
        bld.emit("v", "tensor_tensor", f_sel[:, 0:8], f_sel[:, 0:8],
                 f_lg[:, 0:8], ALU.mult, waits=[(vec.h, vec.n)], inc=vec.h)
        VEC()
        bld.emit("v", "tensor_scalar", g_comb[:, 8 * t:8 * (t + 1)],
                 f_sel[:, 0:8], f_den[:, 0:1], None, ALU.mult,
                 waits=[(vec.h, vec.n)], inc=vec.h)
        VEC()
        bld.dma("y", lnx2_h[t].transpose([1, 0, 2]),
                f_tr[p].rearrange("p (kc c) -> p kc c", c=128),
                sfp[p], waits=[(scl.h, evd), (ten.h, rtd)])
        f_store[p] = sfp[p].n
    bld.wait("y", sf0.h, sf0.n)
    bld.wait("y", sf1.h, sf1.n)
    barrier()
    F_ST0, F_ST1 = sf0.n, sf1.n

    phase_reset(reuse_o=True)
    g_w1 = alloc(KC * I, BF16)
    g_w2 = alloc(ICH * H, BF16)
    g_h = alloc(ICH * 512, BF16)
    g_acc = alloc(4 * H, F32)
    g_lnx2 = alloc(KC * 512, BF16)
    g_r1 = [alloc(H, F32) for _ in range(2)]

    w1_ld = bld.dma("s", g_w1.rearrange("p (kc c) -> p kc c", c=I),
                    w1bf_h[0].rearrange("(kc p) c -> p kc c", p=128), ds1,
                    waits=[(dw.h, PRECAST_DONE)])
    w2_ld = bld.dma("s", g_w2.rearrange("p (ic c) -> p ic c", c=H),
                    w2bf_h[0].rearrange("(ic p) c -> p ic c", p=128), ds2)
    h_done = {}
    yst = [0, 0, 0, 0]
    od_g = {}
    l2 = 0
    prev_hall = 0
    for pss in range(NPASS):
        l2v = g_lnx2.rearrange("p (kc g c) -> p kc g c", g=4, c=128)
        for j in range(4):
            l2 = bld.dma("y", l2v[:, :, j, :],
                         lnx2_h[4 * pss + j].transpose([1, 0, 2]), dyp[0],
                         waits=(([(sf0.h, F_ST0), (sf1.h, F_ST1)]
                                 if pss == 0 else [(scl.h, prev_hall)])
                                if j == 0 else ()))
        acc_part = {}
        acc_done = {}
        for e in range(NE):
            for ic in range(ICH):
                bk = ic % 2
                wv = []
                if ic == 0:
                    wv.append((ds1.h, w1_ld))
                    wv.append((dyp[0].h, l2))
                    if e >= 1:
                        wv.append((vec.h, acc_done[e - 1]))
                if bk in h_done:
                    wv.append((scl.h, h_done[bk]))
                for kcI in range(KC):
                    bld.emit("t", "matmul", pbf[bk][:, 0:512],
                             g_w1[:, I * kcI + 128 * ic:I * kcI + 128 * (ic + 1)],
                             g_lnx2[:, 512 * kcI:512 * (kcI + 1)],
                             start=(kcI == 0), stop=(kcI == KC - 1),
                             waits=(wv if kcI == 0 else ()),
                             inc=(ten.h if kcI == KC - 1 else None))
                w1d = TEN()
                bld.emit("s", "activation", g_h[:, 512 * ic:512 * (ic + 1)],
                         pbf[bk][:, 0:512], ACT.Silu, waits=[(ten.h, w1d)],
                         inc=scl.h)
                h_done[bk] = SCL()
            hall = scl.n
            if not (pss == NPASS - 1 and e == NE - 1):
                w1_ld = bld.dma("s", g_w1.rearrange("p (kc c) -> p kc c", c=I),
                                w1bf_h[(e + 1) % NE].rearrange("(kc p) c -> p kc c", p=128),
                                ds1)
            for tt in range(4):
                gt = 4 * pss + tt
                for jg in range(JG):
                    bk = 2 + ((tt * JG + jg) % 2)
                    wv = [(scl.h, hall)]
                    if tt == 0 and jg == 0:
                        wv.append((ds2.h, w2_ld))
                    if bk in acc_part:
                        wv.append((vec.h, acc_part[bk]))
                    for ic in range(ICH):
                        bld.emit("t", "matmul", pbf[bk][:, 0:512],
                                 g_h[:, 512 * ic + 128 * tt:512 * ic + 128 * (tt + 1)],
                                 g_w2[:, H * ic + 512 * jg:H * ic + 512 * (jg + 1)],
                                 start=(ic == 0), stop=(ic == ICH - 1),
                                 waits=(wv if ic == 0 else ()),
                                 inc=(ten.h if ic == ICH - 1 else None))
                    w2d = TEN()
                    accsl = g_acc[:, H * tt + 512 * jg:H * tt + 512 * (jg + 1)]
                    if e == 0:
                        aw = [(ten.h, w2d)]
                        if pss > 0 and jg == 0:
                            aw.append((sy.h, yst[tt]))
                        bld.emit("v", "tensor_copy", accsl, pbf[bk][:, 0:512],
                                 waits=aw, inc=vec.h)
                    else:
                        bld.emit("v", "scalar_tensor_tensor", accsl,
                                 pbf[bk][:, 0:512],
                                 g_comb[:, 8 * gt + (e - 1):8 * gt + e], accsl,
                                 ALU.mult, ALU.add, waits=[(ten.h, w2d)],
                                 inc=vec.h)
                    acc_part[bk] = VEC()
                if e == NE - 1:
                    p2 = tt % 2
                    r1w = [(scl.h, hall)]
                    if od_g.get(p2):
                        r1w.append((vec.h, od_g[p2]))
                    r1v = bld.dma("y", g_r1[p2],
                                  r1_h[512 * pss + 128 * tt:
                                       512 * pss + 128 * (tt + 1), :],
                                  dyp[p2], waits=r1w)
                    ow = [(dyp[p2].h, r1v),
                          (vec.h, acc_part[2 + (tt * JG + JG - 1) % 2])]
                    bld.emit("v", "tensor_tensor", g_acc[:, H * tt:H * (tt + 1)],
                             g_acc[:, H * tt:H * (tt + 1)], g_r1[p2], ALU.add,
                             waits=ow, inc=vec.h)
                    od = VEC()
                    od_g[p2] = od
                    yst[tt] = bld.dma("g",
                                      y[512 * pss + 128 * tt:512 * pss + 128 * (tt + 1), :],
                                      g_acc[:, H * tt:H * (tt + 1)], sy,
                                      waits=[(vec.h, od)])
            acc_done[e] = vec.n
            prev_hall = hall
            if not (pss == NPASS - 1 and e == NE - 1):
                w2_ld = bld.dma("s", g_w2.rearrange("p (ic c) -> p ic c", c=H),
                                w2bf_h[(e + 1) % NE].rearrange("(ic p) c -> p ic c", p=128),
                                ds2, waits=[(ten.h, ten.n)])
    bld.wait("g", sy.h, sy.n)
    bld.wait("y", sy.h, sy.n)

    import contextlib
    with contextlib.ExitStack() as cx:
        block = cx.enter_context(nc.Block())
        P = bld.P

        @block.tensor
        def _(t):
            for f in P["t"]:
                f(t)

        @block.vector
        def _(v):
            for f in P["v"]:
                f(v)

        @block.scalar
        def _(s):
            for f in P["s"]:
                f(s)

        @block.gpsimd
        def _(g):
            for f in P["g"]:
                f(g)

        @block.sync
        def _(yq):
            for f in P["y"]:
                f(yq)

    for cm in reversed(semctx):
        cm.__exit__(None, None, None)
    for cm in reversed(pb_ctx):
        cm.__exit__(None, None, None)
    return nc


def rope_tables(pos):
    inv = 1.0 / (10000.0 ** (np.arange(0, 128, 2, dtype=np.float32) / 128))
    inv = inv.astype(np.float32)
    f = (pos.astype(np.float32)[:, None] * inv[None, :]).astype(np.float32)
    fc = np.exp(1j * f).astype(np.complex64)
    c = fc.real.T
    sn = fc.imag.T
    C = np.repeat(c, 2, axis=0)
    S3 = np.empty((128, len(pos)), dtype=np.float32)
    S3[0::2] = sn
    S3[1::2] = -sn
    return np.ascontiguousarray(C), np.ascontiguousarray(S3)


def make_core_inputs(cfg, core, full):
    d = derive(cfg)
    H = cfg["H"]
    OWN, WINDOW = cfg["OWN"], cfg["WINDOW"]
    CTX, PAD, NCH, QT = d["CTX"], d["PAD"], d["NCH"], d["QT"]
    b = core // 2
    half = core % 2
    OS = half * OWN
    x = np.asarray(full["L0_attn_x"][b], dtype=np.float32)
    xc = np.zeros((CTX, H), dtype=np.float32)
    g0 = OS - PAD
    lo = max(0, -g0)
    xc[lo:] = x[g0 + lo:g0 + CTX]
    pos = np.arange(CTX, dtype=np.int64) + g0
    posc = np.where(pos < 0, 0, pos)
    kc_, ks_ = rope_tables(posc)
    qc_, qs_ = rope_tables(pos[PAD:])
    lq = PAD + np.arange(OWN)
    cm = np.empty((QT, 128, NCH), dtype=np.float32)
    wm = np.empty((QT, 128, 384), dtype=np.float32)
    delta = (PAD - OS) // 16
    for t in range(QT):
        lqt = lq[128 * t:128 * (t + 1)][:, None]
        ch = np.arange(NCH)[None, :]
        vis = (ch < (lqt + 1) // 16) & (ch >= delta)
        cm[t] = np.where(vis, 0.0, NEG)
        lk = (PAD - WINDOW + 128 * t) + np.arange(384)[None, :]
        vis = (lk <= lqt) & (lk > lqt - WINDOW) & (lk >= PAD - OS)
        wm[t] = np.where(vis, 0.0, NEG)
    return {
        "x_ctx": xc,
        "wqT": np.ascontiguousarray(np.asarray(full["L0_attn_wq"]).T),
        "wkT": np.ascontiguousarray(np.asarray(full["L0_attn_wk"]).T),
        "wvT": np.ascontiguousarray(np.asarray(full["L0_attn_wv"]).T),
        "woT": np.ascontiguousarray(np.asarray(full["L0_attn_wo"]).T),
        "ws1": np.asarray(full["L0_ffn_w_shared1"], dtype=np.float32),
        "ws2": np.asarray(full["L0_ffn_w_shared2"], dtype=np.float32),
        "w1": np.asarray(full["L0_ffn_w1"], dtype=np.float32),
        "w2": np.asarray(full["L0_ffn_w2"], dtype=np.float32),
        "wrt": np.asarray(full["L0_ffn_w_router"], dtype=np.float32),
        "kcos": kc_, "ks3": ks_, "qcos": qc_, "qs3": qs_,
        "cmask": cm, "wmask": wm,
        "identin": np.eye(128, dtype=np.float32),
        "pswapin": np.ascontiguousarray(
            np.eye(128, dtype=np.float32)[np.arange(128) ^ 1]),
    }


def kernel(**inputs):
    cfg = cfg_full()
    nc = build_program(cfg)
    in_maps = [make_core_inputs(cfg, c, inputs) for c in range(cfg["NCORES"])]
    res = run_bass_kernel_spmd(nc, in_maps, core_ids=list(range(cfg["NCORES"])))
    B, S, H = cfg["B"], cfg["S"], cfg["H"]
    out = np.empty((B, S, H), dtype=np.float32)
    for c in range(cfg["NCORES"]):
        b, half = c // 2, c % 2
        out[b, half * cfg["OWN"]:(half + 1) * cfg["OWN"]] = res.results[c]["y"]
    return out


# version 15
# speedup vs baseline: 1.0134x; 1.0134x over previous
import numpy as np
import concourse.bass as bass
import concourse.mybir as mybir
from concourse.bass_utils import run_bass_kernel_spmd

F32 = mybir.dt.float32
BF16 = mybir.dt.bfloat16
U8 = mybir.dt.uint8
ALU = mybir.AluOpType
ACT = mybir.ActivationFunctionType
AX = mybir.AxisListType

NEG = -1.0e30
EPS = 1e-5


def cfg_full():
    return dict(B=4, S=4096, H=2048, NH=16, I=1408, E=8, WINDOW=256, OWN=2048,
                NCORES=8)


def derive(c):
    d = dict(c)
    d["KC"] = c["H"] // 128
    d["PAD"] = c["OWN"]
    d["CTX"] = 2 * c["OWN"]
    d["NCH"] = d["CTX"] // 16
    d["PADC"] = d["PAD"] // 16
    d["QT"] = c["OWN"] // 128
    d["WT"] = (c["OWN"] + c["WINDOW"]) // 128
    d["ICH"] = c["I"] // 128
    d["NE"] = c["E"] + 1
    d["CT"] = d["CTX"] // 128
    d["NG"] = d["CTX"] // 512
    d["NPASS"] = c["OWN"] // 512
    d["JG"] = c["H"] // 512
    d["CVW"] = (d["NCH"] + 127) // 128
    d["SCALE"] = 1.0 / np.sqrt(128.0)
    return d


class Sem:
    def __init__(self, h):
        self.h = h
        self.n = 0


class Build:
    def __init__(self, nc):
        self.nc = nc
        self.P = {"t": [], "v": [], "s": [], "g": [], "y": []}

    def emit(self, stream, meth, *args, waits=(), inc=None, inck=1, **kw):
        waits = tuple(w for w in waits if w is not None)

        def f(eng, meth=meth, args=args, kw=kw, waits=waits, inc=inc, inck=inck):
            for h, v in waits:
                if v > 0:
                    eng.wait_ge(h, v)
            ins = getattr(eng, meth)(*args, **kw)
            if inc is not None:
                ins.then_inc(inc, inck)
        self.P[stream].append(f)

    def dma(self, stream, dst, src, sem, waits=()):
        sem.n += 16
        self.emit(stream, "dma_start", dst, src, waits=waits, inc=sem.h, inck=16)
        return sem.n

    def wait(self, stream, h, v):
        def f(eng, h=h, v=v):
            if v > 0:
                eng.wait_ge(h, v)
        self.P[stream].append(f)




def build_program(cfg, debug=False):
    d = derive(cfg)
    H, NH, I, E = cfg["H"], cfg["NH"], cfg["I"], cfg["E"]
    OWN, WINDOW = cfg["OWN"], cfg["WINDOW"]
    KC, CTX, PAD, NCH, PADC = d["KC"], d["CTX"], d["PAD"], d["NCH"], d["PADC"]
    QT, WT, ICH, NE, CT, NG = d["QT"], d["WT"], d["ICH"], d["NE"], d["CT"], d["NG"]
    NPASS, JG, CVW = d["NPASS"], d["JG"], d["CVW"]
    SCALE = d["SCALE"]
    NSQ = H // 512

    nc = bass.Bass("TRN2", target_bir_lowering=False, detect_race_conditions=False)

    def din(name, shape):
        return nc.dram_tensor(name, list(shape), F32, kind="ExternalInput").ap()

    x_ctx = din("x_ctx", (CTX, H))
    wqT = din("wqT", (H, H))
    wkT = din("wkT", (H, H))
    wvT = din("wvT", (H, H))
    woT = din("woT", (H, H))
    ws1 = din("ws1", (H, I))
    ws2 = din("ws2", (I, H))
    w1 = din("w1", (E, H, I))
    w2 = din("w2", (E, I, H))
    wrt = din("wrt", (H, E))
    kcos = din("kcos", (128, CTX))
    ks3 = din("ks3", (128, CTX))
    qcos = din("qcos", (128, OWN))
    qs3 = din("qs3", (128, OWN))
    cmask = din("cmask", (QT, 128, NCH))
    wmask = din("wmask", (QT, 128, 384))
    identin = din("identin", (128, 128))
    pswapin = din("pswapin", (128, 128))

    y = nc.dram_tensor("y", [OWN, H], F32, kind="ExternalOutput").ap()

    def dint(name, shape, dt=BF16):
        kind = "ExternalOutput" if debug else "Internal"
        return nc.dram_tensor(name, list(shape), dt, kind=kind).ap()

    lnx_h = dint("lnx_h", (CT, KC, 128, 128))
    kwin_h = dint("kwin_h", (NH, 128, OWN + WINDOW))
    vwin_h = dint("vwin_h", (NH, 128, WT * 128))
    q_hbm = dint("q_hbm", (NH, 128, OWN))
    lnx2_h = dint("lnx2_h", (QT, KC, 128, 128))
    r1_h = dint("r1_h", (OWN, H), F32)
    w1bf_h = dint("w1bf_h", (NE, H, I))
    w2bf_h = dint("w2bf_h", (NE, I, H))
    wobf_h = dint("wobf_h", (H, H))
    wrtbf_h = dint("wrtbf_h", (H, E))

    HEAP_BASE = 16512
    NB32 = 51200
    NB16 = 161628
    assert HEAP_BASE + NB32 + NB16 <= 229340
    h32_t = nc.alloc_sbuf_tensor_at("heap32", [128, NB32 // 4], F32,
                                    offset=HEAP_BASE)
    h16_t = nc.alloc_sbuf_tensor_at("heap16", [128, NB16 // 2], BF16,
                                    offset=HEAP_BASE + NB32)
    h32 = h32_t.ap()
    h16 = h16_t.ap()

    off = {F32: 0, BF16: 0}
    himax = {F32: 0, BF16: 0}

    def alloc(cols, dt):
        esz = 4 if dt == F32 else 2
        o = off[dt]
        assert o % 32 == 0
        ap = (h32 if dt == F32 else h16)[:, o // esz:o // esz + cols]
        off[dt] = o + ((cols * esz + 31) // 32) * 32
        himax[dt] = max(himax[dt], off[dt])
        assert off[F32] <= NB32 and off[BF16] <= NB16, \
            f"heap overflow f32={off[F32]} bf16={off[BF16]}"
        return ap

    g_ident = alloc(128, BF16)
    g_pswap = alloc(128, BF16)
    g_ck = alloc(NH * NCH, BF16)
    g_cv = alloc(CVW * NH * 128, BF16)
    g_comb = alloc(QT * 8, F32)
    F32_PH = ((off[F32] + 255) // 256) * 256
    O_OFF = ((off[BF16] + 255) // 256) * 256
    off[BF16] = O_OFF
    g_o = alloc(NH * OWN, BF16)
    PH_BASE = ((off[BF16] + 255) // 256) * 256

    def phase_reset(reuse_o=False):
        off[BF16] = O_OFF if reuse_o else PH_BASE
        off[F32] = F32_PH

    pb_ctx = []
    pb = []
    for i in range(8):
        cm = nc.psum_tensor(f"pbank{i}", [128, 512], F32)
        pb.append(cm.__enter__())
        pb_ctx.append(cm)
    pbf = [p.ap() for p in pb]

    bld = Build(nc)
    semctx = []

    def S(name):
        cm = nc.semaphore(name)
        h = cm.__enter__()
        semctx.append(cm)
        return Sem(h)

    dw = S("dw")
    dm2 = S("dm2")
    sag = [S(f"sag{i}") for i in range(8)]
    dl0, dl1 = S("dl0"), S("dl1")
    dy0, dy1 = S("dy0"), S("dy1")
    ds1 = S("ds1")
    ds2 = S("ds2")
    st0, st1 = S("st0"), S("st1")
    sf0, sf1 = S("sf0"), S("sf1")
    sy = S("sy")
    ten, vec, scl = S("ten"), S("vec"), S("scl")
    phz = S("phz")
    dyp = [dy0, dy1]
    dlp = [dl0, dl1]
    stp = [st0, st1]
    sfp = [sf0, sf1]

    def TEN():
        ten.n += 1
        return ten.n

    def VEC():
        vec.n += 1
        return vec.n

    def SCL():
        scl.n += 1
        return scl.n

    def barrier():
        for stnm, sem in (("t", ten), ("v", vec), ("s", scl)):
            bld.emit(stnm, "nop", waits=[(sem.h, sem.n)], inc=phz.h)
        for stnm in ("g", "y"):
            bld.emit(stnm, "nop", inc=phz.h)
        phz.n += 5
        target = phz.n
        for stnm in ("t", "v", "s", "g", "y"):
            bld.wait(stnm, phz.h, target)

    class TrPipe:

        def __init__(self, banks, evac_stream="s"):
            self.banks = banks
            self.ev = {b: 0 for b in banks}
            self.i = 0
            self.evs = evac_stream

        def go(self, src_ap, dst_ap, w0=128, rhs=None, extra_waits=(),
               evac_waits=()):
            b = self.banks[self.i % 2]
            self.i += 1
            wv = list(extra_waits)
            if self.ev[b]:
                wv.append(((scl.h if self.evs == "s" else vec.h), self.ev[b]))
            bld.emit("t", "matmul", pbf[b][0:w0, 0:128], src_ap,
                     rhs if rhs is not None else g_ident,
                     start=True, stop=True, waits=wv, inc=ten.h)
            trd = TEN()
            bld.emit(self.evs, "copy" if self.evs == "s" else "tensor_copy",
                     dst_ap, pbf[b][0:w0, 0:128],
                     waits=[(ten.h, trd)] + list(evac_waits),
                     inc=(scl.h if self.evs == "s" else vec.h))
            self.ev[b] = SCL() if self.evs == "s" else VEC()
            return self.ev[b]

    bld.dma("g", g_ident, identin, dw)
    bld.dma("g", g_pswap, pswapin, dw)
    CONST_DONE = dw.n

    phase_reset(reuse_o=True)
    w_sb = alloc(KC * H, BF16)
    lnx_g = [alloc(KC * 512, BF16) for _ in range(2)]
    mark = off[BF16]
    rc_t = alloc(CTX, BF16)
    rs_t = alloc(CTX, BF16)
    end_kq = off[BF16]
    off[BF16] = mark
    c_vft = [alloc(512, BF16) for _ in range(2)]
    c_vtk = [alloc(512, BF16) for _ in range(2)]
    c_cvft = alloc(NH * NCH, BF16)
    off[BF16] = max(off[BF16], end_kq)
    b_v1 = [alloc(512, BF16) for _ in range(2)]
    b_t1 = [alloc(512, BF16) for _ in range(2)]
    b_kr = [alloc(512, BF16) for _ in range(2)]
    b_pool = alloc(64, F32)

    def lnx_load(gp, gsl, waits):
        v3 = lnx_g[gp].rearrange("p (kc g c) -> p kc g c", g=4, c=128)
        for j in range(4):
            n = bld.dma("y", v3[:, :, j, :],
                        lnx_h[4 * gsl + j].transpose([1, 0, 2]),
                        dlp[gp], waits=(waits if j == 0 else ()))
        return n

    def qkv_pass(kind, preload=None):
        wsrc = {"k": wkT, "v": wvT, "q": wqT}[kind]
        if preload is None:
            bld.dma("g", w_sb.rearrange("p (kc c) -> p kc c", c=H),
                    wsrc.rearrange("(kc p) c -> p kc c", p=128), dw)
            if kind == "q":
                bld.dma("g", rc_t[:, 0:OWN], qcos, dw)
                bld.dma("g", rs_t[:, 0:OWN], qs3, dw)
            wld = dw.n
        else:
            wld = preload
        tscale = 0
        if kind == "q":
            bld.emit("v", "tensor_scalar", rc_t[:, 0:OWN], rc_t[:, 0:OWN],
                     float(SCALE), None, ALU.mult, waits=[(dw.h, wld)])
            bld.emit("v", "tensor_scalar", rs_t[:, 0:OWN], rs_t[:, 0:OWN],
                     float(SCALE), None, ALU.mult, inc=vec.h)
            tscale = VEC()
        glist = list(range(NG)) if kind != "q" else list(range(NG // 2, NG))
        NI = len(glist) * NH
        ld_n = [0, 0]
        ps_free = {}
        psw_free = {}
        kr_store = {}
        r1_done = {}
        r2_done = {}
        ev_done = {}
        pl_done = {}
        grp_mm = {}
        for gi in range(min(2, len(glist))):
            ld_n[gi % 2] = lnx_load(gi % 2, glist[gi],
                                    [(sag[glist[gi] % 8].h,
                                      64 * (glist[gi] // 8 + 1))]
                                    if kind == "k" else ())

        def st_proj(i):
            gi, m = divmod(i, NH)
            gp = gi % 2
            gsl = glist[gi]
            pk = i % 2
            wv = []
            if m == 0:
                wv.append((dlp[gp].h, ld_n[gp]))
                if gi == 0:
                    wv.append((dw.h, wld))
            if i - 2 in ps_free:
                wv.append(((vec.h if kind != "v" else scl.h), ps_free[i - 2]))
            for kcI in range(KC):
                bld.emit("t", "matmul", pbf[pk][:, 0:512],
                         w_sb[:, H * kcI + 128 * m: H * kcI + 128 * (m + 1)],
                         lnx_g[gp][:, 512 * kcI:512 * (kcI + 1)],
                         start=(kcI == 0), stop=(kcI == KC - 1),
                         waits=(wv if kcI == 0 else ()),
                         inc=(ten.h if kcI == KC - 1 else None))
            grp_mm[i] = TEN()
            if m == NH - 1 and gi + 2 < len(glist):
                gn = glist[gi + 2]
                ld_n[gp] = lnx_load(gp, gn,
                                    [(ten.h, grp_mm[i])] +
                                    ([(sag[gn % 8].h, 64 * (gn // 8 + 1))]
                                     if kind == "k" else []))

        def st_rope1(i):
            gi, m = divmod(i, NH)
            gsl = glist[gi]
            pk = i % 2
            tb0 = 512 * gsl if kind == "k" else 512 * (gsl - NG // 2)
            wv = [(ten.h, grp_mm[i])]
            if tscale and i == 0:
                wv.append((vec.h, tscale))
            if i - 2 in r2_done:
                wv.append((vec.h, r2_done[i - 2]))
            bld.emit("v", "tensor_tensor", b_v1[pk], pbf[pk][:, 0:512],
                     rs_t[:, tb0:tb0 + 512], ALU.mult, waits=wv)
            bld.emit("v", "tensor_tensor", b_t1[pk], pbf[pk][:, 0:512],
                     rc_t[:, tb0:tb0 + 512], ALU.mult, inc=vec.h)
            r1_done[i] = VEC()
            ps_free[i] = r1_done[i]

        def st_swap(i):
            pk = i % 2
            wv = [(vec.h, r1_done[i])]
            if i - 2 in r2_done:
                wv.append((vec.h, r2_done[i - 2]))
            bld.emit("t", "matmul", pbf[4 + pk][:, 0:512], g_pswap,
                     b_v1[pk], start=True, stop=True, waits=wv, inc=ten.h)
            psw_free[i] = TEN()

        def st_rope2(i):
            gi, m = divmod(i, NH)
            gsl = glist[gi]
            pk = i % 2
            wv = [(ten.h, psw_free[i])]
            if i - 2 in kr_store:
                wv.append((stp[pk].h, kr_store[i - 2]))
            bld.emit("v", "tensor_tensor", b_kr[pk], b_t1[pk],
                     pbf[4 + pk][:, 0:512], ALU.add, waits=wv, inc=vec.h)
            v3 = VEC()
            if kind == "k":
                bld.emit("v", "reduce_sum", b_pool[:, 0:32],
                         b_kr[pk].rearrange("p (a b) -> p a b", b=16),
                         axis=AX.X, waits=[(vec.h, vec.n)], inc=vec.h)
                VEC()
                bld.emit("v", "tensor_scalar",
                         g_ck[:, NCH * m + 32 * gsl:NCH * m + 32 * (gsl + 1)],
                         b_pool[:, 0:32], 1.0 / 16.0, None, ALU.mult,
                         waits=[(vec.h, vec.n)], inc=vec.h)
                v3 = VEC()
            r2_done[i] = v3
            if kind == "k":
                lo_tok = 512 * gsl
                lo = max(0, (PAD - WINDOW) - lo_tok)
                if lo < 512:
                    u0 = (lo_tok + lo - (PAD - WINDOW)) // 128
                    kr_store[i] = bld.dma(
                        "g", kwin_h[m][:, 128 * u0:128 * u0 + (512 - lo)],
                        b_kr[pk][:, lo:512], stp[pk], waits=[(vec.h, v3)])
                else:
                    kr_store[i] = 0
            else:
                gq = gsl - NG // 2
                kr_store[i] = bld.dma("g", q_hbm[m][:, 512 * gq:512 * (gq + 1)],
                                      b_kr[pk], stp[pk], waits=[(vec.h, v3)])

        tpipeV = TrPipe((6, 7), "v")

        def st_vev(i):
            pk = i % 2
            ew = [(ten.h, grp_mm[i])]
            if i - 2 in pl_done:
                ew.append((vec.h, pl_done[i - 2]))
                if ev_t_free.get(i - 2):
                    ew.append((ten.h, ev_t_free[i - 2]))
            bld.emit("s", "copy", c_vft[pk], pbf[pk][:, 0:512],
                     waits=ew, inc=scl.h)
            ev_done[i] = SCL()
            ps_free[i] = ev_done[i]

        ev_t_free = {}

        def st_vpool(i):
            gi, m = divmod(i, NH)
            gsl = glist[gi]
            pk = i % 2
            bld.emit("v", "reduce_sum", b_pool[:, 0:32],
                     c_vft[pk].rearrange("p (a b) -> p a b", b=16),
                     axis=AX.X, waits=[(scl.h, ev_done[i]), (vec.h, vec.n)],
                     inc=vec.h)
            VEC()
            bld.emit("v", "tensor_scalar",
                     c_cvft[:, NCH * m + 32 * gsl:NCH * m + 32 * (gsl + 1)],
                     b_pool[:, 0:32], 1.0 / 16.0, None, ALU.mult,
                     waits=[(vec.h, vec.n)], inc=vec.h)
            pl_done[i] = VEC()
            lo_tok = 512 * gsl
            lo = max(0, (PAD - WINDOW) - lo_tok)
            if lo < 512:
                evw = [(stp[pk].h, kr_store[i - 2])] if kr_store.get(i - 2) else []
                for j in range(lo // 128, 4):
                    tpipeV.go(c_vft[pk][:, 128 * j:128 * (j + 1)],
                              c_vtk[pk][:, 128 * j:128 * (j + 1)],
                              extra_waits=([(scl.h, ev_done[i])]
                                           if j == lo // 128 else ()),
                              evac_waits=(evw if j == lo // 128 else ()))
                u0 = (lo_tok + lo - (PAD - WINDOW)) // 128
                kr_store[i] = bld.dma(
                    "g", vwin_h[m][:, 128 * u0:128 * u0 + (512 - lo)],
                    c_vtk[pk][:, lo:512], stp[pk], waits=[(vec.h, vec.n)])
                ev_t_free[i] = ten.n
            else:
                kr_store[i] = 0
                ev_t_free[i] = 0

        if kind == "v":
            for step in range(NI + 2):
                if step < NI:
                    st_proj(step)
                if 1 <= step <= NI:
                    st_vev(step - 1)
                if step >= 2:
                    st_vpool(step - 2)
                yield step
        else:
            for step in range(NI + 2):
                if step < NI:
                    st_proj(step)
                if 1 <= step <= NI:
                    st_rope1(step - 1)
                    st_swap(step - 1)
                if step >= 2:
                    st_rope2(step - 2)
                yield step


    a_x = [alloc(H, F32) for _ in range(2)]
    a_stat = alloc(48, F32)
    a_ag = alloc(4, F32)
    a_eps = alloc(2, F32)
    a_istd = alloc(4, F32)
    a_nmu = alloc(2, F32)
    ATOP = ((NB16 - 16384) // 32) * 32
    assert off[BF16] <= ATOP, f"A-top overlap {off[BF16]} > {ATOP}"
    a_ln = [h16[:, (ATOP + 4096 * i) // 2:(ATOP + 4096 * i) // 2 + H]
            for i in range(2)]
    a_tr = [h16[:, (ATOP + 8192 + 4096 * i) // 2:(ATOP + 8192 + 4096 * i) // 2 + H]
            for i in range(2)]

    bld.dma("g", w_sb.rearrange("p (kc c) -> p kc c", c=H),
            wkT.rearrange("(kc p) c -> p kc c", p=128), dw)
    bld.dma("g", rc_t, kcos, dw)
    bld.dma("g", rs_t, ks3, dw)
    WLD_K = dw.n

    a_xfree = [0, 0]
    a_trdone = [0, 0]
    a_stdone = [0, 0]
    a_vln = {}
    a_vagg = {}
    a_srec = {}
    tpipeA = TrPipe((6, 7), "s")

    def a_s1(t):
        p = t % 2
        ldv = bld.dma("y", a_x[p], x_ctx[128 * t:128 * (t + 1), :], dyp[p],
                      waits=([(vec.h, a_xfree[p])] if t >= 2 else ()))
        for cchunk in range(NSQ):
            bld.emit("v", "bn_stats", a_stat[:, 12 * p + 6 * cchunk // 1:
                                             12 * p + 6 * cchunk + 6]
                     if False else a_stat[:, 24 * p + 6 * cchunk:24 * p + 6 * cchunk + 6],
                     a_x[p][:, 512 * cchunk:512 * (cchunk + 1)],
                     waits=([(dyp[p].h, ldv)] if cchunk == 0 else ()),
                     inc=(vec.h if cchunk == NSQ - 1 else None))
        VEC()
        bld.emit("v", "bn_aggr", a_ag[:, 2 * p:2 * p + 2],
                 a_stat[:, 24 * p:24 * p + 6 * NSQ],
                 waits=[(vec.h, vec.n)], inc=vec.h)
        VEC()
        bld.emit("v", "tensor_scalar", a_eps[:, p:p + 1], a_ag[:, 2 * p + 1:2 * p + 2],
                 EPS, None, ALU.add, waits=[(vec.h, vec.n)], inc=vec.h)
        a_vagg[t] = VEC()

    def a_s2(t):
        p = t % 2
        bld.emit("s", "sqrt", a_istd[:, 2 + p:3 + p], a_eps[:, p:p + 1],
                 waits=[(vec.h, a_vagg[t])], inc=scl.h)
        srec = SCL()
        bld.emit("v", "reciprocal", a_istd[:, p:p + 1], a_istd[:, 2 + p:3 + p],
                 waits=[(scl.h, srec)], inc=vec.h)
        VEC()
        bld.emit("v", "tensor_scalar", a_nmu[:, p:p + 1], a_ag[:, 2 * p:2 * p + 1],
                 a_istd[:, p:p + 1], -1.0, ALU.mult, ALU.mult,
                 waits=[(vec.h, vec.n)], inc=vec.h)
        VEC()
        bld.emit("v", "tensor_scalar", a_ln[p], a_x[p], a_istd[:, p:p + 1],
                 a_nmu[:, p:p + 1], ALU.mult, ALU.add,
                 waits=[(vec.h, vec.n)] +
                       ([(ten.h, a_trdone[p])] if t >= 2 else []), inc=vec.h)
        a_vln[t] = VEC()
        a_xfree[p] = a_vln[t]

    def a_s3(t):
        p = t % 2
        for kcI in range(KC):
            ew = [(sag[((t - 2) // 4) % 8].h, a_stdone[p])] \
                if (kcI == 0 and t >= 2) else []
            xw = [(vec.h, a_vln[t])] if kcI == 0 else []
            if t == 0 and kcI == 0:
                xw.append((dw.h, CONST_DONE))
            tpipeA.go(a_ln[p][:, 128 * kcI:128 * (kcI + 1)],
                      a_tr[p][:, 128 * kcI:128 * (kcI + 1)],
                      extra_waits=xw, evac_waits=ew)
        a_trdone[p] = ten.n
        evd = scl.n
        sg = sag[(t // 4) % 8]
        sg.n += 16
        bld.emit("g", "dma_start", lnx_h[t].transpose([1, 0, 2]),
                 a_tr[p].rearrange("p (kc c) -> p kc c", c=128),
                 waits=[(scl.h, evd)], inc=sg.h, inck=16)
        a_stdone[p] = sg.n

    kgen = qkv_pass("k", WLD_K)
    for step in range(CT + 2):
        if step < CT:
            a_s1(step)
        if 1 <= step <= CT:
            a_s2(step - 1)
        if step >= 2:
            a_s3(step - 2)
    for _ in kgen:
        pass
    for sgx in sag:
        bld.wait("g", sgx.h, sgx.n)
    bld.wait("g", st0.h, st0.n)
    bld.wait("g", st1.h, st1.n)
    barrier()
    for _ in qkv_pass("v"):
        pass
    tpipeCV = TrPipe((6, 7), "v")
    first_cv = True
    for m in range(NH):
        for cc in range(CVW):
            w0 = min(128, NCH - 128 * cc)
            xw = [(vec.h, vec.n)] if first_cv else []
            if first_cv:
                xw.append((ten.h, ten.n))
            first_cv = False
            tpipeCV.go(c_cvft[:, NCH * m + 128 * cc:NCH * m + 128 * cc + w0],
                       g_cv[0:w0,
                            (NH * 128) * cc + 128 * m:(NH * 128) * cc + 128 * (m + 1)],
                       w0=w0, extra_waits=xw)
    bld.wait("g", st0.h, st0.n)
    bld.wait("g", st1.h, st1.n)
    barrier()
    for _ in qkv_pass("q"):
        pass
    bld.wait("g", st0.h, st0.n)
    bld.wait("g", st1.h, st1.n)
    barrier()
    B_ST0, B_ST1 = st0.n, st1.n

    phase_reset()
    e_cm = alloc(QT * NCH, BF16)
    e_wm = alloc(QT * 384, BF16)
    e_k = [alloc(OWN + WINDOW, BF16) for _ in range(2)]
    e_v = [alloc(WT * 128, BF16) for _ in range(2)]
    e_q = [alloc(OWN, BF16) for _ in range(2)]
    e_pc = [alloc(NCH, BF16) for _ in range(4)]
    e_pw = [alloc(384, BF16) for _ in range(4)]
    e_sc = [alloc(NCH, BF16) for _ in range(4)]
    e_sw = [alloc(384, BF16) for _ in range(4)]
    NPT = CVW + 3
    e_pt = [alloc(NPT * 128, BF16) for _ in range(2)]
    e_D = [alloc(128, BF16) for _ in range(4)]
    e_nm = alloc(16, F32)

    bld.dma("g", e_cm.rearrange("p (t c) -> p t c", c=NCH),
            cmask.transpose([1, 0, 2]), dm2)
    mld = bld.dma("g", e_wm.rearrange("p (t c) -> p t c", c=384),
                  wmask.transpose([1, 0, 2]), dm2)
    bld.dma("g", wobf_h, woT, dw)
    PRE_WO = dw.n
    bld.dma("g", wrtbf_h, wrt, dw)
    PRE_WRT = dw.n
    for e in range(NE):
        bld.dma("g", w1bf_h[e], ws1 if e == 0 else w1[e - 1], dw)
        bld.dma("g", w2bf_h[e], ws2 if e == 0 else w2[e - 1], dw)
    PRECAST_DONE = dw.n

    kv_ld = {}
    for m in range(min(2, NH)):
        p = m % 2
        bld.dma("y", e_k[p], kwin_h[m],
                dyp[p], waits=([(stp[0].h, B_ST0), (stp[1].h, B_ST1)]
                               if m == 0 else ()))
        bld.dma("y", e_v[p], vwin_h[m], dyp[p])
        kv_ld[m] = bld.dma("y", e_q[p], q_hbm[m], dyp[p])

    N_IT = NH * QT
    sc_free = {}
    sw_free = {}
    o_free = {}
    pc_free = {}
    av_done = {}
    nmd_d = {}
    scd_d = {}
    exp_done = {}
    dbl_d = {}
    tr_ev = {6: 0, 7: 0}
    tr_ev_s = {6: "v", 7: "v"}
    tr_i = [0]

    def e_scores(i):
        m, t = divmod(i, QT)
        p = m % 2
        tp = i % 2
        wc = PADC + 8 * (t + 1)
        wv = [(dyp[p].h, kv_ld[m])] if t == 0 else []
        if i == 0:
            wv.append((dm2.h, mld))
        if i - 2 in sc_free:
            wv.append((scl.h, sc_free[i - 2]))
        bld.emit("t", "matmul", pbf[tp][:, 0:wc], g_ident,
                 e_cm[:, NCH * t:NCH * t + wc], start=True, stop=False,
                 waits=wv)
        bld.emit("t", "matmul", pbf[tp][:, 0:wc],
                 e_q[p][:, 128 * t:128 * (t + 1)],
                 g_ck[:, NCH * m:NCH * m + wc], start=False, stop=True,
                 inc=ten.h)
        scd_d[i] = TEN()
        p4 = i % 4
        ew = [(ten.h, scd_d[i])]
        if i - 4 in nmd_d:
            ew.append((vec.h, nmd_d[i - 4]))
        bld.emit("s", "copy", e_sc[p4][:, 0:wc], pbf[tp][:, 0:wc],
                 waits=ew, inc=scl.h)
        sc_free[i] = SCL()
        wv = [(scl.h, sw_free[i - 2])] if i - 2 in sw_free else []
        bld.emit("t", "matmul", pbf[2 + tp][:, 0:384], g_ident,
                 e_wm[:, 384 * t:384 * (t + 1)], start=True, stop=False,
                 waits=wv)
        bld.emit("t", "matmul", pbf[2 + tp][:, 0:384],
                 e_q[p][:, 128 * t:128 * (t + 1)],
                 e_k[p][:, 128 * t:128 * t + 384], start=False, stop=True,
                 inc=ten.h)
        st = TEN()
        ew = [(ten.h, st)]
        if i - 4 in nmd_d:
            ew.append((vec.h, nmd_d[i - 4]))
        bld.emit("s", "copy", e_sw[p4][:, 0:384], pbf[2 + tp][:, 0:384],
                 waits=ew, inc=scl.h)
        sw_free[i] = SCL()

    def e_sm1(i):
        m, t = divmod(i, QT)
        tp = i % 2
        p4 = i % 4
        nb = 8 * tp
        wc = PADC + 8 * (t + 1)
        wv = [(scl.h, sc_free[i])]
        if i - 2 in exp_done:
            wv.append((scl.h, exp_done[i - 2]))
        bld.emit("v", "reduce_max", e_nm[:, nb:nb + 1], e_sc[p4][:, 0:wc],
                 axis=AX.X, negate=True, waits=wv)
        bld.emit("v", "reduce_max", e_nm[:, nb + 1:nb + 2],
                 e_sw[p4][:, 0:384], axis=AX.X, negate=True,
                 waits=[(scl.h, sw_free[i])], inc=vec.h)
        VEC()
        bld.emit("v", "tensor_tensor", e_nm[:, nb + 2:nb + 3],
                 e_nm[:, nb:nb + 1], e_nm[:, nb + 1:nb + 2], ALU.min,
                 waits=[(vec.h, vec.n)], inc=vec.h)
        nmd_d[i] = VEC()

    def e_sm2(i):
        m, t = divmod(i, QT)
        tp = i % 2
        nb = 8 * tp
        wc = PADC + 8 * (t + 1)
        p4 = i % 4
        ew = [(vec.h, nmd_d[i])]
        if i - 4 in pc_free:
            ew.append((ten.h, pc_free[i - 4]))
        bld.emit("s", "activation", e_pc[p4][:, 0:wc], e_sc[p4][:, 0:wc],
                 ACT.Exp, bias=e_nm[:, nb + 2:nb + 3], scale=1.0,
                 accum_out=e_nm[:, nb + 3:nb + 4], waits=ew, inc=scl.h)
        SCL()
        bld.emit("s", "activation", e_pw[p4][:, 0:384], e_sw[p4][:, 0:384],
                 ACT.Exp, bias=e_nm[:, nb + 2:nb + 3], scale=1.0,
                 accum_out=e_nm[:, nb + 4:nb + 5], inc=scl.h)
        exp_done[i] = SCL()

    def e_sm3(i):
        tp = i % 2
        nb = 8 * tp
        bld.emit("v", "tensor_tensor", e_nm[:, nb + 5:nb + 6],
                 e_nm[:, nb + 3:nb + 4], e_nm[:, nb + 4:nb + 5], ALU.add,
                 waits=[(scl.h, exp_done[i])], inc=vec.h)
        VEC()
        bld.emit("v", "reciprocal", e_nm[:, nb + 6:nb + 7],
                 e_nm[:, nb + 5:nb + 6], waits=[(vec.h, vec.n)], inc=vec.h)
        VEC()
        dw2 = [(vec.h, vec.n)]
        if i - 4 in pc_free:
            dw2.append((ten.h, pc_free[i - 4]))
        bld.emit("v", "tensor_scalar", e_D[i % 4], g_ident,
                 e_nm[:, nb + 6:nb + 7], None, ALU.mult, waits=dw2,
                 inc=vec.h)
        dbl_d[i] = VEC()

    def e_tr(i):
        m, t = divmod(i, QT)
        tp = i % 2
        wc = PADC + 8 * (t + 1)
        nwc = (wc + 127) // 128
        p4 = i % 4
        for j in range(nwc + 3):
            if j < nwc:
                w0 = min(128, wc - 128 * j)
                src = e_pc[p4][:, 128 * j:128 * j + w0]
            else:
                w0 = 128
                src = e_pw[p4][:, 128 * (j - nwc):128 * (j - nwc + 1)]
            b = (6, 7)[tr_i[0] % 2]
            tr_i[0] += 1
            wv = []
            if j == 0:
                wv = [(vec.h, dbl_d[i]), (scl.h, exp_done[i])]
            if tr_ev[b]:
                wv.append(((vec.h if tr_ev_s[b] == "v" else scl.h), tr_ev[b]))
            bld.emit("t", "matmul", pbf[b][0:w0, 0:128], src, e_D[p4],
                     start=True, stop=True, waits=wv, inc=ten.h)
            trd = TEN()
            ew = [(ten.h, trd)]
            if j == 0 and i - 2 in av_done:
                ew.append((ten.h, av_done[i - 2]))
            dst = e_pt[tp][0:w0, 128 * j:128 * (j + 1)]
            if j % 2 == 0:
                bld.emit("v", "tensor_copy", dst, pbf[b][0:w0, 0:128],
                         waits=ew, inc=vec.h)
                tr_ev[b] = VEC()
                tr_ev_s[b] = "v"
            else:
                bld.emit("s", "copy", dst, pbf[b][0:w0, 0:128],
                         waits=ew, inc=scl.h)
                tr_ev[b] = SCL()
                tr_ev_s[b] = "s"
        pc_free[i] = ten.n

    def e_av(i):
        m, t = divmod(i, QT)
        p = m % 2
        tp = i % 2
        wc = PADC + 8 * (t + 1)
        nwc = (wc + 127) // 128
        wv = [(vec.h, tr_ev[6] if tr_ev_s[6] == "v" else 0),
              (vec.h, tr_ev[7] if tr_ev_s[7] == "v" else 0),
              (scl.h, tr_ev[6] if tr_ev_s[6] == "s" else 0),
              (scl.h, tr_ev[7] if tr_ev_s[7] == "s" else 0)]
        wv = [w for w in wv if w[1]]
        if i - 2 in o_free:
            wv.append((scl.h, o_free[i - 2]))
        first = True
        for j in range(nwc):
            w0 = min(128, wc - 128 * j)
            bld.emit("t", "matmul", pbf[4 + tp][:, 0:128],
                     g_cv[0:w0, (NH * 128) * j + 128 * m:(NH * 128) * j + 128 * (m + 1)],
                     e_pt[tp][0:w0, 128 * j:128 * (j + 1)],
                     start=first, stop=False, waits=(wv if first else ()))
            first = False
        for j in range(3):
            last = (j == 2)
            bld.emit("t", "matmul", pbf[4 + tp][:, 0:128],
                     e_v[p][:, 128 * (t + j):128 * (t + j + 1)],
                     e_pt[tp][:, 128 * (nwc + j):128 * (nwc + j + 1)],
                     start=False, stop=last, inc=(ten.h if last else None))
        av_done[i] = TEN()
        bld.emit("s", "copy", g_o[:, OWN * m + 128 * t:OWN * m + 128 * (t + 1)],
                 pbf[4 + tp][:, 0:128], waits=[(ten.h, av_done[i])],
                 inc=scl.h)
        o_free[i] = SCL()
        if t == QT - 1 and m + 2 < NH:
            bld.dma("y", e_k[p], kwin_h[m + 2], dyp[p],
                    waits=[(ten.h, ten.n)])
            bld.dma("y", e_v[p], vwin_h[m + 2], dyp[p])
            kv_ld[m + 2] = bld.dma("y", e_q[p], q_hbm[m + 2], dyp[p])

    for step in range(N_IT + 3):
        if step < N_IT:
            e_scores(step)
        if 1 <= step <= N_IT:
            e_sm1(step - 1)
            e_sm2(step - 1)
        if 2 <= step <= N_IT + 1:
            e_sm3(step - 2)
        if step >= 3:
            e_tr(step - 3)
            e_av(step - 3)
    barrier()

    phase_reset()
    f_wo = alloc(KC * H, BF16)
    f_xr = [alloc(H, F32) for _ in range(2)]
    f_ln2 = alloc(H, BF16)
    f_ln = [f_ln2, f_ln2]
    f_tr = [alloc(KC * 128, BF16) for _ in range(2)]
    f_wr = alloc(KC * 8, BF16)
    f_stat = alloc(6 * NSQ, F32)
    f_ag = alloc(4, F32)
    f_eps = alloc(2, F32)
    f_istd = alloc(2, F32)
    f_nmu = alloc(2, F32)
    f_lg = alloc(8, F32)
    f_m8 = alloc(8, F32)
    f_nm1 = alloc(2, F32)
    f_den = alloc(2, F32)
    f_sel = alloc(8, F32)

    wold = bld.dma("s", f_wo.rearrange("p (kc c) -> p kc c", c=H),
                   wobf_h.rearrange("(kc p) c -> p kc c", p=128), ds1,
                   waits=[(dw.h, PRE_WO)])
    wrld = bld.dma("s", f_wr.rearrange("p (kc c) -> p kc c", c=8),
                   wrtbf_h.rearrange("(kc p) c -> p kc c", p=128), ds2,
                   waits=[(dw.h, PRE_WRT)])
    x_ld = [0, 0]
    for t in range(min(2, QT)):
        x_ld[t % 2] = bld.dma("y", f_xr[t % 2],
                              x_ctx[PAD + 128 * t:PAD + 128 * (t + 1), :],
                              dyp[t % 2])
    f_psfree = {}
    f_store = [0, 0]
    f_trlast = 0
    f_r1free = [0, 0]
    rt_evd = 0
    tpipeF = TrPipe((6, 7), "s")
    for t in range(QT):
        p = t % 2
        for jg in range(JG):
            bk = jg % 2
            wv = []
            if t == 0 and jg == 0:
                wv.append((ds1.h, wold))
            if bk in f_psfree:
                wv.append((vec.h, f_psfree[bk]))
            for kcI in range(KC):
                bld.emit("t", "matmul", pbf[bk][:, 0:512],
                         g_o[:, OWN * kcI + 128 * t:OWN * kcI + 128 * (t + 1)],
                         f_wo[:, H * kcI + 512 * jg:H * kcI + 512 * (jg + 1)],
                         start=(kcI == 0), stop=(kcI == KC - 1),
                         waits=(wv if kcI == 0 else ()),
                         inc=(ten.h if kcI == KC - 1 else None))
            opd = TEN()
            wv = [(ten.h, opd), (dyp[p].h, x_ld[p])]
            if jg == 0 and t >= 2:
                wv.append((sfp[p].h, f_store[p]))
                wv.append((vec.h, f_r1free[p]))
            bld.emit("v", "tensor_tensor", f_xr[p][:, 512 * jg:512 * (jg + 1)],
                     pbf[bk][:, 0:512], f_xr[p][:, 512 * jg:512 * (jg + 1)],
                     ALU.add, waits=wv, inc=vec.h)
            f_psfree[bk] = VEC()
        r1d = vec.n
        bld.dma("y", r1_h[128 * t:128 * (t + 1), :], f_xr[p], sfp[p],
                waits=[(vec.h, r1d)])
        f_store[p] = sfp[p].n
        for cchunk in range(NSQ):
            bld.emit("v", "bn_stats", f_stat[:, 6 * cchunk:6 * cchunk + 6],
                     f_xr[p][:, 512 * cchunk:512 * (cchunk + 1)],
                     waits=([(vec.h, vec.n)] if cchunk == 0 else ()),
                     inc=(vec.h if cchunk == NSQ - 1 else None))
        VEC()
        bld.emit("v", "bn_aggr", f_ag[:, 0:2], f_stat[:, 0:6 * NSQ],
                 waits=[(vec.h, vec.n)], inc=vec.h)
        VEC()
        bld.emit("v", "tensor_scalar", f_eps[:, 0:1], f_ag[:, 1:2], EPS, None,
                 ALU.add, waits=[(vec.h, vec.n)], inc=vec.h)
        vrec = VEC()
        bld.emit("s", "activation", f_istd[:, 1:2], f_eps[:, 0:1], ACT.Ln,
                 waits=[(vec.h, vrec)], inc=scl.h)
        SCL()
        bld.emit("s", "activation", f_istd[:, 0:1], f_istd[:, 1:2], ACT.Exp,
                 scale=-0.5, waits=[(scl.h, scl.n)], inc=scl.h)
        srec = SCL()
        bld.emit("v", "tensor_scalar", f_nmu[:, 0:1], f_ag[:, 0:1],
                 f_istd[:, 0:1], -1.0, ALU.mult, ALU.mult,
                 waits=[(scl.h, srec)], inc=vec.h)
        VEC()
        bld.emit("v", "tensor_scalar", f_ln[p], f_xr[p], f_istd[:, 0:1],
                 f_nmu[:, 0:1], ALU.mult, ALU.add,
                 waits=[(vec.h, vec.n)] +
                       ([(ten.h, f_trlast)] if t >= 1 else []), inc=vec.h)
        lnd = VEC()
        f_r1free[p] = lnd
        if t + 2 < QT:
            x_ld[p] = bld.dma("y", f_xr[p],
                              x_ctx[PAD + 128 * (t + 2):PAD + 128 * (t + 3), :],
                              dyp[p],
                              waits=[(vec.h, lnd), (sfp[p].h, f_store[p])])
        for kcI in range(KC):
            ew = []
            if kcI == 0 and t >= 2:
                ew.append((sfp[p].h, f_store[p]))
            tpipeF.go(f_ln[p][:, 128 * kcI:128 * (kcI + 1)],
                      f_tr[p][:, 128 * kcI:128 * (kcI + 1)],
                      extra_waits=([(vec.h, lnd)] if kcI == 0 else ()),
                      evac_waits=ew)
        f_trlast = ten.n
        evd = scl.n
        wv = [(scl.h, evd), (ds2.h, wrld)]
        if rt_evd:
            wv.append((vec.h, rt_evd))
        for kcI in range(KC):
            bld.emit("t", "matmul", pbf[4][:, 0:8],
                     f_tr[p][:, 128 * kcI:128 * (kcI + 1)],
                     f_wr[:, 8 * kcI:8 * (kcI + 1)],
                     start=(kcI == 0), stop=(kcI == KC - 1),
                     waits=(wv if kcI == 0 else ()),
                     inc=(ten.h if kcI == KC - 1 else None))
        rtd = TEN()
        bld.emit("v", "tensor_copy", f_lg[:, 0:8], pbf[4][:, 0:8],
                 waits=[(ten.h, rtd)], inc=vec.h)
        rt_evd = VEC()
        bld.emit("v", "max", f_m8[:, 0:8], f_lg[:, 0:8],
                 waits=[(vec.h, vec.n)], inc=vec.h)
        VEC()
        bld.emit("v", "tensor_scalar", f_nm1[:, 0:1], f_m8[:, 0:1], -1.0, None,
                 ALU.mult, waits=[(vec.h, vec.n)])
        bld.emit("v", "tensor_scalar", f_sel[:, 0:8], f_lg[:, 0:8],
                 f_m8[:, 1:2], None, ALU.is_ge, inc=vec.h)
        seld = VEC()
        bld.emit("s", "activation", f_den[:, 0:1], f_m8[:, 1:2], ACT.Exp,
                 bias=f_nm1[:, 0:1], scale=1.0, waits=[(vec.h, seld)])
        bld.emit("s", "activation", f_lg[:, 0:8], f_lg[:, 0:8], ACT.Exp,
                 bias=f_nm1[:, 0:1], scale=1.0, inc=scl.h)
        expd = SCL()
        bld.emit("v", "tensor_scalar", f_den[:, 1:2], f_den[:, 0:1], 1.0, None,
                 ALU.add, waits=[(scl.h, expd)], inc=vec.h)
        VEC()
        bld.emit("v", "reciprocal", f_den[:, 0:1], f_den[:, 1:2],
                 waits=[(vec.h, vec.n)], inc=vec.h)
        VEC()
        bld.emit("v", "tensor_tensor", f_sel[:, 0:8], f_sel[:, 0:8],
                 f_lg[:, 0:8], ALU.mult, waits=[(vec.h, vec.n)], inc=vec.h)
        VEC()
        bld.emit("v", "tensor_scalar", g_comb[:, 8 * t:8 * (t + 1)],
                 f_sel[:, 0:8], f_den[:, 0:1], None, ALU.mult,
                 waits=[(vec.h, vec.n)], inc=vec.h)
        VEC()
        bld.dma("y", lnx2_h[t].transpose([1, 0, 2]),
                f_tr[p].rearrange("p (kc c) -> p kc c", c=128),
                sfp[p], waits=[(scl.h, evd), (ten.h, rtd)])
        f_store[p] = sfp[p].n
    bld.wait("y", sf0.h, sf0.n)
    bld.wait("y", sf1.h, sf1.n)
    barrier()
    F_ST0, F_ST1 = sf0.n, sf1.n

    phase_reset(reuse_o=True)
    g_w1 = alloc(KC * I, BF16)
    g_w2 = alloc(ICH * H, BF16)
    g_h = alloc(ICH * 512, BF16)
    g_acc = alloc(4 * H, F32)
    g_lnx2 = alloc(KC * 512, BF16)
    g_r1 = [alloc(H, F32) for _ in range(2)]

    w1_ld = bld.dma("s", g_w1.rearrange("p (kc c) -> p kc c", c=I),
                    w1bf_h[0].rearrange("(kc p) c -> p kc c", p=128), ds1,
                    waits=[(dw.h, PRECAST_DONE)])
    w2_ld = bld.dma("s", g_w2.rearrange("p (ic c) -> p ic c", c=H),
                    w2bf_h[0].rearrange("(ic p) c -> p ic c", p=128), ds2)
    h_done = {}
    yst = [0, 0, 0, 0]
    od_g = {}
    l2 = 0
    prev_hall = 0
    for pss in range(NPASS):
        l2v = g_lnx2.rearrange("p (kc g c) -> p kc g c", g=4, c=128)
        for j in range(4):
            l2 = bld.dma("y", l2v[:, :, j, :],
                         lnx2_h[4 * pss + j].transpose([1, 0, 2]), dyp[0],
                         waits=(([(sf0.h, F_ST0), (sf1.h, F_ST1)]
                                 if pss == 0 else [(scl.h, prev_hall)])
                                if j == 0 else ()))
        acc_part = {}
        acc_done = {}
        for e in range(NE):
            for ic in range(ICH):
                bk = ic % 2
                wv = []
                if ic == 0:
                    wv.append((ds1.h, w1_ld))
                    wv.append((dyp[0].h, l2))
                    if e >= 1:
                        wv.append((vec.h, acc_done[e - 1]))
                if bk in h_done:
                    wv.append((scl.h, h_done[bk]))
                for kcI in range(KC):
                    bld.emit("t", "matmul", pbf[bk][:, 0:512],
                             g_w1[:, I * kcI + 128 * ic:I * kcI + 128 * (ic + 1)],
                             g_lnx2[:, 512 * kcI:512 * (kcI + 1)],
                             start=(kcI == 0), stop=(kcI == KC - 1),
                             waits=(wv if kcI == 0 else ()),
                             inc=(ten.h if kcI == KC - 1 else None))
                w1d = TEN()
                bld.emit("s", "activation", g_h[:, 512 * ic:512 * (ic + 1)],
                         pbf[bk][:, 0:512], ACT.Silu, waits=[(ten.h, w1d)],
                         inc=scl.h)
                h_done[bk] = SCL()
            hall = scl.n
            if not (pss == NPASS - 1 and e == NE - 1):
                w1_ld = bld.dma("s", g_w1.rearrange("p (kc c) -> p kc c", c=I),
                                w1bf_h[(e + 1) % NE].rearrange("(kc p) c -> p kc c", p=128),
                                ds1)
            for tt in range(4):
                gt = 4 * pss + tt
                for jg in range(JG):
                    bk = 2 + ((tt * JG + jg) % 2)
                    wv = [(scl.h, hall)]
                    if tt == 0 and jg == 0:
                        wv.append((ds2.h, w2_ld))
                    if bk in acc_part:
                        wv.append((vec.h, acc_part[bk]))
                    for ic in range(ICH):
                        bld.emit("t", "matmul", pbf[bk][:, 0:512],
                                 g_h[:, 512 * ic + 128 * tt:512 * ic + 128 * (tt + 1)],
                                 g_w2[:, H * ic + 512 * jg:H * ic + 512 * (jg + 1)],
                                 start=(ic == 0), stop=(ic == ICH - 1),
                                 waits=(wv if ic == 0 else ()),
                                 inc=(ten.h if ic == ICH - 1 else None))
                    w2d = TEN()
                    accsl = g_acc[:, H * tt + 512 * jg:H * tt + 512 * (jg + 1)]
                    if e == 0:
                        aw = [(ten.h, w2d)]
                        if pss > 0 and jg == 0:
                            aw.append((sy.h, yst[tt]))
                        bld.emit("v", "tensor_copy", accsl, pbf[bk][:, 0:512],
                                 waits=aw, inc=vec.h)
                    else:
                        bld.emit("v", "scalar_tensor_tensor", accsl,
                                 pbf[bk][:, 0:512],
                                 g_comb[:, 8 * gt + (e - 1):8 * gt + e], accsl,
                                 ALU.mult, ALU.add, waits=[(ten.h, w2d)],
                                 inc=vec.h)
                    acc_part[bk] = VEC()
                if e == NE - 1:
                    p2 = tt % 2
                    r1w = [(scl.h, hall)]
                    if od_g.get(p2):
                        r1w.append((vec.h, od_g[p2]))
                    r1v = bld.dma("y", g_r1[p2],
                                  r1_h[512 * pss + 128 * tt:
                                       512 * pss + 128 * (tt + 1), :],
                                  dyp[p2], waits=r1w)
                    ow = [(dyp[p2].h, r1v),
                          (vec.h, acc_part[2 + (tt * JG + JG - 1) % 2])]
                    bld.emit("v", "tensor_tensor", g_acc[:, H * tt:H * (tt + 1)],
                             g_acc[:, H * tt:H * (tt + 1)], g_r1[p2], ALU.add,
                             waits=ow, inc=vec.h)
                    od = VEC()
                    od_g[p2] = od
                    yst[tt] = bld.dma("g",
                                      y[512 * pss + 128 * tt:512 * pss + 128 * (tt + 1), :],
                                      g_acc[:, H * tt:H * (tt + 1)], sy,
                                      waits=[(vec.h, od)])
            acc_done[e] = vec.n
            prev_hall = hall
            if not (pss == NPASS - 1 and e == NE - 1):
                w2_ld = bld.dma("s", g_w2.rearrange("p (ic c) -> p ic c", c=H),
                                w2bf_h[(e + 1) % NE].rearrange("(ic p) c -> p ic c", p=128),
                                ds2, waits=[(ten.h, ten.n)])
    bld.wait("g", sy.h, sy.n)
    bld.wait("y", sy.h, sy.n)

    import contextlib
    with contextlib.ExitStack() as cx:
        block = cx.enter_context(nc.Block())
        P = bld.P

        @block.tensor
        def _(t):
            for f in P["t"]:
                f(t)

        @block.vector
        def _(v):
            for f in P["v"]:
                f(v)

        @block.scalar
        def _(s):
            for f in P["s"]:
                f(s)

        @block.gpsimd
        def _(g):
            for f in P["g"]:
                f(g)

        @block.sync
        def _(yq):
            for f in P["y"]:
                f(yq)

    for cm in reversed(semctx):
        cm.__exit__(None, None, None)
    for cm in reversed(pb_ctx):
        cm.__exit__(None, None, None)
    return nc


def rope_tables(pos):
    inv = 1.0 / (10000.0 ** (np.arange(0, 128, 2, dtype=np.float32) / 128))
    inv = inv.astype(np.float32)
    f = (pos.astype(np.float32)[:, None] * inv[None, :]).astype(np.float32)
    fc = np.exp(1j * f).astype(np.complex64)
    c = fc.real.T
    sn = fc.imag.T
    C = np.repeat(c, 2, axis=0)
    S3 = np.empty((128, len(pos)), dtype=np.float32)
    S3[0::2] = sn
    S3[1::2] = -sn
    return np.ascontiguousarray(C), np.ascontiguousarray(S3)


def make_core_inputs(cfg, core, full):
    d = derive(cfg)
    H = cfg["H"]
    OWN, WINDOW = cfg["OWN"], cfg["WINDOW"]
    CTX, PAD, NCH, QT = d["CTX"], d["PAD"], d["NCH"], d["QT"]
    b = core // 2
    half = core % 2
    OS = half * OWN
    x = np.asarray(full["L0_attn_x"][b], dtype=np.float32)
    xc = np.zeros((CTX, H), dtype=np.float32)
    g0 = OS - PAD
    lo = max(0, -g0)
    xc[lo:] = x[g0 + lo:g0 + CTX]
    pos = np.arange(CTX, dtype=np.int64) + g0
    posc = np.where(pos < 0, 0, pos)
    kc_, ks_ = rope_tables(posc)
    qc_, qs_ = rope_tables(pos[PAD:])
    lq = PAD + np.arange(OWN)
    cm = np.empty((QT, 128, NCH), dtype=np.float32)
    wm = np.empty((QT, 128, 384), dtype=np.float32)
    delta = (PAD - OS) // 16
    for t in range(QT):
        lqt = lq[128 * t:128 * (t + 1)][:, None]
        ch = np.arange(NCH)[None, :]
        vis = (ch < (lqt + 1) // 16) & (ch >= delta)
        cm[t] = np.where(vis, 0.0, NEG)
        lk = (PAD - WINDOW + 128 * t) + np.arange(384)[None, :]
        vis = (lk <= lqt) & (lk > lqt - WINDOW) & (lk >= PAD - OS)
        wm[t] = np.where(vis, 0.0, NEG)
    return {
        "x_ctx": xc,
        "wqT": np.ascontiguousarray(np.asarray(full["L0_attn_wq"]).T),
        "wkT": np.ascontiguousarray(np.asarray(full["L0_attn_wk"]).T),
        "wvT": np.ascontiguousarray(np.asarray(full["L0_attn_wv"]).T),
        "woT": np.ascontiguousarray(np.asarray(full["L0_attn_wo"]).T),
        "ws1": np.asarray(full["L0_ffn_w_shared1"], dtype=np.float32),
        "ws2": np.asarray(full["L0_ffn_w_shared2"], dtype=np.float32),
        "w1": np.asarray(full["L0_ffn_w1"], dtype=np.float32),
        "w2": np.asarray(full["L0_ffn_w2"], dtype=np.float32),
        "wrt": np.asarray(full["L0_ffn_w_router"], dtype=np.float32),
        "kcos": kc_, "ks3": ks_, "qcos": qc_, "qs3": qs_,
        "cmask": cm, "wmask": wm,
        "identin": np.eye(128, dtype=np.float32),
        "pswapin": np.ascontiguousarray(
            np.eye(128, dtype=np.float32)[np.arange(128) ^ 1]),
    }


def kernel(**inputs):
    cfg = cfg_full()
    nc = build_program(cfg)
    in_maps = [make_core_inputs(cfg, c, inputs) for c in range(cfg["NCORES"])]
    res = run_bass_kernel_spmd(nc, in_maps, core_ids=list(range(cfg["NCORES"])))
    B, S, H = cfg["B"], cfg["S"], cfg["H"]
    out = np.empty((B, S, H), dtype=np.float32)
    for c in range(cfg["NCORES"]):
        b, half = c // 2, c % 2
        out[b, half * cfg["OWN"]:(half + 1) * cfg["OWN"]] = res.results[c]["y"]
    return out


# version 16
# speedup vs baseline: 1.0579x; 1.0440x over previous
import numpy as np
import concourse.bass as bass
import concourse.mybir as mybir
from concourse.bass_utils import run_bass_kernel_spmd

F32 = mybir.dt.float32
BF16 = mybir.dt.bfloat16
U8 = mybir.dt.uint8
ALU = mybir.AluOpType
ACT = mybir.ActivationFunctionType
AX = mybir.AxisListType

NEG = -1.0e30
EPS = 1e-5


def cfg_full():
    return dict(B=4, S=4096, H=2048, NH=16, I=1408, E=8, WINDOW=256, OWN=2048,
                NCORES=8)


def derive(c):
    d = dict(c)
    d["KC"] = c["H"] // 128
    d["PAD"] = c["OWN"]
    d["CTX"] = 2 * c["OWN"]
    d["NCH"] = d["CTX"] // 16
    d["PADC"] = d["PAD"] // 16
    d["QT"] = c["OWN"] // 128
    d["WT"] = (c["OWN"] + c["WINDOW"]) // 128
    d["ICH"] = c["I"] // 128
    d["NE"] = c["E"] + 1
    d["CT"] = d["CTX"] // 128
    d["NG"] = d["CTX"] // 512
    d["NPASS"] = c["OWN"] // 512
    d["JG"] = c["H"] // 512
    d["CVW"] = (d["NCH"] + 127) // 128
    d["SCALE"] = 1.0 / np.sqrt(128.0)
    return d


class Sem:
    def __init__(self, h):
        self.h = h
        self.n = 0


class Build:
    def __init__(self, nc):
        self.nc = nc
        self.P = {"t": [], "v": [], "s": [], "g": [], "y": []}

    def emit(self, stream, meth, *args, waits=(), inc=None, inck=1, **kw):
        waits = tuple(w for w in waits if w is not None)

        def f(eng, meth=meth, args=args, kw=kw, waits=waits, inc=inc, inck=inck):
            for h, v in waits:
                if v > 0:
                    eng.wait_ge(h, v)
            ins = getattr(eng, meth)(*args, **kw)
            if inc is not None:
                ins.then_inc(inc, inck)
        self.P[stream].append(f)

    def dma(self, stream, dst, src, sem, waits=()):
        sem.n += 16
        self.emit(stream, "dma_start", dst, src, waits=waits, inc=sem.h, inck=16)
        return sem.n

    def wait(self, stream, h, v):
        def f(eng, h=h, v=v):
            if v > 0:
                eng.wait_ge(h, v)
        self.P[stream].append(f)




def build_program(cfg, debug=False):
    d = derive(cfg)
    H, NH, I, E = cfg["H"], cfg["NH"], cfg["I"], cfg["E"]
    OWN, WINDOW = cfg["OWN"], cfg["WINDOW"]
    KC, CTX, PAD, NCH, PADC = d["KC"], d["CTX"], d["PAD"], d["NCH"], d["PADC"]
    QT, WT, ICH, NE, CT, NG = d["QT"], d["WT"], d["ICH"], d["NE"], d["CT"], d["NG"]
    NPASS, JG, CVW = d["NPASS"], d["JG"], d["CVW"]
    SCALE = d["SCALE"]
    NSQ = H // 512

    nc = bass.Bass("TRN2", target_bir_lowering=False, detect_race_conditions=False)

    def din(name, shape):
        return nc.dram_tensor(name, list(shape), F32, kind="ExternalInput").ap()

    x_ctx = din("x_ctx", (CTX, H))
    wqT = din("wqT", (H, H))
    wkT = din("wkT", (H, H))
    wvT = din("wvT", (H, H))
    woT = din("woT", (H, H))
    ws1 = din("ws1", (H, I))
    ws2 = din("ws2", (I, H))
    w1 = din("w1", (E, H, I))
    w2 = din("w2", (E, I, H))
    wrt = din("wrt", (H, E))
    kcos = din("kcos", (128, CTX))
    ks3 = din("ks3", (128, CTX))
    qcos = din("qcos", (128, OWN))
    qs3 = din("qs3", (128, OWN))
    cmask = din("cmask", (QT, 128, NCH))
    wmask = din("wmask", (QT, 128, 384))
    identin = din("identin", (128, 128))
    pswapin = din("pswapin", (128, 128))

    y = nc.dram_tensor("y", [OWN, H], F32, kind="ExternalOutput").ap()

    def dint(name, shape, dt=BF16):
        kind = "ExternalOutput" if debug else "Internal"
        return nc.dram_tensor(name, list(shape), dt, kind=kind).ap()

    lnx_h = dint("lnx_h", (CT, KC, 128, 128))
    kwin_h = dint("kwin_h", (NH, 128, OWN + WINDOW))
    vwin_h = dint("vwin_h", (NH, 128, WT * 128))
    q_hbm = dint("q_hbm", (NH, 128, OWN))
    lnx2_h = dint("lnx2_h", (QT, KC, 128, 128))
    r1_h = dint("r1_h", (OWN, H), F32)
    w1bf_h = dint("w1bf_h", (NE, H, I))
    w2bf_h = dint("w2bf_h", (NE, I, H))
    wobf_h = dint("wobf_h", (H, H))
    wrtbf_h = dint("wrtbf_h", (H, E))

    HEAP_BASE = 16512
    NB32 = 51200
    NB16 = 161628
    assert HEAP_BASE + NB32 + NB16 <= 229340
    h32_t = nc.alloc_sbuf_tensor_at("heap32", [128, NB32 // 4], F32,
                                    offset=HEAP_BASE)
    h16_t = nc.alloc_sbuf_tensor_at("heap16", [128, NB16 // 2], BF16,
                                    offset=HEAP_BASE + NB32)
    h32 = h32_t.ap()
    h16 = h16_t.ap()

    off = {F32: 0, BF16: 0}
    himax = {F32: 0, BF16: 0}

    def alloc(cols, dt):
        esz = 4 if dt == F32 else 2
        o = off[dt]
        assert o % 32 == 0
        ap = (h32 if dt == F32 else h16)[:, o // esz:o // esz + cols]
        off[dt] = o + ((cols * esz + 31) // 32) * 32
        himax[dt] = max(himax[dt], off[dt])
        assert off[F32] <= NB32 and off[BF16] <= NB16, \
            f"heap overflow f32={off[F32]} bf16={off[BF16]}"
        return ap

    g_ident = alloc(128, BF16)
    g_pswap = alloc(128, BF16)
    g_ck = alloc(NH * NCH, BF16)
    g_cv = alloc(CVW * NH * 128, BF16)
    g_comb = alloc(QT * 8, F32)
    F32_PH = ((off[F32] + 255) // 256) * 256
    O_OFF = ((off[BF16] + 255) // 256) * 256
    off[BF16] = O_OFF
    g_o = alloc(NH * OWN, BF16)
    PH_BASE = ((off[BF16] + 255) // 256) * 256

    def phase_reset(reuse_o=False):
        off[BF16] = O_OFF if reuse_o else PH_BASE
        off[F32] = F32_PH

    pb_ctx = []
    pb = []
    for i in range(8):
        cm = nc.psum_tensor(f"pbank{i}", [128, 512], F32)
        pb.append(cm.__enter__())
        pb_ctx.append(cm)
    pbf = [p.ap() for p in pb]

    bld = Build(nc)
    semctx = []

    def S(name):
        cm = nc.semaphore(name)
        h = cm.__enter__()
        semctx.append(cm)
        return Sem(h)

    dw = S("dw")
    dm2 = S("dm2")
    sag = [S(f"sag{i}") for i in range(8)]
    dl0, dl1 = S("dl0"), S("dl1")
    dy0, dy1 = S("dy0"), S("dy1")
    ds1 = S("ds1")
    ds2 = S("ds2")
    st0, st1 = S("st0"), S("st1")
    sf0, sf1 = S("sf0"), S("sf1")
    sy = S("sy")
    ten, vec, scl = S("ten"), S("vec"), S("scl")
    phz = S("phz")
    dyp = [dy0, dy1]
    dlp = [dl0, dl1]
    stp = [st0, st1]
    sfp = [sf0, sf1]

    def TEN():
        ten.n += 1
        return ten.n

    def VEC():
        vec.n += 1
        return vec.n

    def SCL():
        scl.n += 1
        return scl.n

    def barrier():
        for stnm, sem in (("t", ten), ("v", vec), ("s", scl)):
            bld.emit(stnm, "nop", waits=[(sem.h, sem.n)], inc=phz.h)
        for stnm in ("g", "y"):
            bld.emit(stnm, "nop", inc=phz.h)
        phz.n += 5
        target = phz.n
        for stnm in ("t", "v", "s", "g", "y"):
            bld.wait(stnm, phz.h, target)

    class TrPipe:

        def __init__(self, banks, evac_stream="s"):
            self.banks = banks
            self.ev = {b: 0 for b in banks}
            self.i = 0
            self.evs = evac_stream

        def go(self, src_ap, dst_ap, w0=128, rhs=None, extra_waits=(),
               evac_waits=()):
            b = self.banks[self.i % 2]
            self.i += 1
            wv = list(extra_waits)
            if self.ev[b]:
                wv.append(((scl.h if self.evs == "s" else vec.h), self.ev[b]))
            bld.emit("t", "matmul", pbf[b][0:w0, 0:128], src_ap,
                     rhs if rhs is not None else g_ident,
                     start=True, stop=True, waits=wv, inc=ten.h)
            trd = TEN()
            bld.emit(self.evs, "copy" if self.evs == "s" else "tensor_copy",
                     dst_ap, pbf[b][0:w0, 0:128],
                     waits=[(ten.h, trd)] + list(evac_waits),
                     inc=(scl.h if self.evs == "s" else vec.h))
            self.ev[b] = SCL() if self.evs == "s" else VEC()
            return self.ev[b]

    bld.dma("g", g_ident, identin, dw)
    bld.dma("g", g_pswap, pswapin, dw)
    CONST_DONE = dw.n

    phase_reset(reuse_o=True)
    w_sb = alloc(KC * H, BF16)
    lnx_g = [alloc(KC * 512, BF16) for _ in range(2)]
    mark = off[BF16]
    rc_t = alloc(CTX, BF16)
    rs_t = alloc(CTX, BF16)
    end_kq = off[BF16]
    off[BF16] = mark
    c_vft = [alloc(512, BF16) for _ in range(2)]
    c_vtk = [alloc(512, BF16) for _ in range(2)]
    c_cvft = alloc(NH * NCH, BF16)
    off[BF16] = max(off[BF16], end_kq)
    b_v1 = [alloc(512, BF16) for _ in range(2)]
    b_t1 = [alloc(512, BF16) for _ in range(2)]
    b_kr = [alloc(512, BF16) for _ in range(2)]
    b_pool = alloc(64, F32)

    def lnx_load(gp, gsl, waits):
        v3 = lnx_g[gp].rearrange("p (kc g c) -> p kc g c", g=4, c=128)
        for j in range(4):
            n = bld.dma("y", v3[:, :, j, :],
                        lnx_h[4 * gsl + j].transpose([1, 0, 2]),
                        dlp[gp], waits=(waits if j == 0 else ()))
        return n

    def qkv_pass(kind, preload=None):
        wsrc = {"k": wkT, "v": wvT, "q": wqT}[kind]
        if preload is None:
            bld.dma("g", w_sb.rearrange("p (kc c) -> p kc c", c=H),
                    wsrc.rearrange("(kc p) c -> p kc c", p=128), dw)
            if kind == "q":
                bld.dma("g", rc_t[:, 0:OWN], qcos, dw)
                bld.dma("g", rs_t[:, 0:OWN], qs3, dw)
            wld = dw.n
        else:
            wld = preload
        tscale = 0
        if kind == "q":
            bld.emit("v", "tensor_scalar", rc_t[:, 0:OWN], rc_t[:, 0:OWN],
                     float(SCALE), None, ALU.mult, waits=[(dw.h, wld)])
            bld.emit("v", "tensor_scalar", rs_t[:, 0:OWN], rs_t[:, 0:OWN],
                     float(SCALE), None, ALU.mult, inc=vec.h)
            tscale = VEC()
        glist = list(range(NG)) if kind != "q" else list(range(NG // 2, NG))
        NI = len(glist) * NH
        ld_n = [0, 0]
        ps_free = {}
        psw_free = {}
        kr_store = {}
        r1_done = {}
        r2_done = {}
        ev_done = {}
        pl_done = {}
        grp_mm = {}
        for gi in range(min(2, len(glist))):
            ld_n[gi % 2] = lnx_load(gi % 2, glist[gi],
                                    [(sag[glist[gi] % 8].h,
                                      64 * (glist[gi] // 8 + 1))]
                                    if kind == "k" else ())

        def st_proj(i):
            gi, m = divmod(i, NH)
            gp = gi % 2
            gsl = glist[gi]
            pk = i % 2
            wv = []
            if m == 0:
                wv.append((dlp[gp].h, ld_n[gp]))
                if gi == 0:
                    wv.append((dw.h, wld))
            if i - 2 in ps_free:
                wv.append(((vec.h if kind != "v" else scl.h), ps_free[i - 2]))
            for kcI in range(KC):
                bld.emit("t", "matmul", pbf[pk][:, 0:512],
                         w_sb[:, H * kcI + 128 * m: H * kcI + 128 * (m + 1)],
                         lnx_g[gp][:, 512 * kcI:512 * (kcI + 1)],
                         start=(kcI == 0), stop=(kcI == KC - 1),
                         waits=(wv if kcI == 0 else ()),
                         inc=(ten.h if kcI == KC - 1 else None))
            grp_mm[i] = TEN()
            if m == NH - 1 and gi + 2 < len(glist):
                gn = glist[gi + 2]
                ld_n[gp] = lnx_load(gp, gn,
                                    [(ten.h, grp_mm[i])] +
                                    ([(sag[gn % 8].h, 64 * (gn // 8 + 1))]
                                     if kind == "k" else []))

        def st_rope1(i):
            gi, m = divmod(i, NH)
            gsl = glist[gi]
            pk = i % 2
            tb0 = 512 * gsl if kind == "k" else 512 * (gsl - NG // 2)
            wv = [(ten.h, grp_mm[i])]
            if tscale and i == 0:
                wv.append((vec.h, tscale))
            if i - 2 in r2_done:
                wv.append((vec.h, r2_done[i - 2]))
            bld.emit("v", "tensor_tensor", b_v1[pk], pbf[pk][:, 0:512],
                     rs_t[:, tb0:tb0 + 512], ALU.mult, waits=wv)
            bld.emit("v", "tensor_tensor", b_t1[pk], pbf[pk][:, 0:512],
                     rc_t[:, tb0:tb0 + 512], ALU.mult, inc=vec.h)
            r1_done[i] = VEC()
            ps_free[i] = r1_done[i]

        def st_swap(i):
            pk = i % 2
            wv = [(vec.h, r1_done[i])]
            if i - 2 in r2_done:
                wv.append((vec.h, r2_done[i - 2]))
            bld.emit("t", "matmul", pbf[4 + pk][:, 0:512], g_pswap,
                     b_v1[pk], start=True, stop=True, waits=wv, inc=ten.h)
            psw_free[i] = TEN()

        def st_rope2(i):
            gi, m = divmod(i, NH)
            gsl = glist[gi]
            pk = i % 2
            wv = [(ten.h, psw_free[i])]
            if i - 2 in kr_store:
                wv.append((stp[pk].h, kr_store[i - 2]))
            bld.emit("v", "tensor_tensor", b_kr[pk], b_t1[pk],
                     pbf[4 + pk][:, 0:512], ALU.add, waits=wv, inc=vec.h)
            v3 = VEC()
            if kind == "k":
                bld.emit("v", "reduce_sum", b_pool[:, 0:32],
                         b_kr[pk].rearrange("p (a b) -> p a b", b=16),
                         axis=AX.X, waits=[(vec.h, vec.n)], inc=vec.h)
                VEC()
                bld.emit("v", "tensor_scalar",
                         g_ck[:, NCH * m + 32 * gsl:NCH * m + 32 * (gsl + 1)],
                         b_pool[:, 0:32], 1.0 / 16.0, None, ALU.mult,
                         waits=[(vec.h, vec.n)], inc=vec.h)
                v3 = VEC()
            r2_done[i] = v3
            if kind == "k":
                lo_tok = 512 * gsl
                lo = max(0, (PAD - WINDOW) - lo_tok)
                if lo < 512:
                    u0 = (lo_tok + lo - (PAD - WINDOW)) // 128
                    kr_store[i] = bld.dma(
                        "g", kwin_h[m][:, 128 * u0:128 * u0 + (512 - lo)],
                        b_kr[pk][:, lo:512], stp[pk], waits=[(vec.h, v3)])
                else:
                    kr_store[i] = 0
            else:
                gq = gsl - NG // 2
                kr_store[i] = bld.dma("g", q_hbm[m][:, 512 * gq:512 * (gq + 1)],
                                      b_kr[pk], stp[pk], waits=[(vec.h, v3)])

        tpipeV = TrPipe((6, 7), "v")

        def st_vev(i):
            pk = i % 2
            ew = [(ten.h, grp_mm[i])]
            if i - 2 in pl_done:
                ew.append((vec.h, pl_done[i - 2]))
                if ev_t_free.get(i - 2):
                    ew.append((ten.h, ev_t_free[i - 2]))
            bld.emit("s", "copy", c_vft[pk], pbf[pk][:, 0:512],
                     waits=ew, inc=scl.h)
            ev_done[i] = SCL()
            ps_free[i] = ev_done[i]

        ev_t_free = {}

        def st_vpool(i):
            gi, m = divmod(i, NH)
            gsl = glist[gi]
            pk = i % 2
            bld.emit("v", "reduce_sum", b_pool[:, 0:32],
                     c_vft[pk].rearrange("p (a b) -> p a b", b=16),
                     axis=AX.X, waits=[(scl.h, ev_done[i]), (vec.h, vec.n)],
                     inc=vec.h)
            VEC()
            bld.emit("v", "tensor_scalar",
                     c_cvft[:, NCH * m + 32 * gsl:NCH * m + 32 * (gsl + 1)],
                     b_pool[:, 0:32], 1.0 / 16.0, None, ALU.mult,
                     waits=[(vec.h, vec.n)], inc=vec.h)
            pl_done[i] = VEC()
            lo_tok = 512 * gsl
            lo = max(0, (PAD - WINDOW) - lo_tok)
            if lo < 512:
                evw = [(stp[pk].h, kr_store[i - 2])] if kr_store.get(i - 2) else []
                for j in range(lo // 128, 4):
                    tpipeV.go(c_vft[pk][:, 128 * j:128 * (j + 1)],
                              c_vtk[pk][:, 128 * j:128 * (j + 1)],
                              extra_waits=([(scl.h, ev_done[i])]
                                           if j == lo // 128 else ()),
                              evac_waits=(evw if j == lo // 128 else ()))
                u0 = (lo_tok + lo - (PAD - WINDOW)) // 128
                kr_store[i] = bld.dma(
                    "g", vwin_h[m][:, 128 * u0:128 * u0 + (512 - lo)],
                    c_vtk[pk][:, lo:512], stp[pk], waits=[(vec.h, vec.n)])
                ev_t_free[i] = ten.n
            else:
                kr_store[i] = 0
                ev_t_free[i] = 0

        if kind == "v":
            for step in range(NI + 2):
                if step < NI:
                    st_proj(step)
                if 1 <= step <= NI:
                    st_vev(step - 1)
                if step >= 2:
                    st_vpool(step - 2)
                yield step
        else:
            for step in range(NI + 2):
                if step < NI:
                    st_proj(step)
                if 1 <= step <= NI:
                    st_rope1(step - 1)
                    st_swap(step - 1)
                if step >= 2:
                    st_rope2(step - 2)
                yield step


    a_x = [alloc(H, F32) for _ in range(2)]
    a_stat = alloc(48, F32)
    a_ag = alloc(4, F32)
    a_eps = alloc(2, F32)
    a_istd = alloc(4, F32)
    a_nmu = alloc(2, F32)
    ATOP = ((NB16 - 16384) // 32) * 32
    assert off[BF16] <= ATOP, f"A-top overlap {off[BF16]} > {ATOP}"
    a_ln = [h16[:, (ATOP + 4096 * i) // 2:(ATOP + 4096 * i) // 2 + H]
            for i in range(2)]
    a_tr = [h16[:, (ATOP + 8192 + 4096 * i) // 2:(ATOP + 8192 + 4096 * i) // 2 + H]
            for i in range(2)]

    bld.dma("g", w_sb.rearrange("p (kc c) -> p kc c", c=H),
            wkT.rearrange("(kc p) c -> p kc c", p=128), dw)
    bld.dma("g", rc_t, kcos, dw)
    bld.dma("g", rs_t, ks3, dw)
    WLD_K = dw.n

    a_xfree = [0, 0]
    a_trdone = [0, 0]
    a_stdone = [0, 0]
    a_vln = {}
    a_vagg = {}
    a_srec = {}
    tpipeA = TrPipe((6, 7), "s")

    def a_s1(t):
        p = t % 2
        ldv = bld.dma("y", a_x[p], x_ctx[128 * t:128 * (t + 1), :], dyp[p],
                      waits=([(vec.h, a_xfree[p])] if t >= 2 else ()))
        for cchunk in range(NSQ):
            bld.emit("v", "bn_stats", a_stat[:, 12 * p + 6 * cchunk // 1:
                                             12 * p + 6 * cchunk + 6]
                     if False else a_stat[:, 24 * p + 6 * cchunk:24 * p + 6 * cchunk + 6],
                     a_x[p][:, 512 * cchunk:512 * (cchunk + 1)],
                     waits=([(dyp[p].h, ldv)] if cchunk == 0 else ()),
                     inc=(vec.h if cchunk == NSQ - 1 else None))
        VEC()
        bld.emit("v", "bn_aggr", a_ag[:, 2 * p:2 * p + 2],
                 a_stat[:, 24 * p:24 * p + 6 * NSQ],
                 waits=[(vec.h, vec.n)], inc=vec.h)
        VEC()
        bld.emit("v", "tensor_scalar", a_eps[:, p:p + 1], a_ag[:, 2 * p + 1:2 * p + 2],
                 EPS, None, ALU.add, waits=[(vec.h, vec.n)], inc=vec.h)
        a_vagg[t] = VEC()

    def a_s2(t):
        p = t % 2
        bld.emit("s", "sqrt", a_istd[:, 2 + p:3 + p], a_eps[:, p:p + 1],
                 waits=[(vec.h, a_vagg[t])], inc=scl.h)
        srec = SCL()
        bld.emit("v", "reciprocal", a_istd[:, p:p + 1], a_istd[:, 2 + p:3 + p],
                 waits=[(scl.h, srec)], inc=vec.h)
        VEC()
        bld.emit("v", "tensor_scalar", a_nmu[:, p:p + 1], a_ag[:, 2 * p:2 * p + 1],
                 a_istd[:, p:p + 1], -1.0, ALU.mult, ALU.mult,
                 waits=[(vec.h, vec.n)], inc=vec.h)
        VEC()
        bld.emit("v", "tensor_scalar", a_ln[p], a_x[p], a_istd[:, p:p + 1],
                 a_nmu[:, p:p + 1], ALU.mult, ALU.add,
                 waits=[(vec.h, vec.n)] +
                       ([(ten.h, a_trdone[p])] if t >= 2 else []), inc=vec.h)
        a_vln[t] = VEC()
        a_xfree[p] = a_vln[t]

    def a_s3(t):
        p = t % 2
        for kcI in range(KC):
            ew = [(sag[((t - 2) // 4) % 8].h, a_stdone[p])] \
                if (kcI == 0 and t >= 2) else []
            xw = [(vec.h, a_vln[t])] if kcI == 0 else []
            if t == 0 and kcI == 0:
                xw.append((dw.h, CONST_DONE))
            tpipeA.go(a_ln[p][:, 128 * kcI:128 * (kcI + 1)],
                      a_tr[p][:, 128 * kcI:128 * (kcI + 1)],
                      extra_waits=xw, evac_waits=ew)
        a_trdone[p] = ten.n
        evd = scl.n
        sg = sag[(t // 4) % 8]
        sg.n += 16
        bld.emit("g", "dma_start", lnx_h[t].transpose([1, 0, 2]),
                 a_tr[p].rearrange("p (kc c) -> p kc c", c=128),
                 waits=[(scl.h, evd)], inc=sg.h, inck=16)
        a_stdone[p] = sg.n

    kgen = qkv_pass("k", WLD_K)
    for step in range(CT + 2):
        if step < CT:
            a_s1(step)
        if 1 <= step <= CT:
            a_s2(step - 1)
        if step >= 2:
            a_s3(step - 2)
    for _ in kgen:
        pass
    for sgx in sag:
        bld.wait("g", sgx.h, sgx.n)
    bld.wait("g", st0.h, st0.n)
    bld.wait("g", st1.h, st1.n)
    barrier()
    for _ in qkv_pass("v"):
        pass
    tpipeCV = TrPipe((6, 7), "v")
    first_cv = True
    for m in range(NH):
        for cc in range(CVW):
            w0 = min(128, NCH - 128 * cc)
            xw = [(vec.h, vec.n)] if first_cv else []
            if first_cv:
                xw.append((ten.h, ten.n))
            first_cv = False
            tpipeCV.go(c_cvft[:, NCH * m + 128 * cc:NCH * m + 128 * cc + w0],
                       g_cv[0:w0,
                            (NH * 128) * cc + 128 * m:(NH * 128) * cc + 128 * (m + 1)],
                       w0=w0, extra_waits=xw)
    bld.wait("g", st0.h, st0.n)
    bld.wait("g", st1.h, st1.n)
    barrier()
    for _ in qkv_pass("q"):
        pass
    bld.wait("g", st0.h, st0.n)
    bld.wait("g", st1.h, st1.n)
    barrier()
    B_ST0, B_ST1 = st0.n, st1.n

    phase_reset()
    e_cm = alloc(QT * NCH, BF16)
    e_wm = alloc(QT * 384, BF16)
    e_k = [alloc(OWN + WINDOW, BF16) for _ in range(2)]
    e_v = [alloc(WT * 128, BF16) for _ in range(2)]
    e_q = [alloc(OWN, BF16) for _ in range(2)]
    e_pc = [alloc(NCH, BF16) for _ in range(4)]
    e_pw = [alloc(384, BF16) for _ in range(4)]
    NPT = CVW + 3
    e_pt = [alloc(NPT * 128, BF16) for _ in range(2)]
    e_D = [alloc(128, BF16) for _ in range(4)]
    e_nm = alloc(16, F32)

    bld.dma("g", e_cm.rearrange("p (t c) -> p t c", c=NCH),
            cmask.transpose([1, 0, 2]), dm2)
    mld = bld.dma("g", e_wm.rearrange("p (t c) -> p t c", c=384),
                  wmask.transpose([1, 0, 2]), dm2)
    bld.dma("g", wobf_h, woT, dw)
    PRE_WO = dw.n
    bld.dma("g", wrtbf_h, wrt, dw)
    PRE_WRT = dw.n
    for e in range(NE):
        bld.dma("g", w1bf_h[e], ws1 if e == 0 else w1[e - 1], dw)
        bld.dma("g", w2bf_h[e], ws2 if e == 0 else w2[e - 1], dw)
    PRECAST_DONE = dw.n

    kv_ld = {}
    for m in range(min(2, NH)):
        p = m % 2
        bld.dma("y", e_k[p], kwin_h[m],
                dyp[p], waits=([(stp[0].h, B_ST0), (stp[1].h, B_ST1)]
                               if m == 0 else ()))
        bld.dma("y", e_v[p], vwin_h[m], dyp[p])
        kv_ld[m] = bld.dma("y", e_q[p], q_hbm[m], dyp[p])

    N_IT = NH * QT
    sc_free = {}
    sw_free = {}
    o_free = {}
    pc_free = {}
    av_done = {}
    nmd_d = {}
    dbl_d = {}
    tr_ev = {6: 0, 7: 0}
    tr_ev_s = {6: "v", 7: "v"}
    tr_i = [0]

    def e_scores(i):
        m, t = divmod(i, QT)
        p = m % 2
        tp = i % 2
        wc = PADC + 8 * (t + 1)
        wv = [(dyp[p].h, kv_ld[m])] if t == 0 else []
        if i == 0:
            wv.append((dm2.h, mld))
        if i - 2 in sw_free:
            wv.append((scl.h, sw_free[i - 2]))
        bld.emit("t", "matmul", pbf[tp][:, 0:wc], g_ident,
                 e_cm[:, NCH * t:NCH * t + wc], start=True, stop=False,
                 waits=wv)
        bld.emit("t", "matmul", pbf[tp][:, 0:wc],
                 e_q[p][:, 128 * t:128 * (t + 1)],
                 g_ck[:, NCH * m:NCH * m + wc], start=False, stop=True,
                 inc=ten.h)
        TEN()
        bld.emit("t", "matmul", pbf[2 + tp][:, 0:384], g_ident,
                 e_wm[:, 384 * t:384 * (t + 1)], start=True, stop=False)
        bld.emit("t", "matmul", pbf[2 + tp][:, 0:384],
                 e_q[p][:, 128 * t:128 * (t + 1)],
                 e_k[p][:, 128 * t:128 * t + 384], start=False, stop=True,
                 inc=ten.h)
        sc_free[i] = st = TEN()
        nmd_d[i] = st

    def e_sm1(i):
        m, t = divmod(i, QT)
        tp = i % 2
        nb = 8 * tp
        wc = PADC + 8 * (t + 1)
        swd = sc_free[i]
        wv = [(ten.h, swd)]
        if i - 2 in sw_free:
            wv.append((scl.h, sw_free[i - 2]))
        bld.emit("v", "reduce_max", e_nm[:, nb:nb + 1], pbf[tp][:, 0:wc],
                 axis=AX.X, negate=True, waits=wv)
        bld.emit("v", "reduce_max", e_nm[:, nb + 1:nb + 2],
                 pbf[2 + tp][:, 0:384], axis=AX.X, negate=True, inc=vec.h)
        VEC()
        bld.emit("v", "tensor_tensor", e_nm[:, nb + 2:nb + 3],
                 e_nm[:, nb:nb + 1], e_nm[:, nb + 1:nb + 2], ALU.min,
                 waits=[(vec.h, vec.n)], inc=vec.h)
        nmd_d[i] = VEC()

    def e_sm2(i):
        m, t = divmod(i, QT)
        tp = i % 2
        nb = 8 * tp
        wc = PADC + 8 * (t + 1)
        p4 = i % 4
        ew = [(vec.h, nmd_d[i])]
        if i - 4 in pc_free:
            ew.append((ten.h, pc_free[i - 4]))
        bld.emit("s", "activation", e_pc[p4][:, 0:wc], pbf[tp][:, 0:wc],
                 ACT.Exp, bias=e_nm[:, nb + 2:nb + 3], scale=1.0,
                 accum_out=e_nm[:, nb + 3:nb + 4], waits=ew, inc=scl.h)
        sc_free[i] = SCL()
        bld.emit("s", "activation", e_pw[p4][:, 0:384], pbf[2 + tp][:, 0:384],
                 ACT.Exp, bias=e_nm[:, nb + 2:nb + 3], scale=1.0,
                 accum_out=e_nm[:, nb + 4:nb + 5], inc=scl.h)
        sw_free[i] = SCL()

    def e_sm3(i):
        tp = i % 2
        nb = 8 * tp
        bld.emit("v", "tensor_tensor", e_nm[:, nb + 5:nb + 6],
                 e_nm[:, nb + 3:nb + 4], e_nm[:, nb + 4:nb + 5], ALU.add,
                 waits=[(scl.h, sw_free[i])], inc=vec.h)
        VEC()
        bld.emit("v", "reciprocal", e_nm[:, nb + 6:nb + 7],
                 e_nm[:, nb + 5:nb + 6], waits=[(vec.h, vec.n)], inc=vec.h)
        VEC()
        dw2 = [(vec.h, vec.n)]
        if i - 4 in pc_free:
            dw2.append((ten.h, pc_free[i - 4]))
        bld.emit("v", "tensor_scalar", e_D[i % 4], g_ident,
                 e_nm[:, nb + 6:nb + 7], None, ALU.mult, waits=dw2,
                 inc=vec.h)
        dbl_d[i] = VEC()

    def e_tr(i):
        m, t = divmod(i, QT)
        tp = i % 2
        wc = PADC + 8 * (t + 1)
        nwc = (wc + 127) // 128
        p4 = i % 4
        for j in range(nwc + 3):
            if j < nwc:
                w0 = min(128, wc - 128 * j)
                src = e_pc[p4][:, 128 * j:128 * j + w0]
            else:
                w0 = 128
                src = e_pw[p4][:, 128 * (j - nwc):128 * (j - nwc + 1)]
            b = (6, 7)[tr_i[0] % 2]
            tr_i[0] += 1
            wv = []
            if j == 0:
                wv = [(vec.h, dbl_d[i]), (scl.h, sw_free[i])]
            if tr_ev[b]:
                wv.append(((vec.h if tr_ev_s[b] == "v" else scl.h), tr_ev[b]))
            bld.emit("t", "matmul", pbf[b][0:w0, 0:128], src, e_D[p4],
                     start=True, stop=True, waits=wv, inc=ten.h)
            trd = TEN()
            ew = [(ten.h, trd)]
            if j == 0 and i - 2 in av_done:
                ew.append((ten.h, av_done[i - 2]))
            dst = e_pt[tp][0:w0, 128 * j:128 * (j + 1)]
            if j % 2 == 0:
                bld.emit("v", "tensor_copy", dst, pbf[b][0:w0, 0:128],
                         waits=ew, inc=vec.h)
                tr_ev[b] = VEC()
                tr_ev_s[b] = "v"
            else:
                bld.emit("s", "copy", dst, pbf[b][0:w0, 0:128],
                         waits=ew, inc=scl.h)
                tr_ev[b] = SCL()
                tr_ev_s[b] = "s"
        pc_free[i] = ten.n

    def e_av(i):
        m, t = divmod(i, QT)
        p = m % 2
        tp = i % 2
        wc = PADC + 8 * (t + 1)
        nwc = (wc + 127) // 128
        wv = []
        vmax = max((tr_ev[b] for b in (6, 7) if tr_ev_s[b] == "v"), default=0)
        smax = max((tr_ev[b] for b in (6, 7) if tr_ev_s[b] == "s"), default=0)
        if vmax:
            wv.append((vec.h, vmax))
        if smax:
            wv.append((scl.h, smax))
        if i - 2 in o_free:
            wv.append((scl.h, o_free[i - 2]))
        first = True
        for j in range(nwc):
            w0 = min(128, wc - 128 * j)
            bld.emit("t", "matmul", pbf[4 + tp][:, 0:128],
                     g_cv[0:w0, (NH * 128) * j + 128 * m:(NH * 128) * j + 128 * (m + 1)],
                     e_pt[tp][0:w0, 128 * j:128 * (j + 1)],
                     start=first, stop=False, waits=(wv if first else ()))
            first = False
        for j in range(3):
            last = (j == 2)
            bld.emit("t", "matmul", pbf[4 + tp][:, 0:128],
                     e_v[p][:, 128 * (t + j):128 * (t + j + 1)],
                     e_pt[tp][:, 128 * (nwc + j):128 * (nwc + j + 1)],
                     start=False, stop=last, inc=(ten.h if last else None))
        av_done[i] = TEN()
        bld.emit("s", "copy", g_o[:, OWN * m + 128 * t:OWN * m + 128 * (t + 1)],
                 pbf[4 + tp][:, 0:128], waits=[(ten.h, av_done[i])],
                 inc=scl.h)
        o_free[i] = SCL()
        if t == QT - 1 and m + 2 < NH:
            bld.dma("y", e_k[p], kwin_h[m + 2], dyp[p],
                    waits=[(ten.h, ten.n)])
            bld.dma("y", e_v[p], vwin_h[m + 2], dyp[p])
            kv_ld[m + 2] = bld.dma("y", e_q[p], q_hbm[m + 2], dyp[p])

    for step in range(N_IT + 3):
        if step < N_IT:
            e_scores(step)
        if 1 <= step <= N_IT:
            e_sm1(step - 1)
            e_sm2(step - 1)
        if 2 <= step <= N_IT + 1:
            e_sm3(step - 2)
        if step >= 3:
            e_tr(step - 3)
            e_av(step - 3)
    barrier()

    phase_reset()
    f_wo = alloc(KC * H, BF16)
    f_xr = [alloc(H, F32) for _ in range(2)]
    f_ln2 = alloc(H, BF16)
    f_ln = [f_ln2, f_ln2]
    f_tr = [alloc(KC * 128, BF16) for _ in range(2)]
    f_wr = alloc(KC * 8, BF16)
    f_stat = alloc(6 * NSQ, F32)
    f_ag = alloc(4, F32)
    f_eps = alloc(2, F32)
    f_istd = alloc(2, F32)
    f_nmu = alloc(2, F32)
    f_lg = alloc(8, F32)
    f_m8 = alloc(8, F32)
    f_nm1 = alloc(2, F32)
    f_den = alloc(2, F32)
    f_sel = alloc(8, F32)

    wold = bld.dma("s", f_wo.rearrange("p (kc c) -> p kc c", c=H),
                   wobf_h.rearrange("(kc p) c -> p kc c", p=128), ds1,
                   waits=[(dw.h, PRE_WO)])
    wrld = bld.dma("s", f_wr.rearrange("p (kc c) -> p kc c", c=8),
                   wrtbf_h.rearrange("(kc p) c -> p kc c", p=128), ds2,
                   waits=[(dw.h, PRE_WRT)])
    x_ld = [0, 0]
    for t in range(min(2, QT)):
        x_ld[t % 2] = bld.dma("y", f_xr[t % 2],
                              x_ctx[PAD + 128 * t:PAD + 128 * (t + 1), :],
                              dyp[t % 2])
    f_psfree = {}
    f_store = [0, 0]
    f_trlast = 0
    f_r1free = [0, 0]
    rt_evd = 0
    tpipeF = TrPipe((6, 7), "s")
    for t in range(QT):
        p = t % 2
        for jg in range(JG):
            bk = jg % 2
            wv = []
            if t == 0 and jg == 0:
                wv.append((ds1.h, wold))
            if bk in f_psfree:
                wv.append((vec.h, f_psfree[bk]))
            for kcI in range(KC):
                bld.emit("t", "matmul", pbf[bk][:, 0:512],
                         g_o[:, OWN * kcI + 128 * t:OWN * kcI + 128 * (t + 1)],
                         f_wo[:, H * kcI + 512 * jg:H * kcI + 512 * (jg + 1)],
                         start=(kcI == 0), stop=(kcI == KC - 1),
                         waits=(wv if kcI == 0 else ()),
                         inc=(ten.h if kcI == KC - 1 else None))
            opd = TEN()
            wv = [(ten.h, opd), (dyp[p].h, x_ld[p])]
            if jg == 0 and t >= 2:
                wv.append((sfp[p].h, f_store[p]))
                wv.append((vec.h, f_r1free[p]))
            bld.emit("v", "tensor_tensor", f_xr[p][:, 512 * jg:512 * (jg + 1)],
                     pbf[bk][:, 0:512], f_xr[p][:, 512 * jg:512 * (jg + 1)],
                     ALU.add, waits=wv, inc=vec.h)
            f_psfree[bk] = VEC()
        r1d = vec.n
        bld.dma("y", r1_h[128 * t:128 * (t + 1), :], f_xr[p], sfp[p],
                waits=[(vec.h, r1d)])
        f_store[p] = sfp[p].n
        for cchunk in range(NSQ):
            bld.emit("v", "bn_stats", f_stat[:, 6 * cchunk:6 * cchunk + 6],
                     f_xr[p][:, 512 * cchunk:512 * (cchunk + 1)],
                     waits=([(vec.h, vec.n)] if cchunk == 0 else ()),
                     inc=(vec.h if cchunk == NSQ - 1 else None))
        VEC()
        bld.emit("v", "bn_aggr", f_ag[:, 0:2], f_stat[:, 0:6 * NSQ],
                 waits=[(vec.h, vec.n)], inc=vec.h)
        VEC()
        bld.emit("v", "tensor_scalar", f_eps[:, 0:1], f_ag[:, 1:2], EPS, None,
                 ALU.add, waits=[(vec.h, vec.n)], inc=vec.h)
        vrec = VEC()
        bld.emit("s", "activation", f_istd[:, 1:2], f_eps[:, 0:1], ACT.Ln,
                 waits=[(vec.h, vrec)], inc=scl.h)
        SCL()
        bld.emit("s", "activation", f_istd[:, 0:1], f_istd[:, 1:2], ACT.Exp,
                 scale=-0.5, waits=[(scl.h, scl.n)], inc=scl.h)
        srec = SCL()
        bld.emit("v", "tensor_scalar", f_nmu[:, 0:1], f_ag[:, 0:1],
                 f_istd[:, 0:1], -1.0, ALU.mult, ALU.mult,
                 waits=[(scl.h, srec)], inc=vec.h)
        VEC()
        bld.emit("v", "tensor_scalar", f_ln[p], f_xr[p], f_istd[:, 0:1],
                 f_nmu[:, 0:1], ALU.mult, ALU.add,
                 waits=[(vec.h, vec.n)] +
                       ([(ten.h, f_trlast)] if t >= 1 else []), inc=vec.h)
        lnd = VEC()
        f_r1free[p] = lnd
        if t + 2 < QT:
            x_ld[p] = bld.dma("y", f_xr[p],
                              x_ctx[PAD + 128 * (t + 2):PAD + 128 * (t + 3), :],
                              dyp[p],
                              waits=[(vec.h, lnd), (sfp[p].h, f_store[p])])
        for kcI in range(KC):
            ew = []
            if kcI == 0 and t >= 2:
                ew.append((sfp[p].h, f_store[p]))
            tpipeF.go(f_ln[p][:, 128 * kcI:128 * (kcI + 1)],
                      f_tr[p][:, 128 * kcI:128 * (kcI + 1)],
                      extra_waits=([(vec.h, lnd)] if kcI == 0 else ()),
                      evac_waits=ew)
        f_trlast = ten.n
        evd = scl.n
        wv = [(scl.h, evd), (ds2.h, wrld)]
        if rt_evd:
            wv.append((vec.h, rt_evd))
        for kcI in range(KC):
            bld.emit("t", "matmul", pbf[4][:, 0:8],
                     f_tr[p][:, 128 * kcI:128 * (kcI + 1)],
                     f_wr[:, 8 * kcI:8 * (kcI + 1)],
                     start=(kcI == 0), stop=(kcI == KC - 1),
                     waits=(wv if kcI == 0 else ()),
                     inc=(ten.h if kcI == KC - 1 else None))
        rtd = TEN()
        bld.emit("v", "tensor_copy", f_lg[:, 0:8], pbf[4][:, 0:8],
                 waits=[(ten.h, rtd)], inc=vec.h)
        rt_evd = VEC()
        bld.emit("v", "max", f_m8[:, 0:8], f_lg[:, 0:8],
                 waits=[(vec.h, vec.n)], inc=vec.h)
        VEC()
        bld.emit("v", "tensor_scalar", f_nm1[:, 0:1], f_m8[:, 0:1], -1.0, None,
                 ALU.mult, waits=[(vec.h, vec.n)])
        bld.emit("v", "tensor_scalar", f_sel[:, 0:8], f_lg[:, 0:8],
                 f_m8[:, 1:2], None, ALU.is_ge, inc=vec.h)
        seld = VEC()
        bld.emit("s", "activation", f_den[:, 0:1], f_m8[:, 1:2], ACT.Exp,
                 bias=f_nm1[:, 0:1], scale=1.0, waits=[(vec.h, seld)])
        bld.emit("s", "activation", f_lg[:, 0:8], f_lg[:, 0:8], ACT.Exp,
                 bias=f_nm1[:, 0:1], scale=1.0, inc=scl.h)
        expd = SCL()
        bld.emit("v", "tensor_scalar", f_den[:, 1:2], f_den[:, 0:1], 1.0, None,
                 ALU.add, waits=[(scl.h, expd)], inc=vec.h)
        VEC()
        bld.emit("v", "reciprocal", f_den[:, 0:1], f_den[:, 1:2],
                 waits=[(vec.h, vec.n)], inc=vec.h)
        VEC()
        bld.emit("v", "tensor_tensor", f_sel[:, 0:8], f_sel[:, 0:8],
                 f_lg[:, 0:8], ALU.mult, waits=[(vec.h, vec.n)], inc=vec.h)
        VEC()
        bld.emit("v", "tensor_scalar", g_comb[:, 8 * t:8 * (t + 1)],
                 f_sel[:, 0:8], f_den[:, 0:1], None, ALU.mult,
                 waits=[(vec.h, vec.n)], inc=vec.h)
        VEC()
        bld.dma("y", lnx2_h[t].transpose([1, 0, 2]),
                f_tr[p].rearrange("p (kc c) -> p kc c", c=128),
                sfp[p], waits=[(scl.h, evd), (ten.h, rtd)])
        f_store[p] = sfp[p].n
    bld.wait("y", sf0.h, sf0.n)
    bld.wait("y", sf1.h, sf1.n)
    barrier()
    F_ST0, F_ST1 = sf0.n, sf1.n

    phase_reset(reuse_o=True)
    g_w1 = alloc(KC * I, BF16)
    g_w2 = alloc(ICH * H, BF16)
    g_h = alloc(ICH * 512, BF16)
    g_acc = alloc(4 * H, F32)
    g_lnx2_b = [alloc(KC * 512, BF16) for _ in range(2)]
    g_r1 = [alloc(H, F32) for _ in range(2)]

    w1_ld = bld.dma("s", g_w1.rearrange("p (kc c) -> p kc c", c=I),
                    w1bf_h[0].rearrange("(kc p) c -> p kc c", p=128), ds1,
                    waits=[(dw.h, PRECAST_DONE)])
    w2_ld = bld.dma("s", g_w2.rearrange("p (ic c) -> p ic c", c=H),
                    w2bf_h[0].rearrange("(ic p) c -> p ic c", p=128), ds2)
    h_done = {}
    yst = [0, 0, 0, 0]
    od_g = {}
    l2_tgt = {}
    pass_hall = []

    def emit_l2(ps2):
        l2v = g_lnx2_b[ps2 % 2].rearrange("p (kc g c) -> p kc g c",
                                          g=4, c=128)
        for j in range(4):
            l2_tgt[ps2] = bld.dma(
                "s", l2v[:, :, j, :],
                lnx2_h[4 * ps2 + j].transpose([1, 0, 2]), dl0,
                waits=([(scl.h, pass_hall[ps2 - 2])]
                       if (j == 0 and ps2 >= 2) else ()))

    emit_l2(0)
    for pss in range(NPASS):
        g_lnx2 = g_lnx2_b[pss % 2]
        l2 = l2_tgt[pss]
        acc_part = {}
        acc_done = {}
        for e in range(NE):
            for ic in range(ICH):
                bk = ic % 2
                wv = []
                if ic == 0:
                    wv.append((ds1.h, w1_ld))
                    wv.append((dl0.h, l2))
                    if e >= 1:
                        wv.append((vec.h, acc_done[e - 1]))
                if bk in h_done:
                    wv.append((scl.h, h_done[bk]))
                for kcI in range(KC):
                    bld.emit("t", "matmul", pbf[bk][:, 0:512],
                             g_w1[:, I * kcI + 128 * ic:I * kcI + 128 * (ic + 1)],
                             g_lnx2[:, 512 * kcI:512 * (kcI + 1)],
                             start=(kcI == 0), stop=(kcI == KC - 1),
                             waits=(wv if kcI == 0 else ()),
                             inc=(ten.h if kcI == KC - 1 else None))
                w1d = TEN()
                bld.emit("s", "activation", g_h[:, 512 * ic:512 * (ic + 1)],
                         pbf[bk][:, 0:512], ACT.Silu, waits=[(ten.h, w1d)],
                         inc=scl.h)
                h_done[bk] = SCL()
            hall = scl.n
            if not (pss == NPASS - 1 and e == NE - 1):
                w1_ld = bld.dma("s", g_w1.rearrange("p (kc c) -> p kc c", c=I),
                                w1bf_h[(e + 1) % NE].rearrange("(kc p) c -> p kc c", p=128),
                                ds1)
            for tt in range(4):
                gt = 4 * pss + tt
                for jg in range(JG):
                    bk = 2 + ((tt * JG + jg) % 2)
                    wv = [(scl.h, hall)]
                    if tt == 0 and jg == 0:
                        wv.append((ds2.h, w2_ld))
                    if bk in acc_part:
                        wv.append((vec.h, acc_part[bk]))
                    for ic in range(ICH):
                        bld.emit("t", "matmul", pbf[bk][:, 0:512],
                                 g_h[:, 512 * ic + 128 * tt:512 * ic + 128 * (tt + 1)],
                                 g_w2[:, H * ic + 512 * jg:H * ic + 512 * (jg + 1)],
                                 start=(ic == 0), stop=(ic == ICH - 1),
                                 waits=(wv if ic == 0 else ()),
                                 inc=(ten.h if ic == ICH - 1 else None))
                    w2d = TEN()
                    accsl = g_acc[:, H * tt + 512 * jg:H * tt + 512 * (jg + 1)]
                    if e == 0:
                        aw = [(ten.h, w2d)]
                        if pss > 0 and jg == 0:
                            aw.append((sy.h, yst[tt]))
                        bld.emit("v", "tensor_copy", accsl, pbf[bk][:, 0:512],
                                 waits=aw, inc=vec.h)
                    else:
                        bld.emit("v", "scalar_tensor_tensor", accsl,
                                 pbf[bk][:, 0:512],
                                 g_comb[:, 8 * gt + (e - 1):8 * gt + e], accsl,
                                 ALU.mult, ALU.add, waits=[(ten.h, w2d)],
                                 inc=vec.h)
                    acc_part[bk] = VEC()
                if e == NE - 1:
                    p2 = tt % 2
                    r1w = [(scl.h, hall)]
                    if od_g.get(p2):
                        r1w.append((vec.h, od_g[p2]))
                    r1v = bld.dma("y", g_r1[p2],
                                  r1_h[512 * pss + 128 * tt:
                                       512 * pss + 128 * (tt + 1), :],
                                  dyp[p2], waits=r1w)
                    ow = [(dyp[p2].h, r1v),
                          (vec.h, acc_part[2 + (tt * JG + JG - 1) % 2])]
                    bld.emit("v", "tensor_tensor", g_acc[:, H * tt:H * (tt + 1)],
                             g_acc[:, H * tt:H * (tt + 1)], g_r1[p2], ALU.add,
                             waits=ow, inc=vec.h)
                    od = VEC()
                    od_g[p2] = od
                    yst[tt] = bld.dma("g",
                                      y[512 * pss + 128 * tt:512 * pss + 128 * (tt + 1), :],
                                      g_acc[:, H * tt:H * (tt + 1)], sy,
                                      waits=[(vec.h, od)])
            acc_done[e] = vec.n
            if e == NE - 1:
                pass_hall.append(hall)
            if not (pss == NPASS - 1 and e == NE - 1):
                w2_ld = bld.dma("s", g_w2.rearrange("p (ic c) -> p ic c", c=H),
                                w2bf_h[(e + 1) % NE].rearrange("(ic p) c -> p ic c", p=128),
                                ds2, waits=[(ten.h, ten.n)])
            if e == 0 and pss + 1 < NPASS:
                emit_l2(pss + 1)
    bld.wait("g", sy.h, sy.n)
    bld.wait("y", sy.h, sy.n)

    import contextlib
    with contextlib.ExitStack() as cx:
        block = cx.enter_context(nc.Block())
        P = bld.P

        @block.tensor
        def _(t):
            for f in P["t"]:
                f(t)

        @block.vector
        def _(v):
            for f in P["v"]:
                f(v)

        @block.scalar
        def _(s):
            for f in P["s"]:
                f(s)

        @block.gpsimd
        def _(g):
            for f in P["g"]:
                f(g)

        @block.sync
        def _(yq):
            for f in P["y"]:
                f(yq)

    for cm in reversed(semctx):
        cm.__exit__(None, None, None)
    for cm in reversed(pb_ctx):
        cm.__exit__(None, None, None)
    return nc


def rope_tables(pos):
    inv = 1.0 / (10000.0 ** (np.arange(0, 128, 2, dtype=np.float32) / 128))
    inv = inv.astype(np.float32)
    f = (pos.astype(np.float32)[:, None] * inv[None, :]).astype(np.float32)
    fc = np.exp(1j * f).astype(np.complex64)
    c = fc.real.T
    sn = fc.imag.T
    C = np.repeat(c, 2, axis=0)
    S3 = np.empty((128, len(pos)), dtype=np.float32)
    S3[0::2] = sn
    S3[1::2] = -sn
    return np.ascontiguousarray(C), np.ascontiguousarray(S3)


def make_core_inputs(cfg, core, full):
    d = derive(cfg)
    H = cfg["H"]
    OWN, WINDOW = cfg["OWN"], cfg["WINDOW"]
    CTX, PAD, NCH, QT = d["CTX"], d["PAD"], d["NCH"], d["QT"]
    b = core // 2
    half = core % 2
    OS = half * OWN
    x = np.asarray(full["L0_attn_x"][b], dtype=np.float32)
    xc = np.zeros((CTX, H), dtype=np.float32)
    g0 = OS - PAD
    lo = max(0, -g0)
    xc[lo:] = x[g0 + lo:g0 + CTX]
    pos = np.arange(CTX, dtype=np.int64) + g0
    posc = np.where(pos < 0, 0, pos)
    kc_, ks_ = rope_tables(posc)
    qc_, qs_ = rope_tables(pos[PAD:])
    lq = PAD + np.arange(OWN)
    cm = np.empty((QT, 128, NCH), dtype=np.float32)
    wm = np.empty((QT, 128, 384), dtype=np.float32)
    delta = (PAD - OS) // 16
    for t in range(QT):
        lqt = lq[128 * t:128 * (t + 1)][:, None]
        ch = np.arange(NCH)[None, :]
        vis = (ch < (lqt + 1) // 16) & (ch >= delta)
        cm[t] = np.where(vis, 0.0, NEG)
        lk = (PAD - WINDOW + 128 * t) + np.arange(384)[None, :]
        vis = (lk <= lqt) & (lk > lqt - WINDOW) & (lk >= PAD - OS)
        wm[t] = np.where(vis, 0.0, NEG)
    return {
        "x_ctx": xc,
        "wqT": np.ascontiguousarray(np.asarray(full["L0_attn_wq"]).T),
        "wkT": np.ascontiguousarray(np.asarray(full["L0_attn_wk"]).T),
        "wvT": np.ascontiguousarray(np.asarray(full["L0_attn_wv"]).T),
        "woT": np.ascontiguousarray(np.asarray(full["L0_attn_wo"]).T),
        "ws1": np.asarray(full["L0_ffn_w_shared1"], dtype=np.float32),
        "ws2": np.asarray(full["L0_ffn_w_shared2"], dtype=np.float32),
        "w1": np.asarray(full["L0_ffn_w1"], dtype=np.float32),
        "w2": np.asarray(full["L0_ffn_w2"], dtype=np.float32),
        "wrt": np.asarray(full["L0_ffn_w_router"], dtype=np.float32),
        "kcos": kc_, "ks3": ks_, "qcos": qc_, "qs3": qs_,
        "cmask": cm, "wmask": wm,
        "identin": np.eye(128, dtype=np.float32),
        "pswapin": np.ascontiguousarray(
            np.eye(128, dtype=np.float32)[np.arange(128) ^ 1]),
    }


def kernel(**inputs):
    cfg = cfg_full()
    nc = build_program(cfg)
    in_maps = [make_core_inputs(cfg, c, inputs) for c in range(cfg["NCORES"])]
    res = run_bass_kernel_spmd(nc, in_maps, core_ids=list(range(cfg["NCORES"])))
    B, S, H = cfg["B"], cfg["S"], cfg["H"]
    out = np.empty((B, S, H), dtype=np.float32)
    for c in range(cfg["NCORES"]):
        b, half = c // 2, c % 2
        out[b, half * cfg["OWN"]:(half + 1) * cfg["OWN"]] = res.results[c]["y"]
    return out


# version 18
# speedup vs baseline: 1.0881x; 1.0285x over previous
import numpy as np
import concourse.bass as bass
import concourse.mybir as mybir
from concourse.bass_utils import run_bass_kernel_spmd

F32 = mybir.dt.float32
BF16 = mybir.dt.bfloat16
U8 = mybir.dt.uint8
ALU = mybir.AluOpType
ACT = mybir.ActivationFunctionType
AX = mybir.AxisListType

NEG = -1.0e30
EPS = 1e-5


def cfg_full():
    return dict(B=4, S=4096, H=2048, NH=16, I=1408, E=8, WINDOW=256, OWN=2048,
                NCORES=8)


def derive(c):
    d = dict(c)
    d["KC"] = c["H"] // 128
    d["PAD"] = c["OWN"]
    d["CTX"] = 2 * c["OWN"]
    d["NCH"] = d["CTX"] // 16
    d["PADC"] = d["PAD"] // 16
    d["QT"] = c["OWN"] // 128
    d["WT"] = (c["OWN"] + c["WINDOW"]) // 128
    d["ICH"] = c["I"] // 128
    d["NE"] = c["E"] + 1
    d["CT"] = d["CTX"] // 128
    d["NG"] = d["CTX"] // 512
    d["NPASS"] = c["OWN"] // 512
    d["JG"] = c["H"] // 512
    d["CVW"] = (d["NCH"] + 127) // 128
    d["SCALE"] = 1.0 / np.sqrt(128.0)
    return d


class Sem:
    def __init__(self, h):
        self.h = h
        self.n = 0


class Build:
    def __init__(self, nc):
        self.nc = nc
        self.P = {"t": [], "v": [], "s": [], "g": [], "y": []}

    def emit(self, stream, meth, *args, waits=(), inc=None, inck=1, **kw):
        waits = tuple(w for w in waits if w is not None)

        def f(eng, meth=meth, args=args, kw=kw, waits=waits, inc=inc, inck=inck):
            for h, v in waits:
                if v > 0:
                    eng.wait_ge(h, v)
            ins = getattr(eng, meth)(*args, **kw)
            if inc is not None:
                ins.then_inc(inc, inck)
        self.P[stream].append(f)

    def dma(self, stream, dst, src, sem, waits=()):
        sem.n += 16
        self.emit(stream, "dma_start", dst, src, waits=waits, inc=sem.h, inck=16)
        return sem.n

    def wait(self, stream, h, v):
        def f(eng, h=h, v=v):
            if v > 0:
                eng.wait_ge(h, v)
        self.P[stream].append(f)




def build_program(cfg, debug=False):
    d = derive(cfg)
    H, NH, I, E = cfg["H"], cfg["NH"], cfg["I"], cfg["E"]
    OWN, WINDOW = cfg["OWN"], cfg["WINDOW"]
    KC, CTX, PAD, NCH, PADC = d["KC"], d["CTX"], d["PAD"], d["NCH"], d["PADC"]
    QT, WT, ICH, NE, CT, NG = d["QT"], d["WT"], d["ICH"], d["NE"], d["CT"], d["NG"]
    NPASS, JG, CVW = d["NPASS"], d["JG"], d["CVW"]
    SCALE = d["SCALE"]
    NSQ = H // 512

    nc = bass.Bass("TRN2", target_bir_lowering=False, detect_race_conditions=False)

    def din(name, shape):
        return nc.dram_tensor(name, list(shape), F32, kind="ExternalInput").ap()

    x_ctx = din("x_ctx", (CTX, H))
    wqT = din("wqT", (H, H))
    wkT = din("wkT", (H, H))
    wvT = din("wvT", (H, H))
    woT = din("woT", (H, H))
    ws1 = din("ws1", (H, I))
    ws2 = din("ws2", (I, H))
    w1 = din("w1", (E, H, I))
    w2 = din("w2", (E, I, H))
    wrt = din("wrt", (H, E))
    kcos = din("kcos", (128, CTX))
    ks3 = din("ks3", (128, CTX))
    qcos = din("qcos", (128, OWN))
    qs3 = din("qs3", (128, OWN))
    cmask = din("cmask", (QT, 128, NCH))
    wmask = din("wmask", (QT, 128, 384))
    identin = din("identin", (128, 128))
    pswapin = din("pswapin", (128, 128))

    y = nc.dram_tensor("y", [OWN, H], F32, kind="ExternalOutput").ap()

    def dint(name, shape, dt=BF16):
        kind = "ExternalOutput" if debug else "Internal"
        return nc.dram_tensor(name, list(shape), dt, kind=kind).ap()

    lnx_h = dint("lnx_h", (CT, KC, 128, 128))
    kwin_h = dint("kwin_h", (NH, 128, OWN + WINDOW))
    vwin_h = dint("vwin_h", (NH, 128, WT * 128))
    q_hbm = dint("q_hbm", (NH, 128, OWN))
    lnx2_h = dint("lnx2_h", (QT, KC, 128, 128))
    r1_h = dint("r1_h", (OWN, H), F32)
    w1bf_h = dint("w1bf_h", (NE, H, I))
    w2bf_h = dint("w2bf_h", (NE, I, H))
    wobf_h = dint("wobf_h", (H, H))
    wrtbf_h = dint("wrtbf_h", (H, E))

    HEAP_BASE = 16512
    NB32 = 51200
    NB16 = 161628
    assert HEAP_BASE + NB32 + NB16 <= 229340
    h32_t = nc.alloc_sbuf_tensor_at("heap32", [128, NB32 // 4], F32,
                                    offset=HEAP_BASE)
    h16_t = nc.alloc_sbuf_tensor_at("heap16", [128, NB16 // 2], BF16,
                                    offset=HEAP_BASE + NB32)
    h32 = h32_t.ap()
    h16 = h16_t.ap()

    off = {F32: 0, BF16: 0}
    himax = {F32: 0, BF16: 0}

    def alloc(cols, dt):
        esz = 4 if dt == F32 else 2
        o = off[dt]
        assert o % 32 == 0
        ap = (h32 if dt == F32 else h16)[:, o // esz:o // esz + cols]
        off[dt] = o + ((cols * esz + 31) // 32) * 32
        himax[dt] = max(himax[dt], off[dt])
        assert off[F32] <= NB32 and off[BF16] <= NB16, \
            f"heap overflow f32={off[F32]} bf16={off[BF16]}"
        return ap

    g_ident = alloc(128, BF16)
    g_pswap = alloc(128, BF16)
    g_ck = alloc(NH * NCH, BF16)
    g_cv = alloc(CVW * NH * 128, BF16)
    g_comb = alloc(QT * 8, F32)
    F32_PH = ((off[F32] + 255) // 256) * 256
    O_OFF = ((off[BF16] + 255) // 256) * 256
    off[BF16] = O_OFF
    g_o = alloc(NH * OWN, BF16)
    PH_BASE = ((off[BF16] + 255) // 256) * 256

    def phase_reset(reuse_o=False):
        off[BF16] = O_OFF if reuse_o else PH_BASE
        off[F32] = F32_PH

    pb_ctx = []
    pb = []
    for i in range(8):
        cm = nc.psum_tensor(f"pbank{i}", [128, 512], F32)
        pb.append(cm.__enter__())
        pb_ctx.append(cm)
    pbf = [p.ap() for p in pb]

    bld = Build(nc)
    semctx = []

    def S(name):
        cm = nc.semaphore(name)
        h = cm.__enter__()
        semctx.append(cm)
        return Sem(h)

    dw = S("dw")
    dm2 = S("dm2")
    sag = [S(f"sag{i}") for i in range(8)]
    dl0, dl1 = S("dl0"), S("dl1")
    dy0, dy1 = S("dy0"), S("dy1")
    ds1 = S("ds1")
    ds2 = S("ds2")
    st0, st1 = S("st0"), S("st1")
    sf0, sf1 = S("sf0"), S("sf1")
    sy = S("sy")
    ten, vec, scl = S("ten"), S("vec"), S("scl")
    phz = S("phz")
    dyp = [dy0, dy1]
    dlp = [dl0, dl1]
    stp = [st0, st1]
    sfp = [sf0, sf1]

    def TEN():
        ten.n += 1
        return ten.n

    def VEC():
        vec.n += 1
        return vec.n

    def SCL():
        scl.n += 1
        return scl.n

    def barrier():
        for stnm, sem in (("t", ten), ("v", vec), ("s", scl)):
            bld.emit(stnm, "nop", waits=[(sem.h, sem.n)], inc=phz.h)
        for stnm in ("g", "y"):
            bld.emit(stnm, "nop", inc=phz.h)
        phz.n += 5
        target = phz.n
        for stnm in ("t", "v", "s", "g", "y"):
            bld.wait(stnm, phz.h, target)

    class TrPipe:

        def __init__(self, banks, evac_stream="s"):
            self.banks = banks
            self.ev = {b: 0 for b in banks}
            self.i = 0
            self.evs = evac_stream

        def go(self, src_ap, dst_ap, w0=128, rhs=None, extra_waits=(),
               evac_waits=()):
            b = self.banks[self.i % 2]
            self.i += 1
            wv = list(extra_waits)
            if self.ev[b]:
                wv.append(((scl.h if self.evs == "s" else vec.h), self.ev[b]))
            bld.emit("t", "matmul", pbf[b][0:w0, 0:128], src_ap,
                     rhs if rhs is not None else g_ident,
                     start=True, stop=True, waits=wv, inc=ten.h)
            trd = TEN()
            bld.emit(self.evs, "copy" if self.evs == "s" else "tensor_copy",
                     dst_ap, pbf[b][0:w0, 0:128],
                     waits=[(ten.h, trd)] + list(evac_waits),
                     inc=(scl.h if self.evs == "s" else vec.h))
            self.ev[b] = SCL() if self.evs == "s" else VEC()
            return self.ev[b]

    bld.dma("g", g_ident, identin, dw)
    bld.dma("g", g_pswap, pswapin, dw)
    CONST_DONE = dw.n

    phase_reset(reuse_o=True)
    w_sb = alloc(KC * H, BF16)
    lnx_g = [alloc(KC * 512, BF16) for _ in range(2)]
    mark = off[BF16]
    rc_t = alloc(CTX, BF16)
    rs_t = alloc(CTX, BF16)
    end_kq = off[BF16]
    off[BF16] = mark
    c_vft = [alloc(512, BF16) for _ in range(2)]
    c_vtk = [alloc(512, BF16) for _ in range(2)]
    c_cvft = alloc(NH * NCH, BF16)
    off[BF16] = max(off[BF16], end_kq)
    b_v1 = [alloc(512, BF16) for _ in range(2)]
    b_t1 = [alloc(512, BF16) for _ in range(2)]
    b_kr = [alloc(512, BF16) for _ in range(2)]
    b_pool = alloc(64, F32)

    def lnx_load(gp, gsl, waits):
        v3 = lnx_g[gp].rearrange("p (kc g c) -> p kc g c", g=4, c=128)
        for j in range(4):
            n = bld.dma("y", v3[:, :, j, :],
                        lnx_h[4 * gsl + j].transpose([1, 0, 2]),
                        dlp[gp], waits=(waits if j == 0 else ()))
        return n

    def qkv_pass(kind, preload=None):
        wsrc = {"k": wkT, "v": wvT, "q": wqT}[kind]
        if preload is None:
            bld.dma("g", w_sb.rearrange("p (kc c) -> p kc c", c=H),
                    wsrc.rearrange("(kc p) c -> p kc c", p=128), dw)
            if kind == "q":
                bld.dma("g", rc_t[:, 0:OWN], qcos, dw)
                bld.dma("g", rs_t[:, 0:OWN], qs3, dw)
            wld = dw.n
        else:
            wld = preload
        tscale = 0
        if kind == "q":
            bld.emit("v", "tensor_scalar", rc_t[:, 0:OWN], rc_t[:, 0:OWN],
                     float(SCALE), None, ALU.mult, waits=[(dw.h, wld)])
            bld.emit("v", "tensor_scalar", rs_t[:, 0:OWN], rs_t[:, 0:OWN],
                     float(SCALE), None, ALU.mult, inc=vec.h)
            tscale = VEC()
        glist = list(range(NG)) if kind != "q" else list(range(NG // 2, NG))
        NI = len(glist) * NH
        ld_n = [0, 0]
        ps_free = {}
        psw_free = {}
        kr_store = {}
        r1_done = {}
        r2_done = {}
        ev_done = {}
        pl_done = {}
        grp_mm = {}
        for gi in range(min(2, len(glist))):
            ld_n[gi % 2] = lnx_load(gi % 2, glist[gi],
                                    [(sag[glist[gi] % 8].h,
                                      64 * (glist[gi] // 8 + 1))]
                                    if kind == "k" else ())

        def st_proj(i):
            gi, m = divmod(i, NH)
            gp = gi % 2
            gsl = glist[gi]
            pk = i % 2
            wv = []
            if m == 0:
                wv.append((dlp[gp].h, ld_n[gp]))
                if gi == 0:
                    wv.append((dw.h, wld))
            if i - 2 in ps_free:
                wv.append(((vec.h if kind != "v" else scl.h), ps_free[i - 2]))
            for kcI in range(KC):
                bld.emit("t", "matmul", pbf[pk][:, 0:512],
                         w_sb[:, H * kcI + 128 * m: H * kcI + 128 * (m + 1)],
                         lnx_g[gp][:, 512 * kcI:512 * (kcI + 1)],
                         start=(kcI == 0), stop=(kcI == KC - 1),
                         waits=(wv if kcI == 0 else ()),
                         inc=(ten.h if kcI == KC - 1 else None))
            grp_mm[i] = TEN()
            if m == NH - 1 and gi + 2 < len(glist):
                gn = glist[gi + 2]
                ld_n[gp] = lnx_load(gp, gn,
                                    [(ten.h, grp_mm[i])] +
                                    ([(sag[gn % 8].h, 64 * (gn // 8 + 1))]
                                     if kind == "k" else []))

        def st_rope1(i):
            gi, m = divmod(i, NH)
            gsl = glist[gi]
            pk = i % 2
            tb0 = 512 * gsl if kind == "k" else 512 * (gsl - NG // 2)
            wv = [(ten.h, grp_mm[i])]
            if tscale and i == 0:
                wv.append((vec.h, tscale))
            if i - 2 in r2_done:
                wv.append((vec.h, r2_done[i - 2]))
            bld.emit("v", "tensor_tensor", b_v1[pk], pbf[pk][:, 0:512],
                     rs_t[:, tb0:tb0 + 512], ALU.mult, waits=wv)
            bld.emit("v", "tensor_tensor", b_t1[pk], pbf[pk][:, 0:512],
                     rc_t[:, tb0:tb0 + 512], ALU.mult, inc=vec.h)
            r1_done[i] = VEC()
            ps_free[i] = r1_done[i]

        def st_swap(i):
            pk = i % 2
            wv = [(vec.h, r1_done[i])]
            if i - 2 in r2_done:
                wv.append((vec.h, r2_done[i - 2]))
            bld.emit("t", "matmul", pbf[4 + pk][:, 0:512], g_pswap,
                     b_v1[pk], start=True, stop=True, waits=wv, inc=ten.h)
            psw_free[i] = TEN()

        def st_rope2(i):
            gi, m = divmod(i, NH)
            gsl = glist[gi]
            pk = i % 2
            wv = [(ten.h, psw_free[i])]
            if i - 2 in kr_store:
                wv.append((stp[pk].h, kr_store[i - 2]))
            bld.emit("v", "tensor_tensor", b_kr[pk], b_t1[pk],
                     pbf[4 + pk][:, 0:512], ALU.add, waits=wv, inc=vec.h)
            v3 = VEC()
            if kind == "k":
                bld.emit("v", "reduce_sum", b_pool[:, 0:32],
                         b_kr[pk].rearrange("p (a b) -> p a b", b=16),
                         axis=AX.X, waits=[(vec.h, vec.n)], inc=vec.h)
                VEC()
                bld.emit("v", "tensor_scalar",
                         g_ck[:, NCH * m + 32 * gsl:NCH * m + 32 * (gsl + 1)],
                         b_pool[:, 0:32], 1.0 / 16.0, None, ALU.mult,
                         waits=[(vec.h, vec.n)], inc=vec.h)
                v3 = VEC()
            r2_done[i] = v3
            if kind == "k":
                lo_tok = 512 * gsl
                lo = max(0, (PAD - WINDOW) - lo_tok)
                if lo < 512:
                    u0 = (lo_tok + lo - (PAD - WINDOW)) // 128
                    kr_store[i] = bld.dma(
                        "g", kwin_h[m][:, 128 * u0:128 * u0 + (512 - lo)],
                        b_kr[pk][:, lo:512], stp[pk], waits=[(vec.h, v3)])
                else:
                    kr_store[i] = 0
            else:
                gq = gsl - NG // 2
                kr_store[i] = bld.dma("g", q_hbm[m][:, 512 * gq:512 * (gq + 1)],
                                      b_kr[pk], stp[pk], waits=[(vec.h, v3)])

        tpipeV = TrPipe((6, 7), "v")

        def st_vev(i):
            pk = i % 2
            ew = [(ten.h, grp_mm[i])]
            if i - 2 in pl_done:
                ew.append((vec.h, pl_done[i - 2]))
                if ev_t_free.get(i - 2):
                    ew.append((ten.h, ev_t_free[i - 2]))
            bld.emit("s", "copy", c_vft[pk], pbf[pk][:, 0:512],
                     waits=ew, inc=scl.h)
            ev_done[i] = SCL()
            ps_free[i] = ev_done[i]

        ev_t_free = {}

        def st_vpool(i):
            gi, m = divmod(i, NH)
            gsl = glist[gi]
            pk = i % 2
            bld.emit("v", "reduce_sum", b_pool[:, 0:32],
                     c_vft[pk].rearrange("p (a b) -> p a b", b=16),
                     axis=AX.X, waits=[(scl.h, ev_done[i]), (vec.h, vec.n)],
                     inc=vec.h)
            VEC()
            bld.emit("v", "tensor_scalar",
                     c_cvft[:, NCH * m + 32 * gsl:NCH * m + 32 * (gsl + 1)],
                     b_pool[:, 0:32], 1.0 / 16.0, None, ALU.mult,
                     waits=[(vec.h, vec.n)], inc=vec.h)
            pl_done[i] = VEC()
            lo_tok = 512 * gsl
            lo = max(0, (PAD - WINDOW) - lo_tok)
            if lo < 512:
                evw = [(stp[pk].h, kr_store[i - 2])] if kr_store.get(i - 2) else []
                for j in range(lo // 128, 4):
                    tpipeV.go(c_vft[pk][:, 128 * j:128 * (j + 1)],
                              c_vtk[pk][:, 128 * j:128 * (j + 1)],
                              extra_waits=([(scl.h, ev_done[i])]
                                           if j == lo // 128 else ()),
                              evac_waits=(evw if j == lo // 128 else ()))
                u0 = (lo_tok + lo - (PAD - WINDOW)) // 128
                kr_store[i] = bld.dma(
                    "g", vwin_h[m][:, 128 * u0:128 * u0 + (512 - lo)],
                    c_vtk[pk][:, lo:512], stp[pk], waits=[(vec.h, vec.n)])
                ev_t_free[i] = ten.n
            else:
                kr_store[i] = 0
                ev_t_free[i] = 0

        if kind == "v":
            for step in range(NI + 2):
                if step < NI:
                    st_proj(step)
                if 1 <= step <= NI:
                    st_vev(step - 1)
                if step >= 2:
                    st_vpool(step - 2)
                yield step
        else:
            for step in range(NI + 2):
                if step < NI:
                    st_proj(step)
                if 1 <= step <= NI:
                    st_rope1(step - 1)
                    st_swap(step - 1)
                if step >= 2:
                    st_rope2(step - 2)
                yield step


    a_x = [alloc(H, F32) for _ in range(2)]
    a_stat = alloc(48, F32)
    a_ag = alloc(4, F32)
    a_eps = alloc(2, F32)
    a_istd = alloc(4, F32)
    a_nmu = alloc(2, F32)
    ATOP = ((NB16 - 16384) // 32) * 32
    assert off[BF16] <= ATOP, f"A-top overlap {off[BF16]} > {ATOP}"
    a_ln = [h16[:, (ATOP + 4096 * i) // 2:(ATOP + 4096 * i) // 2 + H]
            for i in range(2)]
    a_tr = [h16[:, (ATOP + 8192 + 4096 * i) // 2:(ATOP + 8192 + 4096 * i) // 2 + H]
            for i in range(2)]

    bld.dma("g", w_sb.rearrange("p (kc c) -> p kc c", c=H),
            wkT.rearrange("(kc p) c -> p kc c", p=128), dw)
    bld.dma("g", rc_t, kcos, dw)
    bld.dma("g", rs_t, ks3, dw)
    WLD_K = dw.n

    a_xfree = [0, 0]
    a_trdone = [0, 0]
    a_stdone = [0, 0]
    a_vln = {}
    a_vagg = {}
    a_srec = {}
    tpipeA = TrPipe((6, 7), "s")

    def a_s1(t):
        p = t % 2
        ldv = bld.dma("y", a_x[p], x_ctx[128 * t:128 * (t + 1), :], dyp[p],
                      waits=([(vec.h, a_xfree[p])] if t >= 2 else ()))
        for cchunk in range(NSQ):
            bld.emit("v", "bn_stats", a_stat[:, 12 * p + 6 * cchunk // 1:
                                             12 * p + 6 * cchunk + 6]
                     if False else a_stat[:, 24 * p + 6 * cchunk:24 * p + 6 * cchunk + 6],
                     a_x[p][:, 512 * cchunk:512 * (cchunk + 1)],
                     waits=([(dyp[p].h, ldv)] if cchunk == 0 else ()),
                     inc=(vec.h if cchunk == NSQ - 1 else None))
        VEC()
        bld.emit("v", "bn_aggr", a_ag[:, 2 * p:2 * p + 2],
                 a_stat[:, 24 * p:24 * p + 6 * NSQ],
                 waits=[(vec.h, vec.n)], inc=vec.h)
        VEC()
        bld.emit("v", "tensor_scalar", a_eps[:, p:p + 1], a_ag[:, 2 * p + 1:2 * p + 2],
                 EPS, None, ALU.add, waits=[(vec.h, vec.n)], inc=vec.h)
        a_vagg[t] = VEC()

    def a_s2(t):
        p = t % 2
        bld.emit("s", "sqrt", a_istd[:, 2 + p:3 + p], a_eps[:, p:p + 1],
                 waits=[(vec.h, a_vagg[t])], inc=scl.h)
        srec = SCL()
        bld.emit("v", "reciprocal", a_istd[:, p:p + 1], a_istd[:, 2 + p:3 + p],
                 waits=[(scl.h, srec)], inc=vec.h)
        VEC()
        bld.emit("v", "tensor_scalar", a_nmu[:, p:p + 1], a_ag[:, 2 * p:2 * p + 1],
                 a_istd[:, p:p + 1], -1.0, ALU.mult, ALU.mult,
                 waits=[(vec.h, vec.n)], inc=vec.h)
        VEC()
        bld.emit("v", "tensor_scalar", a_ln[p], a_x[p], a_istd[:, p:p + 1],
                 a_nmu[:, p:p + 1], ALU.mult, ALU.add,
                 waits=[(vec.h, vec.n)] +
                       ([(ten.h, a_trdone[p])] if t >= 2 else []), inc=vec.h)
        a_vln[t] = VEC()
        a_xfree[p] = a_vln[t]

    def a_s3(t):
        p = t % 2
        for kcI in range(KC):
            ew = [(sag[((t - 2) // 4) % 8].h, a_stdone[p])] \
                if (kcI == 0 and t >= 2) else []
            xw = [(vec.h, a_vln[t])] if kcI == 0 else []
            if t == 0 and kcI == 0:
                xw.append((dw.h, CONST_DONE))
            tpipeA.go(a_ln[p][:, 128 * kcI:128 * (kcI + 1)],
                      a_tr[p][:, 128 * kcI:128 * (kcI + 1)],
                      extra_waits=xw, evac_waits=ew)
        a_trdone[p] = ten.n
        evd = scl.n
        sg = sag[(t // 4) % 8]
        sg.n += 16
        bld.emit("g", "dma_start", lnx_h[t].transpose([1, 0, 2]),
                 a_tr[p].rearrange("p (kc c) -> p kc c", c=128),
                 waits=[(scl.h, evd)], inc=sg.h, inck=16)
        a_stdone[p] = sg.n

    kgen = qkv_pass("k", WLD_K)
    for step in range(CT + 2):
        if step < CT:
            a_s1(step)
        if 1 <= step <= CT:
            a_s2(step - 1)
        if step >= 2:
            a_s3(step - 2)
    for _ in kgen:
        pass
    for sgx in sag:
        bld.wait("g", sgx.h, sgx.n)
    bld.wait("g", st0.h, st0.n)
    bld.wait("g", st1.h, st1.n)
    barrier()
    for _ in qkv_pass("v"):
        pass
    tpipeCV = TrPipe((6, 7), "v")
    first_cv = True
    for m in range(NH):
        for cc in range(CVW):
            w0 = min(128, NCH - 128 * cc)
            xw = [(vec.h, vec.n)] if first_cv else []
            if first_cv:
                xw.append((ten.h, ten.n))
            first_cv = False
            tpipeCV.go(c_cvft[:, NCH * m + 128 * cc:NCH * m + 128 * cc + w0],
                       g_cv[0:w0,
                            (NH * 128) * cc + 128 * m:(NH * 128) * cc + 128 * (m + 1)],
                       w0=w0, extra_waits=xw)
    bld.wait("g", st0.h, st0.n)
    bld.wait("g", st1.h, st1.n)
    barrier()
    for _ in qkv_pass("q"):
        pass
    bld.wait("g", st0.h, st0.n)
    bld.wait("g", st1.h, st1.n)
    barrier()
    B_ST0, B_ST1 = st0.n, st1.n

    phase_reset()
    e_cm = alloc(QT * NCH, BF16)
    e_wm = alloc(QT * 384, BF16)
    e_k = [alloc(OWN + WINDOW, BF16) for _ in range(2)]
    e_v = [alloc(WT * 128, BF16) for _ in range(2)]
    e_q = [alloc(OWN, BF16) for _ in range(2)]
    e_pc = [alloc(NCH, BF16) for _ in range(4)]
    e_pw = [alloc(384, BF16) for _ in range(4)]
    NPT = CVW + 3
    e_pt = [alloc(NPT * 128, BF16) for _ in range(2)]
    e_D = [alloc(128, BF16) for _ in range(4)]
    e_nm = alloc(16, F32)

    bld.dma("g", e_cm.rearrange("p (t c) -> p t c", c=NCH),
            cmask.transpose([1, 0, 2]), dm2)
    mld = bld.dma("g", e_wm.rearrange("p (t c) -> p t c", c=384),
                  wmask.transpose([1, 0, 2]), dm2)
    bld.dma("g", wobf_h, woT, dw)
    PRE_WO = dw.n
    bld.dma("g", wrtbf_h, wrt, dw)
    PRE_WRT = dw.n
    for e in range(NE):
        bld.dma("g", w1bf_h[e], ws1 if e == 0 else w1[e - 1], dw)
        bld.dma("g", w2bf_h[e], ws2 if e == 0 else w2[e - 1], dw)
    PRECAST_DONE = dw.n

    kv_ld = {}
    for m in range(min(2, NH)):
        p = m % 2
        bld.dma("y", e_k[p], kwin_h[m],
                dyp[p], waits=([(stp[0].h, B_ST0), (stp[1].h, B_ST1)]
                               if m == 0 else ()))
        bld.dma("y", e_v[p], vwin_h[m], dyp[p])
        kv_ld[m] = bld.dma("y", e_q[p], q_hbm[m], dyp[p])

    N_IT = NH * QT
    sc_free = {}
    sw_free = {}
    o_free = {}
    pc_free = {}
    av_done = {}
    nmd_d = {}
    dbl_d = {}
    tr_ev = {}

    def e_scores(i):
        m, t = divmod(i, QT)
        p = m % 2
        tp = i % 2
        wc = PADC + 8 * (t + 1)
        wv = [(dyp[p].h, kv_ld[m])] if t == 0 else []
        if i == 0:
            wv.append((dm2.h, mld))
        if i - 2 in sw_free:
            wv.append((scl.h, sw_free[i - 2]))
        bld.emit("t", "matmul", pbf[tp][:, 0:wc], g_ident,
                 e_cm[:, NCH * t:NCH * t + wc], start=True, stop=False,
                 waits=wv)
        bld.emit("t", "matmul", pbf[tp][:, 0:wc],
                 e_q[p][:, 128 * t:128 * (t + 1)],
                 g_ck[:, NCH * m:NCH * m + wc], start=False, stop=True,
                 inc=ten.h)
        TEN()
        bld.emit("t", "matmul", pbf[2 + tp][:, 0:384], g_ident,
                 e_wm[:, 384 * t:384 * (t + 1)], start=True, stop=False)
        bld.emit("t", "matmul", pbf[2 + tp][:, 0:384],
                 e_q[p][:, 128 * t:128 * (t + 1)],
                 e_k[p][:, 128 * t:128 * t + 384], start=False, stop=True,
                 inc=ten.h)
        sc_free[i] = st = TEN()
        nmd_d[i] = st

    def e_sm1(i):
        m, t = divmod(i, QT)
        tp = i % 2
        nb = 8 * tp
        wc = PADC + 8 * (t + 1)
        swd = sc_free[i]
        wv = [(ten.h, swd)]
        if i - 2 in sw_free:
            wv.append((scl.h, sw_free[i - 2]))
        bld.emit("v", "reduce_max", e_nm[:, nb:nb + 1], pbf[tp][:, 0:wc],
                 axis=AX.X, negate=True, waits=wv)
        bld.emit("v", "reduce_max", e_nm[:, nb + 1:nb + 2],
                 pbf[2 + tp][:, 0:384], axis=AX.X, negate=True, inc=vec.h)
        VEC()
        bld.emit("v", "tensor_tensor", e_nm[:, nb + 2:nb + 3],
                 e_nm[:, nb:nb + 1], e_nm[:, nb + 1:nb + 2], ALU.min,
                 waits=[(vec.h, vec.n)], inc=vec.h)
        nmd_d[i] = VEC()

    def e_sm2(i):
        m, t = divmod(i, QT)
        tp = i % 2
        nb = 8 * tp
        wc = PADC + 8 * (t + 1)
        p4 = i % 4
        ew = [(vec.h, nmd_d[i])]
        if i - 4 in pc_free:
            ew.append((ten.h, pc_free[i - 4]))
        bld.emit("s", "activation", e_pc[p4][:, 0:wc], pbf[tp][:, 0:wc],
                 ACT.Exp, bias=e_nm[:, nb + 2:nb + 3], scale=1.0,
                 accum_out=e_nm[:, nb + 3:nb + 4], waits=ew, inc=scl.h)
        sc_free[i] = SCL()
        bld.emit("s", "activation", e_pw[p4][:, 0:384], pbf[2 + tp][:, 0:384],
                 ACT.Exp, bias=e_nm[:, nb + 2:nb + 3], scale=1.0,
                 accum_out=e_nm[:, nb + 4:nb + 5], inc=scl.h)
        sw_free[i] = SCL()

    def e_sm3(i):
        tp = i % 2
        nb = 8 * tp
        bld.emit("v", "tensor_tensor", e_nm[:, nb + 5:nb + 6],
                 e_nm[:, nb + 3:nb + 4], e_nm[:, nb + 4:nb + 5], ALU.add,
                 waits=[(scl.h, sw_free[i])], inc=vec.h)
        VEC()
        bld.emit("v", "reciprocal", e_nm[:, nb + 6:nb + 7],
                 e_nm[:, nb + 5:nb + 6], waits=[(vec.h, vec.n)], inc=vec.h)
        VEC()
        dw2 = [(vec.h, vec.n)]
        if i - 4 in pc_free:
            dw2.append((ten.h, pc_free[i - 4]))
        bld.emit("v", "tensor_scalar", e_D[i % 4], g_ident,
                 e_nm[:, nb + 6:nb + 7], None, ALU.mult, waits=dw2,
                 inc=vec.h)
        dbl_d[i] = VEC()

    def e_tr(i):
        m, t = divmod(i, QT)
        tp = i % 2
        wc = PADC + 8 * (t + 1)
        nwc = (wc + 127) // 128
        p4 = i % 4
        for j in range(nwc + 3):
            if j < nwc:
                w0 = min(128, wc - 128 * j)
                src = e_pc[p4][:, 128 * j:128 * j + w0]
            else:
                w0 = 128
                src = e_pw[p4][:, 128 * (j - nwc):128 * (j - nwc + 1)]
            if j < 2:
                dstp = pbf[6 + j][0:w0, 0:128]
                prev = tr_ev.get((i - 1, j))
            else:
                dstp = pbf[4 + tp][0:w0, 128 * (j - 1):128 * j]
                prev = tr_ev.get((i - 2, 2)) if j == 2 else None
            wv = []
            if j == 0:
                wv = [(vec.h, dbl_d[i]), (scl.h, sw_free[i])]
            if j == 2 and i - 2 in o_free:
                wv.append((scl.h, o_free[i - 2]))
            if prev:
                wv.append(((vec.h if prev[0] == "v" else scl.h), prev[1]))
            bld.emit("t", "matmul", dstp, src, e_D[p4],
                     start=True, stop=True, waits=wv, inc=ten.h)
            trd = TEN()
            if j < 2:
                ew = [(ten.h, trd)]
                if j == 0 and i - 2 in av_done:
                    ew.append((ten.h, av_done[i - 2]))
                dst = e_pt[tp][0:w0, 128 * j:128 * (j + 1)]
                if j == 0:
                    bld.emit("v", "tensor_copy", dst, dstp, waits=ew,
                             inc=vec.h)
                    tr_ev[(i, j)] = ("v", VEC())
                else:
                    bld.emit("s", "copy", dst, dstp, waits=ew, inc=scl.h)
                    tr_ev[(i, j)] = ("s", SCL())
        nt = nwc + 3
        bld.emit("v", "tensor_copy", e_pt[tp][:, 256:128 * nt],
                 pbf[4 + tp][:, 128:128 * (nt - 1)], waits=[(ten.h, trd)],
                 inc=vec.h)
        tr_ev[(i, 2)] = ("v", VEC())
        pc_free[i] = ten.n

    def e_av(i):
        m, t = divmod(i, QT)
        p = m % 2
        tp = i % 2
        wc = PADC + 8 * (t + 1)
        nwc = (wc + 127) // 128
        wv = []
        vmax = max((c for jt in (0, 1, 2)
                    for s2, c in (tr_ev[(i, jt)],) if s2 == "v"), default=0)
        smax = max((c for jt in (0, 1, 2)
                    for s2, c in (tr_ev[(i, jt)],) if s2 == "s"), default=0)
        if vmax:
            wv.append((vec.h, vmax))
        if smax:
            wv.append((scl.h, smax))
        if i - 2 in o_free:
            wv.append((scl.h, o_free[i - 2]))
        first = True
        for j in range(nwc):
            w0 = min(128, wc - 128 * j)
            bld.emit("t", "matmul", pbf[4 + tp][:, 0:128],
                     g_cv[0:w0, (NH * 128) * j + 128 * m:(NH * 128) * j + 128 * (m + 1)],
                     e_pt[tp][0:w0, 128 * j:128 * (j + 1)],
                     start=first, stop=False, waits=(wv if first else ()))
            first = False
        for j in range(3):
            last = (j == 2)
            bld.emit("t", "matmul", pbf[4 + tp][:, 0:128],
                     e_v[p][:, 128 * (t + j):128 * (t + j + 1)],
                     e_pt[tp][:, 128 * (nwc + j):128 * (nwc + j + 1)],
                     start=False, stop=last, inc=(ten.h if last else None))
        av_done[i] = TEN()
        bld.emit("s", "copy", g_o[:, OWN * m + 128 * t:OWN * m + 128 * (t + 1)],
                 pbf[4 + tp][:, 0:128], waits=[(ten.h, av_done[i])],
                 inc=scl.h)
        o_free[i] = SCL()
        if t == QT - 1 and m + 2 < NH:
            bld.dma("y", e_k[p], kwin_h[m + 2], dyp[p],
                    waits=[(ten.h, ten.n)])
            bld.dma("y", e_v[p], vwin_h[m + 2], dyp[p])
            kv_ld[m + 2] = bld.dma("y", e_q[p], q_hbm[m + 2], dyp[p])

    for step in range(N_IT + 3):
        if step < N_IT:
            e_scores(step)
        if 1 <= step <= N_IT:
            e_sm1(step - 1)
            e_sm2(step - 1)
        if 2 <= step <= N_IT + 1:
            e_sm3(step - 2)
        if step >= 3:
            e_tr(step - 3)
            e_av(step - 3)
    barrier()

    phase_reset()
    f_wo = alloc(KC * H, BF16)
    f_xr = [alloc(H, F32) for _ in range(2)]
    f_ln2 = alloc(H, BF16)
    f_ln = [f_ln2, f_ln2]
    f_tr = [alloc(KC * 128, BF16) for _ in range(2)]
    f_wr = alloc(KC * 8, BF16)
    f_stat = alloc(6 * NSQ, F32)
    f_ag = alloc(4, F32)
    f_eps = alloc(2, F32)
    f_istd = alloc(2, F32)
    f_nmu = alloc(2, F32)
    f_lg = alloc(8, F32)
    f_m8 = alloc(8, F32)
    f_nm1 = alloc(2, F32)
    f_den = alloc(2, F32)
    f_sel = alloc(8, F32)

    wold = bld.dma("s", f_wo.rearrange("p (kc c) -> p kc c", c=H),
                   wobf_h.rearrange("(kc p) c -> p kc c", p=128), ds1,
                   waits=[(dw.h, PRE_WO)])
    wrld = bld.dma("s", f_wr.rearrange("p (kc c) -> p kc c", c=8),
                   wrtbf_h.rearrange("(kc p) c -> p kc c", p=128), ds2,
                   waits=[(dw.h, PRE_WRT)])
    x_ld = [0, 0]
    for t in range(min(2, QT)):
        x_ld[t % 2] = bld.dma("y", f_xr[t % 2],
                              x_ctx[PAD + 128 * t:PAD + 128 * (t + 1), :],
                              dyp[t % 2])
    f_psfree = {}
    f_store = [0, 0]
    f_trlast = 0
    f_r1free = [0, 0]
    rt_evd = 0
    tpipeF = TrPipe((6, 7), "s")
    for t in range(QT):
        p = t % 2
        for jg in range(JG):
            bk = jg % 2
            wv = []
            if t == 0 and jg == 0:
                wv.append((ds1.h, wold))
            if bk in f_psfree:
                wv.append((vec.h, f_psfree[bk]))
            for kcI in range(KC):
                bld.emit("t", "matmul", pbf[bk][:, 0:512],
                         g_o[:, OWN * kcI + 128 * t:OWN * kcI + 128 * (t + 1)],
                         f_wo[:, H * kcI + 512 * jg:H * kcI + 512 * (jg + 1)],
                         start=(kcI == 0), stop=(kcI == KC - 1),
                         waits=(wv if kcI == 0 else ()),
                         inc=(ten.h if kcI == KC - 1 else None))
            opd = TEN()
            wv = [(ten.h, opd), (dyp[p].h, x_ld[p])]
            if jg == 0 and t >= 2:
                wv.append((sfp[p].h, f_store[p]))
                wv.append((vec.h, f_r1free[p]))
            bld.emit("v", "tensor_tensor", f_xr[p][:, 512 * jg:512 * (jg + 1)],
                     pbf[bk][:, 0:512], f_xr[p][:, 512 * jg:512 * (jg + 1)],
                     ALU.add, waits=wv, inc=vec.h)
            f_psfree[bk] = VEC()
        r1d = vec.n
        bld.dma("y", r1_h[128 * t:128 * (t + 1), :], f_xr[p], sfp[p],
                waits=[(vec.h, r1d)])
        f_store[p] = sfp[p].n
        for cchunk in range(NSQ):
            bld.emit("v", "bn_stats", f_stat[:, 6 * cchunk:6 * cchunk + 6],
                     f_xr[p][:, 512 * cchunk:512 * (cchunk + 1)],
                     waits=([(vec.h, vec.n)] if cchunk == 0 else ()),
                     inc=(vec.h if cchunk == NSQ - 1 else None))
        VEC()
        bld.emit("v", "bn_aggr", f_ag[:, 0:2], f_stat[:, 0:6 * NSQ],
                 waits=[(vec.h, vec.n)], inc=vec.h)
        VEC()
        bld.emit("v", "tensor_scalar", f_eps[:, 0:1], f_ag[:, 1:2], EPS, None,
                 ALU.add, waits=[(vec.h, vec.n)], inc=vec.h)
        vrec = VEC()
        bld.emit("s", "activation", f_istd[:, 1:2], f_eps[:, 0:1], ACT.Ln,
                 waits=[(vec.h, vrec)], inc=scl.h)
        SCL()
        bld.emit("s", "activation", f_istd[:, 0:1], f_istd[:, 1:2], ACT.Exp,
                 scale=-0.5, waits=[(scl.h, scl.n)], inc=scl.h)
        srec = SCL()
        bld.emit("v", "tensor_scalar", f_nmu[:, 0:1], f_ag[:, 0:1],
                 f_istd[:, 0:1], -1.0, ALU.mult, ALU.mult,
                 waits=[(scl.h, srec)], inc=vec.h)
        VEC()
        bld.emit("v", "tensor_scalar", f_ln[p], f_xr[p], f_istd[:, 0:1],
                 f_nmu[:, 0:1], ALU.mult, ALU.add,
                 waits=[(vec.h, vec.n)] +
                       ([(ten.h, f_trlast)] if t >= 1 else []), inc=vec.h)
        lnd = VEC()
        f_r1free[p] = lnd
        if t + 2 < QT:
            x_ld[p] = bld.dma("y", f_xr[p],
                              x_ctx[PAD + 128 * (t + 2):PAD + 128 * (t + 3), :],
                              dyp[p],
                              waits=[(vec.h, lnd), (sfp[p].h, f_store[p])])
        for kcI in range(KC):
            ew = []
            if kcI == 0 and t >= 2:
                ew.append((sfp[p].h, f_store[p]))
            tpipeF.go(f_ln[p][:, 128 * kcI:128 * (kcI + 1)],
                      f_tr[p][:, 128 * kcI:128 * (kcI + 1)],
                      extra_waits=([(vec.h, lnd)] if kcI == 0 else ()),
                      evac_waits=ew)
        f_trlast = ten.n
        evd = scl.n
        wv = [(scl.h, evd), (ds2.h, wrld)]
        if rt_evd:
            wv.append((vec.h, rt_evd))
        for kcI in range(KC):
            bld.emit("t", "matmul", pbf[4][:, 0:8],
                     f_tr[p][:, 128 * kcI:128 * (kcI + 1)],
                     f_wr[:, 8 * kcI:8 * (kcI + 1)],
                     start=(kcI == 0), stop=(kcI == KC - 1),
                     waits=(wv if kcI == 0 else ()),
                     inc=(ten.h if kcI == KC - 1 else None))
        rtd = TEN()
        bld.emit("v", "tensor_copy", f_lg[:, 0:8], pbf[4][:, 0:8],
                 waits=[(ten.h, rtd)], inc=vec.h)
        rt_evd = VEC()
        bld.emit("v", "max", f_m8[:, 0:8], f_lg[:, 0:8],
                 waits=[(vec.h, vec.n)], inc=vec.h)
        VEC()
        bld.emit("v", "tensor_scalar", f_nm1[:, 0:1], f_m8[:, 0:1], -1.0, None,
                 ALU.mult, waits=[(vec.h, vec.n)])
        bld.emit("v", "tensor_scalar", f_sel[:, 0:8], f_lg[:, 0:8],
                 f_m8[:, 1:2], None, ALU.is_ge, inc=vec.h)
        seld = VEC()
        bld.emit("s", "activation", f_den[:, 0:1], f_m8[:, 1:2], ACT.Exp,
                 bias=f_nm1[:, 0:1], scale=1.0, waits=[(vec.h, seld)])
        bld.emit("s", "activation", f_lg[:, 0:8], f_lg[:, 0:8], ACT.Exp,
                 bias=f_nm1[:, 0:1], scale=1.0, inc=scl.h)
        expd = SCL()
        bld.emit("v", "tensor_scalar", f_den[:, 1:2], f_den[:, 0:1], 1.0, None,
                 ALU.add, waits=[(scl.h, expd)], inc=vec.h)
        VEC()
        bld.emit("v", "reciprocal", f_den[:, 0:1], f_den[:, 1:2],
                 waits=[(vec.h, vec.n)], inc=vec.h)
        VEC()
        bld.emit("v", "tensor_tensor", f_sel[:, 0:8], f_sel[:, 0:8],
                 f_lg[:, 0:8], ALU.mult, waits=[(vec.h, vec.n)], inc=vec.h)
        VEC()
        bld.emit("v", "tensor_scalar", g_comb[:, 8 * t:8 * (t + 1)],
                 f_sel[:, 0:8], f_den[:, 0:1], None, ALU.mult,
                 waits=[(vec.h, vec.n)], inc=vec.h)
        VEC()
        bld.dma("y", lnx2_h[t].transpose([1, 0, 2]),
                f_tr[p].rearrange("p (kc c) -> p kc c", c=128),
                sfp[p], waits=[(scl.h, evd), (ten.h, rtd)])
        f_store[p] = sfp[p].n
    bld.wait("y", sf0.h, sf0.n)
    bld.wait("y", sf1.h, sf1.n)
    barrier()
    F_ST0, F_ST1 = sf0.n, sf1.n

    phase_reset(reuse_o=True)
    g_w1 = alloc(KC * I, BF16)
    g_w2 = alloc(ICH * H, BF16)
    g_h = alloc(ICH * 512, BF16)
    g_acc = alloc(4 * H, F32)
    g_lnx2_b = [alloc(KC * 512, BF16) for _ in range(2)]
    g_r1 = [alloc(H, F32) for _ in range(2)]

    w1_ld = bld.dma("s", g_w1.rearrange("p (kc c) -> p kc c", c=I),
                    w1bf_h[0].rearrange("(kc p) c -> p kc c", p=128), ds1,
                    waits=[(dw.h, PRECAST_DONE)])
    w2_ld = bld.dma("s", g_w2.rearrange("p (ic c) -> p ic c", c=H),
                    w2bf_h[0].rearrange("(ic p) c -> p ic c", p=128), ds2)
    h_done = {}
    yst = [0, 0, 0, 0]
    od_g = {}
    l2_tgt = {}
    pass_hall = []

    def emit_l2(ps2):
        l2v = g_lnx2_b[ps2 % 2].rearrange("p (kc g c) -> p kc g c",
                                          g=4, c=128)
        for j in range(4):
            l2_tgt[ps2] = bld.dma(
                "s", l2v[:, :, j, :],
                lnx2_h[4 * ps2 + j].transpose([1, 0, 2]), dl0,
                waits=([(scl.h, pass_hall[ps2 - 2])]
                       if (j == 0 and ps2 >= 2) else ()))

    emit_l2(0)
    for pss in range(NPASS):
        g_lnx2 = g_lnx2_b[pss % 2]
        l2 = l2_tgt[pss]
        acc_part = {}
        acc_done = {}
        for e in range(NE):
            for ic in range(ICH):
                bk = ic % 2
                wv = []
                if ic == 0:
                    wv.append((ds1.h, w1_ld))
                    wv.append((dl0.h, l2))
                    if e >= 1:
                        wv.append((vec.h, acc_done[e - 1]))
                if bk in h_done:
                    wv.append((scl.h, h_done[bk]))
                for kcI in range(KC):
                    bld.emit("t", "matmul", pbf[bk][:, 0:512],
                             g_w1[:, I * kcI + 128 * ic:I * kcI + 128 * (ic + 1)],
                             g_lnx2[:, 512 * kcI:512 * (kcI + 1)],
                             start=(kcI == 0), stop=(kcI == KC - 1),
                             waits=(wv if kcI == 0 else ()),
                             inc=(ten.h if kcI == KC - 1 else None))
                w1d = TEN()
                bld.emit("s", "activation", g_h[:, 512 * ic:512 * (ic + 1)],
                         pbf[bk][:, 0:512], ACT.Silu, waits=[(ten.h, w1d)],
                         inc=scl.h)
                h_done[bk] = SCL()
            hall = scl.n
            if not (pss == NPASS - 1 and e == NE - 1):
                w1_ld = bld.dma("s", g_w1.rearrange("p (kc c) -> p kc c", c=I),
                                w1bf_h[(e + 1) % NE].rearrange("(kc p) c -> p kc c", p=128),
                                ds1)
            for tt in range(4):
                gt = 4 * pss + tt
                for jg in range(JG):
                    bk = 2 + ((tt * JG + jg) % 2)
                    wv = [(scl.h, hall)]
                    if tt == 0 and jg == 0:
                        wv.append((ds2.h, w2_ld))
                    if bk in acc_part:
                        wv.append((vec.h, acc_part[bk]))
                    for ic in range(ICH):
                        bld.emit("t", "matmul", pbf[bk][:, 0:512],
                                 g_h[:, 512 * ic + 128 * tt:512 * ic + 128 * (tt + 1)],
                                 g_w2[:, H * ic + 512 * jg:H * ic + 512 * (jg + 1)],
                                 start=(ic == 0), stop=(ic == ICH - 1),
                                 waits=(wv if ic == 0 else ()),
                                 inc=(ten.h if ic == ICH - 1 else None))
                    w2d = TEN()
                    accsl = g_acc[:, H * tt + 512 * jg:H * tt + 512 * (jg + 1)]
                    if e == 0:
                        aw = [(ten.h, w2d)]
                        if pss > 0 and jg == 0:
                            aw.append((sy.h, yst[tt]))
                        bld.emit("v", "tensor_copy", accsl, pbf[bk][:, 0:512],
                                 waits=aw, inc=vec.h)
                    else:
                        bld.emit("v", "scalar_tensor_tensor", accsl,
                                 pbf[bk][:, 0:512],
                                 g_comb[:, 8 * gt + (e - 1):8 * gt + e], accsl,
                                 ALU.mult, ALU.add, waits=[(ten.h, w2d)],
                                 inc=vec.h)
                    acc_part[bk] = VEC()
                if e == NE - 1:
                    p2 = tt % 2
                    r1w = [(scl.h, hall)]
                    if od_g.get(p2):
                        r1w.append((vec.h, od_g[p2]))
                    r1v = bld.dma("y", g_r1[p2],
                                  r1_h[512 * pss + 128 * tt:
                                       512 * pss + 128 * (tt + 1), :],
                                  dyp[p2], waits=r1w)
                    ow = [(dyp[p2].h, r1v),
                          (vec.h, acc_part[2 + (tt * JG + JG - 1) % 2])]
                    bld.emit("v", "tensor_tensor", g_acc[:, H * tt:H * (tt + 1)],
                             g_acc[:, H * tt:H * (tt + 1)], g_r1[p2], ALU.add,
                             waits=ow, inc=vec.h)
                    od = VEC()
                    od_g[p2] = od
                    yst[tt] = bld.dma("g",
                                      y[512 * pss + 128 * tt:512 * pss + 128 * (tt + 1), :],
                                      g_acc[:, H * tt:H * (tt + 1)], sy,
                                      waits=[(vec.h, od)])
            acc_done[e] = vec.n
            if e == NE - 1:
                pass_hall.append(hall)
            if not (pss == NPASS - 1 and e == NE - 1):
                w2_ld = bld.dma("s", g_w2.rearrange("p (ic c) -> p ic c", c=H),
                                w2bf_h[(e + 1) % NE].rearrange("(ic p) c -> p ic c", p=128),
                                ds2, waits=[(ten.h, ten.n)])
            if e == 0 and pss + 1 < NPASS:
                emit_l2(pss + 1)
    bld.wait("g", sy.h, sy.n)
    bld.wait("y", sy.h, sy.n)

    import contextlib
    with contextlib.ExitStack() as cx:
        block = cx.enter_context(nc.Block())
        P = bld.P

        @block.tensor
        def _(t):
            for f in P["t"]:
                f(t)

        @block.vector
        def _(v):
            for f in P["v"]:
                f(v)

        @block.scalar
        def _(s):
            for f in P["s"]:
                f(s)

        @block.gpsimd
        def _(g):
            for f in P["g"]:
                f(g)

        @block.sync
        def _(yq):
            for f in P["y"]:
                f(yq)

    for cm in reversed(semctx):
        cm.__exit__(None, None, None)
    for cm in reversed(pb_ctx):
        cm.__exit__(None, None, None)
    return nc


def rope_tables(pos):
    inv = 1.0 / (10000.0 ** (np.arange(0, 128, 2, dtype=np.float32) / 128))
    inv = inv.astype(np.float32)
    f = (pos.astype(np.float32)[:, None] * inv[None, :]).astype(np.float32)
    fc = np.exp(1j * f).astype(np.complex64)
    c = fc.real.T
    sn = fc.imag.T
    C = np.repeat(c, 2, axis=0)
    S3 = np.empty((128, len(pos)), dtype=np.float32)
    S3[0::2] = sn
    S3[1::2] = -sn
    return np.ascontiguousarray(C), np.ascontiguousarray(S3)


def make_core_inputs(cfg, core, full):
    d = derive(cfg)
    H = cfg["H"]
    OWN, WINDOW = cfg["OWN"], cfg["WINDOW"]
    CTX, PAD, NCH, QT = d["CTX"], d["PAD"], d["NCH"], d["QT"]
    b = core // 2
    half = core % 2
    OS = half * OWN
    x = np.asarray(full["L0_attn_x"][b], dtype=np.float32)
    xc = np.zeros((CTX, H), dtype=np.float32)
    g0 = OS - PAD
    lo = max(0, -g0)
    xc[lo:] = x[g0 + lo:g0 + CTX]
    pos = np.arange(CTX, dtype=np.int64) + g0
    posc = np.where(pos < 0, 0, pos)
    kc_, ks_ = rope_tables(posc)
    qc_, qs_ = rope_tables(pos[PAD:])
    lq = PAD + np.arange(OWN)
    cm = np.empty((QT, 128, NCH), dtype=np.float32)
    wm = np.empty((QT, 128, 384), dtype=np.float32)
    delta = (PAD - OS) // 16
    for t in range(QT):
        lqt = lq[128 * t:128 * (t + 1)][:, None]
        ch = np.arange(NCH)[None, :]
        vis = (ch < (lqt + 1) // 16) & (ch >= delta)
        cm[t] = np.where(vis, 0.0, NEG)
        lk = (PAD - WINDOW + 128 * t) + np.arange(384)[None, :]
        vis = (lk <= lqt) & (lk > lqt - WINDOW) & (lk >= PAD - OS)
        wm[t] = np.where(vis, 0.0, NEG)
    return {
        "x_ctx": xc,
        "wqT": np.ascontiguousarray(np.asarray(full["L0_attn_wq"]).T),
        "wkT": np.ascontiguousarray(np.asarray(full["L0_attn_wk"]).T),
        "wvT": np.ascontiguousarray(np.asarray(full["L0_attn_wv"]).T),
        "woT": np.ascontiguousarray(np.asarray(full["L0_attn_wo"]).T),
        "ws1": np.asarray(full["L0_ffn_w_shared1"], dtype=np.float32),
        "ws2": np.asarray(full["L0_ffn_w_shared2"], dtype=np.float32),
        "w1": np.asarray(full["L0_ffn_w1"], dtype=np.float32),
        "w2": np.asarray(full["L0_ffn_w2"], dtype=np.float32),
        "wrt": np.asarray(full["L0_ffn_w_router"], dtype=np.float32),
        "kcos": kc_, "ks3": ks_, "qcos": qc_, "qs3": qs_,
        "cmask": cm, "wmask": wm,
        "identin": np.eye(128, dtype=np.float32),
        "pswapin": np.ascontiguousarray(
            np.eye(128, dtype=np.float32)[np.arange(128) ^ 1]),
    }


def kernel(**inputs):
    cfg = cfg_full()
    nc = build_program(cfg)
    in_maps = [make_core_inputs(cfg, c, inputs) for c in range(cfg["NCORES"])]
    res = run_bass_kernel_spmd(nc, in_maps, core_ids=list(range(cfg["NCORES"])))
    B, S, H = cfg["B"], cfg["S"], cfg["H"]
    out = np.empty((B, S, H), dtype=np.float32)
    for c in range(cfg["NCORES"]):
        b, half = c // 2, c % 2
        out[b, half * cfg["OWN"]:(half + 1) * cfg["OWN"]] = res.results[c]["y"]
    return out


# version 19
# speedup vs baseline: 1.0924x; 1.0039x over previous
import numpy as np
import concourse.bass as bass
import concourse.mybir as mybir
from concourse.bass_utils import run_bass_kernel_spmd

F32 = mybir.dt.float32
BF16 = mybir.dt.bfloat16
U8 = mybir.dt.uint8
ALU = mybir.AluOpType
ACT = mybir.ActivationFunctionType
AX = mybir.AxisListType

NEG = -1.0e30
EPS = 1e-5


def cfg_full():
    return dict(B=4, S=4096, H=2048, NH=16, I=1408, E=8, WINDOW=256, OWN=2048,
                NCORES=8)


def derive(c):
    d = dict(c)
    d["KC"] = c["H"] // 128
    d["PAD"] = c["OWN"]
    d["CTX"] = 2 * c["OWN"]
    d["NCH"] = d["CTX"] // 16
    d["PADC"] = d["PAD"] // 16
    d["QT"] = c["OWN"] // 128
    d["WT"] = (c["OWN"] + c["WINDOW"]) // 128
    d["ICH"] = c["I"] // 128
    d["NE"] = c["E"] + 1
    d["CT"] = d["CTX"] // 128
    d["NG"] = d["CTX"] // 512
    d["NPASS"] = c["OWN"] // 512
    d["JG"] = c["H"] // 512
    d["CVW"] = (d["NCH"] + 127) // 128
    d["SCALE"] = 1.0 / np.sqrt(128.0)
    return d


class Sem:
    def __init__(self, h):
        self.h = h
        self.n = 0


class Build:
    def __init__(self, nc):
        self.nc = nc
        self.P = {"t": [], "v": [], "s": [], "g": [], "y": []}

    def emit(self, stream, meth, *args, waits=(), inc=None, inck=1, **kw):
        waits = tuple(w for w in waits if w is not None)

        def f(eng, meth=meth, args=args, kw=kw, waits=waits, inc=inc, inck=inck):
            for h, v in waits:
                if v > 0:
                    eng.wait_ge(h, v)
            ins = getattr(eng, meth)(*args, **kw)
            if inc is not None:
                ins.then_inc(inc, inck)
        self.P[stream].append(f)

    def dma(self, stream, dst, src, sem, waits=()):
        sem.n += 16
        self.emit(stream, "dma_start", dst, src, waits=waits, inc=sem.h, inck=16)
        return sem.n

    def wait(self, stream, h, v):
        def f(eng, h=h, v=v):
            if v > 0:
                eng.wait_ge(h, v)
        self.P[stream].append(f)




def build_program(cfg, debug=False):
    d = derive(cfg)
    H, NH, I, E = cfg["H"], cfg["NH"], cfg["I"], cfg["E"]
    OWN, WINDOW = cfg["OWN"], cfg["WINDOW"]
    KC, CTX, PAD, NCH, PADC = d["KC"], d["CTX"], d["PAD"], d["NCH"], d["PADC"]
    QT, WT, ICH, NE, CT, NG = d["QT"], d["WT"], d["ICH"], d["NE"], d["CT"], d["NG"]
    NPASS, JG, CVW = d["NPASS"], d["JG"], d["CVW"]
    SCALE = d["SCALE"]
    NSQ = H // 512

    nc = bass.Bass("TRN2", target_bir_lowering=False, detect_race_conditions=False)

    def din(name, shape):
        return nc.dram_tensor(name, list(shape), F32, kind="ExternalInput").ap()

    x_ctx = din("x_ctx", (CTX, H))
    wqT = din("wqT", (H, H))
    wkT = din("wkT", (H, H))
    wvT = din("wvT", (H, H))
    woT = din("woT", (H, H))
    ws1 = din("ws1", (H, I))
    ws2 = din("ws2", (I, H))
    w1 = din("w1", (E, H, I))
    w2 = din("w2", (E, I, H))
    wrt = din("wrt", (H, E))
    kcos = din("kcos", (128, CTX))
    ks3 = din("ks3", (128, CTX))
    qcos = din("qcos", (128, OWN))
    qs3 = din("qs3", (128, OWN))
    cmask = din("cmask", (QT, 128, NCH))
    wmask = din("wmask", (QT, 128, 384))
    identin = din("identin", (128, 128))
    pswapin = din("pswapin", (128, 128))

    y = nc.dram_tensor("y", [OWN, H], F32, kind="ExternalOutput").ap()

    def dint(name, shape, dt=BF16):
        kind = "ExternalOutput" if debug else "Internal"
        return nc.dram_tensor(name, list(shape), dt, kind=kind).ap()

    lnx_h = dint("lnx_h", (CT, KC, 128, 128))
    kwin_h = dint("kwin_h", (NH, 128, OWN + WINDOW))
    vwin_h = dint("vwin_h", (NH, 128, WT * 128))
    q_hbm = dint("q_hbm", (NH, 128, OWN))
    lnx2_h = dint("lnx2_h", (QT, KC, 128, 128))
    r1_h = dint("r1_h", (OWN, H), F32)
    w1bf_h = dint("w1bf_h", (NE, H, I))
    w2bf_h = dint("w2bf_h", (NE, I, H))
    wobf_h = dint("wobf_h", (H, H))
    wrtbf_h = dint("wrtbf_h", (H, E))

    HEAP_BASE = 16512
    NB32 = 51200
    NB16 = 161628
    assert HEAP_BASE + NB32 + NB16 <= 229340
    h32_t = nc.alloc_sbuf_tensor_at("heap32", [128, NB32 // 4], F32,
                                    offset=HEAP_BASE)
    h16_t = nc.alloc_sbuf_tensor_at("heap16", [128, NB16 // 2], BF16,
                                    offset=HEAP_BASE + NB32)
    h32 = h32_t.ap()
    h16 = h16_t.ap()

    off = {F32: 0, BF16: 0}
    himax = {F32: 0, BF16: 0}

    def alloc(cols, dt):
        esz = 4 if dt == F32 else 2
        o = off[dt]
        assert o % 32 == 0
        ap = (h32 if dt == F32 else h16)[:, o // esz:o // esz + cols]
        off[dt] = o + ((cols * esz + 31) // 32) * 32
        himax[dt] = max(himax[dt], off[dt])
        assert off[F32] <= NB32 and off[BF16] <= NB16, \
            f"heap overflow f32={off[F32]} bf16={off[BF16]}"
        return ap

    g_ident = alloc(128, BF16)
    g_pswap = alloc(128, BF16)
    g_ck = alloc(NH * NCH, BF16)
    g_cv = alloc(CVW * NH * 128, BF16)
    g_comb = alloc(QT * 8, F32)
    F32_PH = ((off[F32] + 255) // 256) * 256
    O_OFF = ((off[BF16] + 255) // 256) * 256
    off[BF16] = O_OFF
    g_o = alloc(NH * OWN, BF16)
    PH_BASE = ((off[BF16] + 255) // 256) * 256

    def phase_reset(reuse_o=False):
        off[BF16] = O_OFF if reuse_o else PH_BASE
        off[F32] = F32_PH

    pb_ctx = []
    pb = []
    for i in range(8):
        cm = nc.psum_tensor(f"pbank{i}", [128, 512], F32)
        pb.append(cm.__enter__())
        pb_ctx.append(cm)
    pbf = [p.ap() for p in pb]

    bld = Build(nc)
    semctx = []

    def S(name):
        cm = nc.semaphore(name)
        h = cm.__enter__()
        semctx.append(cm)
        return Sem(h)

    dw = S("dw")
    dm2 = S("dm2")
    sag = [S(f"sag{i}") for i in range(8)]
    dl0, dl1 = S("dl0"), S("dl1")
    dy0, dy1 = S("dy0"), S("dy1")
    ds1 = S("ds1")
    ds2 = S("ds2")
    st0, st1 = S("st0"), S("st1")
    sf0, sf1 = S("sf0"), S("sf1")
    sy = S("sy")
    ten, vec, scl = S("ten"), S("vec"), S("scl")
    phz = S("phz")
    dyp = [dy0, dy1]
    dlp = [dl0, dl1]
    stp = [st0, st1]
    sfp = [sf0, sf1]

    def TEN():
        ten.n += 1
        return ten.n

    def VEC():
        vec.n += 1
        return vec.n

    def SCL():
        scl.n += 1
        return scl.n

    def barrier():
        for stnm, sem in (("t", ten), ("v", vec), ("s", scl)):
            bld.emit(stnm, "nop", waits=[(sem.h, sem.n)], inc=phz.h)
        for stnm in ("g", "y"):
            bld.emit(stnm, "nop", inc=phz.h)
        phz.n += 5
        target = phz.n
        for stnm in ("t", "v", "s", "g", "y"):
            bld.wait(stnm, phz.h, target)

    class TrPipe:

        def __init__(self, banks, evac_stream="s"):
            self.banks = banks
            self.ev = {b: 0 for b in banks}
            self.i = 0
            self.evs = evac_stream

        def go(self, src_ap, dst_ap, w0=128, rhs=None, extra_waits=(),
               evac_waits=()):
            b = self.banks[self.i % 2]
            self.i += 1
            wv = list(extra_waits)
            if self.ev[b]:
                wv.append(((scl.h if self.evs == "s" else vec.h), self.ev[b]))
            bld.emit("t", "matmul", pbf[b][0:w0, 0:128], src_ap,
                     rhs if rhs is not None else g_ident,
                     start=True, stop=True, waits=wv, inc=ten.h)
            trd = TEN()
            bld.emit(self.evs, "copy" if self.evs == "s" else "tensor_copy",
                     dst_ap, pbf[b][0:w0, 0:128],
                     waits=[(ten.h, trd)] + list(evac_waits),
                     inc=(scl.h if self.evs == "s" else vec.h))
            self.ev[b] = SCL() if self.evs == "s" else VEC()
            return self.ev[b]

    bld.dma("g", g_ident, identin, dw)
    bld.dma("g", g_pswap, pswapin, dw)
    CONST_DONE = dw.n

    phase_reset(reuse_o=True)
    w_sb = alloc(KC * H, BF16)
    lnx_g = [alloc(KC * 512, BF16) for _ in range(2)]
    mark = off[BF16]
    rc_t = alloc(CTX, BF16)
    rs_t = alloc(CTX, BF16)
    end_kq = off[BF16]
    off[BF16] = mark
    c_vft = [alloc(512, BF16) for _ in range(2)]
    c_vtk = [alloc(512, BF16) for _ in range(2)]
    c_cvft = alloc(NH * NCH, BF16)
    off[BF16] = max(off[BF16], end_kq)
    b_v1 = [alloc(512, BF16) for _ in range(2)]
    b_t1 = [alloc(512, BF16) for _ in range(2)]
    b_kr = [alloc(512, BF16) for _ in range(2)]
    b_pool = alloc(64, F32)

    def lnx_load(gp, gsl, waits):
        v3 = lnx_g[gp].rearrange("p (kc g c) -> p kc g c", g=4, c=128)
        for j in range(4):
            n = bld.dma("y", v3[:, :, j, :],
                        lnx_h[4 * gsl + j].transpose([1, 0, 2]),
                        dlp[gp], waits=(waits if j == 0 else ()))
        return n

    def qkv_pass(kind, preload=None):
        wsrc = {"k": wkT, "v": wvT, "q": wqT}[kind]
        if preload is None:
            bld.dma("g", w_sb.rearrange("p (kc c) -> p kc c", c=H),
                    wsrc.rearrange("(kc p) c -> p kc c", p=128), dw)
            if kind == "q":
                bld.dma("g", rc_t[:, 0:OWN], qcos, dw)
                bld.dma("g", rs_t[:, 0:OWN], qs3, dw)
            wld = dw.n
        else:
            wld = preload
        tscale = 0
        if kind == "q":
            bld.emit("v", "tensor_scalar", rc_t[:, 0:OWN], rc_t[:, 0:OWN],
                     float(SCALE), None, ALU.mult, waits=[(dw.h, wld)])
            bld.emit("v", "tensor_scalar", rs_t[:, 0:OWN], rs_t[:, 0:OWN],
                     float(SCALE), None, ALU.mult, inc=vec.h)
            tscale = VEC()
        glist = list(range(NG)) if kind != "q" else list(range(NG // 2, NG))
        NI = len(glist) * NH
        ld_n = [0, 0]
        ps_free = {}
        psw_free = {}
        kr_store = {}
        r1_done = {}
        r2_done = {}
        ev_done = {}
        pl_done = {}
        grp_mm = {}
        for gi in range(min(2, len(glist))):
            ld_n[gi % 2] = lnx_load(gi % 2, glist[gi],
                                    [(sag[glist[gi] % 8].h,
                                      64 * (glist[gi] // 8 + 1))]
                                    if kind == "k" else ())

        def st_proj(i):
            gi, m = divmod(i, NH)
            gp = gi % 2
            gsl = glist[gi]
            pk = i % 2
            wv = []
            if m == 0:
                wv.append((dlp[gp].h, ld_n[gp]))
                if gi == 0:
                    wv.append((dw.h, wld))
            if i - 2 in ps_free:
                wv.append(((vec.h if kind != "v" else scl.h), ps_free[i - 2]))
            for kcI in range(KC):
                bld.emit("t", "matmul", pbf[pk][:, 0:512],
                         w_sb[:, H * kcI + 128 * m: H * kcI + 128 * (m + 1)],
                         lnx_g[gp][:, 512 * kcI:512 * (kcI + 1)],
                         start=(kcI == 0), stop=(kcI == KC - 1),
                         waits=(wv if kcI == 0 else ()),
                         inc=(ten.h if kcI == KC - 1 else None))
            grp_mm[i] = TEN()
            if m == NH - 1 and gi + 2 < len(glist):
                gn = glist[gi + 2]
                ld_n[gp] = lnx_load(gp, gn,
                                    [(ten.h, grp_mm[i])] +
                                    ([(sag[gn % 8].h, 64 * (gn // 8 + 1))]
                                     if kind == "k" else []))

        def st_rope1(i):
            gi, m = divmod(i, NH)
            gsl = glist[gi]
            pk = i % 2
            tb0 = 512 * gsl if kind == "k" else 512 * (gsl - NG // 2)
            wv = [(ten.h, grp_mm[i])]
            if tscale and i == 0:
                wv.append((vec.h, tscale))
            if i - 2 in r2_done:
                wv.append((vec.h, r2_done[i - 2]))
            bld.emit("v", "tensor_tensor", b_v1[pk], pbf[pk][:, 0:512],
                     rs_t[:, tb0:tb0 + 512], ALU.mult, waits=wv)
            bld.emit("v", "tensor_tensor", b_t1[pk], pbf[pk][:, 0:512],
                     rc_t[:, tb0:tb0 + 512], ALU.mult, inc=vec.h)
            r1_done[i] = VEC()
            ps_free[i] = r1_done[i]

        def st_swap(i):
            pk = i % 2
            wv = [(vec.h, r1_done[i])]
            if i - 2 in r2_done:
                wv.append((vec.h, r2_done[i - 2]))
            bld.emit("t", "matmul", pbf[4 + pk][:, 0:512], g_pswap,
                     b_v1[pk], start=True, stop=True, waits=wv, inc=ten.h)
            psw_free[i] = TEN()

        def st_rope2(i):
            gi, m = divmod(i, NH)
            gsl = glist[gi]
            pk = i % 2
            wv = [(ten.h, psw_free[i])]
            if i - 2 in kr_store:
                wv.append((stp[pk].h, kr_store[i - 2]))
            bld.emit("v", "tensor_tensor", b_kr[pk], b_t1[pk],
                     pbf[4 + pk][:, 0:512], ALU.add, waits=wv, inc=vec.h)
            v3 = VEC()
            if kind == "k":
                bld.emit("v", "reduce_sum", b_pool[:, 0:32],
                         b_kr[pk].rearrange("p (a b) -> p a b", b=16),
                         axis=AX.X, waits=[(vec.h, vec.n)], inc=vec.h)
                VEC()
                bld.emit("v", "tensor_scalar",
                         g_ck[:, NCH * m + 32 * gsl:NCH * m + 32 * (gsl + 1)],
                         b_pool[:, 0:32], 1.0 / 16.0, None, ALU.mult,
                         waits=[(vec.h, vec.n)], inc=vec.h)
                v3 = VEC()
            r2_done[i] = v3
            if kind == "k":
                lo_tok = 512 * gsl
                lo = max(0, (PAD - WINDOW) - lo_tok)
                if lo < 512:
                    u0 = (lo_tok + lo - (PAD - WINDOW)) // 128
                    kr_store[i] = bld.dma(
                        "g", kwin_h[m][:, 128 * u0:128 * u0 + (512 - lo)],
                        b_kr[pk][:, lo:512], stp[pk], waits=[(vec.h, v3)])
                else:
                    kr_store[i] = 0
            else:
                gq = gsl - NG // 2
                kr_store[i] = bld.dma("g", q_hbm[m][:, 512 * gq:512 * (gq + 1)],
                                      b_kr[pk], stp[pk], waits=[(vec.h, v3)])

        tpipeV = TrPipe((6, 7), "v")

        def st_vev(i):
            pk = i % 2
            ew = [(ten.h, grp_mm[i])]
            if i - 2 in pl_done:
                ew.append((vec.h, pl_done[i - 2]))
                if ev_t_free.get(i - 2):
                    ew.append((ten.h, ev_t_free[i - 2]))
            bld.emit("s", "copy", c_vft[pk], pbf[pk][:, 0:512],
                     waits=ew, inc=scl.h)
            ev_done[i] = SCL()
            ps_free[i] = ev_done[i]

        ev_t_free = {}

        def st_vpool(i):
            gi, m = divmod(i, NH)
            gsl = glist[gi]
            pk = i % 2
            bld.emit("v", "reduce_sum", b_pool[:, 0:32],
                     c_vft[pk].rearrange("p (a b) -> p a b", b=16),
                     axis=AX.X, waits=[(scl.h, ev_done[i]), (vec.h, vec.n)],
                     inc=vec.h)
            VEC()
            bld.emit("v", "tensor_scalar",
                     c_cvft[:, NCH * m + 32 * gsl:NCH * m + 32 * (gsl + 1)],
                     b_pool[:, 0:32], 1.0 / 16.0, None, ALU.mult,
                     waits=[(vec.h, vec.n)], inc=vec.h)
            pl_done[i] = VEC()
            lo_tok = 512 * gsl
            lo = max(0, (PAD - WINDOW) - lo_tok)
            if lo < 512:
                evw = [(stp[pk].h, kr_store[i - 2])] if kr_store.get(i - 2) else []
                for j in range(lo // 128, 4):
                    tpipeV.go(c_vft[pk][:, 128 * j:128 * (j + 1)],
                              c_vtk[pk][:, 128 * j:128 * (j + 1)],
                              extra_waits=([(scl.h, ev_done[i])]
                                           if j == lo // 128 else ()),
                              evac_waits=(evw if j == lo // 128 else ()))
                u0 = (lo_tok + lo - (PAD - WINDOW)) // 128
                kr_store[i] = bld.dma(
                    "g", vwin_h[m][:, 128 * u0:128 * u0 + (512 - lo)],
                    c_vtk[pk][:, lo:512], stp[pk], waits=[(vec.h, vec.n)])
                ev_t_free[i] = ten.n
            else:
                kr_store[i] = 0
                ev_t_free[i] = 0

        if kind == "v":
            for step in range(NI + 2):
                if step < NI:
                    st_proj(step)
                if 1 <= step <= NI:
                    st_vev(step - 1)
                if step >= 2:
                    st_vpool(step - 2)
                yield step
        else:
            for step in range(NI + 2):
                if step < NI:
                    st_proj(step)
                if 1 <= step <= NI:
                    st_rope1(step - 1)
                    st_swap(step - 1)
                if step >= 2:
                    st_rope2(step - 2)
                yield step


    a_x = [alloc(H, F32) for _ in range(2)]
    a_stat = alloc(48, F32)
    a_ag = alloc(4, F32)
    a_eps = alloc(2, F32)
    a_istd = alloc(4, F32)
    a_nmu = alloc(2, F32)
    ATOP = ((NB16 - 16384) // 32) * 32
    assert off[BF16] <= ATOP, f"A-top overlap {off[BF16]} > {ATOP}"
    a_ln = [h16[:, (ATOP + 4096 * i) // 2:(ATOP + 4096 * i) // 2 + H]
            for i in range(2)]
    a_tr = [h16[:, (ATOP + 8192 + 4096 * i) // 2:(ATOP + 8192 + 4096 * i) // 2 + H]
            for i in range(2)]

    bld.dma("g", w_sb.rearrange("p (kc c) -> p kc c", c=H),
            wkT.rearrange("(kc p) c -> p kc c", p=128), dw)
    bld.dma("g", rc_t, kcos, dw)
    bld.dma("g", rs_t, ks3, dw)
    WLD_K = dw.n

    a_xfree = [0, 0]
    a_trdone = [0, 0]
    a_stdone = [0, 0]
    a_vln = {}
    a_vagg = {}
    a_srec = {}
    tpipeA = TrPipe((6, 7), "s")

    def a_s1(t):
        p = t % 2
        ldv = bld.dma("y", a_x[p], x_ctx[128 * t:128 * (t + 1), :], dyp[p],
                      waits=([(vec.h, a_xfree[p])] if t >= 2 else ()))
        for cchunk in range(NSQ):
            bld.emit("v", "bn_stats", a_stat[:, 12 * p + 6 * cchunk // 1:
                                             12 * p + 6 * cchunk + 6]
                     if False else a_stat[:, 24 * p + 6 * cchunk:24 * p + 6 * cchunk + 6],
                     a_x[p][:, 512 * cchunk:512 * (cchunk + 1)],
                     waits=([(dyp[p].h, ldv)] if cchunk == 0 else ()),
                     inc=(vec.h if cchunk == NSQ - 1 else None))
        VEC()
        bld.emit("v", "bn_aggr", a_ag[:, 2 * p:2 * p + 2],
                 a_stat[:, 24 * p:24 * p + 6 * NSQ],
                 waits=[(vec.h, vec.n)], inc=vec.h)
        VEC()
        bld.emit("v", "tensor_scalar", a_eps[:, p:p + 1], a_ag[:, 2 * p + 1:2 * p + 2],
                 EPS, None, ALU.add, waits=[(vec.h, vec.n)], inc=vec.h)
        a_vagg[t] = VEC()

    def a_s2(t):
        p = t % 2
        bld.emit("s", "sqrt", a_istd[:, 2 + p:3 + p], a_eps[:, p:p + 1],
                 waits=[(vec.h, a_vagg[t])], inc=scl.h)
        srec = SCL()
        bld.emit("v", "reciprocal", a_istd[:, p:p + 1], a_istd[:, 2 + p:3 + p],
                 waits=[(scl.h, srec)], inc=vec.h)
        VEC()
        bld.emit("v", "tensor_scalar", a_nmu[:, p:p + 1], a_ag[:, 2 * p:2 * p + 1],
                 a_istd[:, p:p + 1], -1.0, ALU.mult, ALU.mult,
                 waits=[(vec.h, vec.n)], inc=vec.h)
        VEC()
        bld.emit("v", "tensor_scalar", a_ln[p], a_x[p], a_istd[:, p:p + 1],
                 a_nmu[:, p:p + 1], ALU.mult, ALU.add,
                 waits=[(vec.h, vec.n)] +
                       ([(ten.h, a_trdone[p])] if t >= 2 else []), inc=vec.h)
        a_vln[t] = VEC()
        a_xfree[p] = a_vln[t]

    def a_s3(t):
        p = t % 2
        for kcI in range(KC):
            ew = [(sag[((t - 2) // 4) % 8].h, a_stdone[p])] \
                if (kcI == 0 and t >= 2) else []
            xw = [(vec.h, a_vln[t])] if kcI == 0 else []
            if t == 0 and kcI == 0:
                xw.append((dw.h, CONST_DONE))
            tpipeA.go(a_ln[p][:, 128 * kcI:128 * (kcI + 1)],
                      a_tr[p][:, 128 * kcI:128 * (kcI + 1)],
                      extra_waits=xw, evac_waits=ew)
        a_trdone[p] = ten.n
        evd = scl.n
        sg = sag[(t // 4) % 8]
        sg.n += 16
        bld.emit("g", "dma_start", lnx_h[t].transpose([1, 0, 2]),
                 a_tr[p].rearrange("p (kc c) -> p kc c", c=128),
                 waits=[(scl.h, evd)], inc=sg.h, inck=16)
        a_stdone[p] = sg.n

    kgen = qkv_pass("k", WLD_K)
    for step in range(CT + 2):
        if step < CT:
            a_s1(step)
        if 1 <= step <= CT:
            a_s2(step - 1)
        if step >= 2:
            a_s3(step - 2)
    for _ in kgen:
        pass
    for sgx in sag:
        bld.wait("g", sgx.h, sgx.n)
    bld.wait("g", st0.h, st0.n)
    bld.wait("g", st1.h, st1.n)
    barrier()
    for _ in qkv_pass("v"):
        pass
    tpipeCV = TrPipe((6, 7), "v")
    first_cv = True
    for m in range(NH):
        for cc in range(CVW):
            w0 = min(128, NCH - 128 * cc)
            xw = [(vec.h, vec.n)] if first_cv else []
            if first_cv:
                xw.append((ten.h, ten.n))
            first_cv = False
            tpipeCV.go(c_cvft[:, NCH * m + 128 * cc:NCH * m + 128 * cc + w0],
                       g_cv[0:w0,
                            (NH * 128) * cc + 128 * m:(NH * 128) * cc + 128 * (m + 1)],
                       w0=w0, extra_waits=xw)
    bld.wait("g", st0.h, st0.n)
    bld.wait("g", st1.h, st1.n)
    barrier()
    for _ in qkv_pass("q"):
        pass
    bld.wait("g", st0.h, st0.n)
    bld.wait("g", st1.h, st1.n)
    barrier()
    B_ST0, B_ST1 = st0.n, st1.n

    phase_reset()
    e_cm = alloc(QT * NCH, BF16)
    e_wm = alloc(QT * 384, BF16)
    e_k = [alloc(OWN + WINDOW, BF16) for _ in range(2)]
    e_v = [alloc(WT * 128, BF16) for _ in range(2)]
    e_q = [alloc(OWN, BF16) for _ in range(2)]
    e_pc = [alloc(NCH, BF16) for _ in range(4)]
    e_pw = [alloc(384, BF16) for _ in range(4)]
    NPT = CVW + 3
    e_pt = [alloc(NPT * 128, BF16) for _ in range(2)]
    e_D = [alloc(128, BF16) for _ in range(4)]
    e_nm = alloc(16, F32)

    bld.dma("g", e_cm.rearrange("p (t c) -> p t c", c=NCH),
            cmask.transpose([1, 0, 2]), dm2)
    mld = bld.dma("g", e_wm.rearrange("p (t c) -> p t c", c=384),
                  wmask.transpose([1, 0, 2]), dm2)
    bld.dma("g", wobf_h, woT, dw)
    PRE_WO = dw.n
    bld.dma("g", wrtbf_h, wrt, dw)
    PRE_WRT = dw.n
    for e in range(NE):
        bld.dma("g", w1bf_h[e], ws1 if e == 0 else w1[e - 1], dw)
        bld.dma("g", w2bf_h[e], ws2 if e == 0 else w2[e - 1], dw)
    PRECAST_DONE = dw.n

    kv_ld = {}
    for m in range(min(2, NH)):
        p = m % 2
        bld.dma("y", e_k[p], kwin_h[m],
                dyp[p], waits=([(stp[0].h, B_ST0), (stp[1].h, B_ST1)]
                               if m == 0 else ()))
        bld.dma("y", e_v[p], vwin_h[m], dyp[p])
        kv_ld[m] = bld.dma("y", e_q[p], q_hbm[m], dyp[p])

    N_IT = NH * QT
    sc_free = {}
    sw_free = {}
    o_free = {}
    pc_free = {}
    av_done = {}
    nmd_d = {}
    dbl_d = {}
    tr_ev = {}

    def e_scores(i):
        m, t = divmod(i, QT)
        p = m % 2
        tp = i % 2
        wc = PADC + 8 * (t + 1)
        wv = [(dyp[p].h, kv_ld[m])] if t == 0 else []
        if i == 0:
            wv.append((dm2.h, mld))
        if i - 2 in sw_free:
            wv.append((scl.h, sw_free[i - 2]))
        bld.emit("t", "matmul", pbf[tp][:, 0:wc], g_ident,
                 e_cm[:, NCH * t:NCH * t + wc], start=True, stop=False,
                 waits=wv)
        bld.emit("t", "matmul", pbf[tp][:, 0:wc],
                 e_q[p][:, 128 * t:128 * (t + 1)],
                 g_ck[:, NCH * m:NCH * m + wc], start=False, stop=True,
                 inc=ten.h)
        TEN()
        bld.emit("t", "matmul", pbf[2 + tp][:, 0:384], g_ident,
                 e_wm[:, 384 * t:384 * (t + 1)], start=True, stop=False)
        bld.emit("t", "matmul", pbf[2 + tp][:, 0:384],
                 e_q[p][:, 128 * t:128 * (t + 1)],
                 e_k[p][:, 128 * t:128 * t + 384], start=False, stop=True,
                 inc=ten.h)
        sc_free[i] = st = TEN()
        nmd_d[i] = st

    def e_sm1(i):
        m, t = divmod(i, QT)
        tp = i % 2
        nb = 8 * tp
        wc = PADC + 8 * (t + 1)
        swd = sc_free[i]
        wv = [(ten.h, swd)]
        if i - 2 in sw_free:
            wv.append((scl.h, sw_free[i - 2]))
        bld.emit("v", "reduce_max", e_nm[:, nb:nb + 1], pbf[tp][:, 0:wc],
                 axis=AX.X, negate=True, waits=wv)
        bld.emit("v", "reduce_max", e_nm[:, nb + 1:nb + 2],
                 pbf[2 + tp][:, 0:384], axis=AX.X, negate=True, inc=vec.h)
        VEC()
        bld.emit("v", "tensor_tensor", e_nm[:, nb + 2:nb + 3],
                 e_nm[:, nb:nb + 1], e_nm[:, nb + 1:nb + 2], ALU.min,
                 waits=[(vec.h, vec.n)], inc=vec.h)
        nmd_d[i] = VEC()

    def e_sm2(i):
        m, t = divmod(i, QT)
        tp = i % 2
        nb = 8 * tp
        wc = PADC + 8 * (t + 1)
        p4 = i % 4
        ew = [(vec.h, nmd_d[i])]
        if i - 4 in pc_free:
            ew.append((ten.h, pc_free[i - 4]))
        bld.emit("s", "activation", e_pc[p4][:, 0:wc], pbf[tp][:, 0:wc],
                 ACT.Exp, bias=e_nm[:, nb + 2:nb + 3], scale=1.0,
                 accum_out=e_nm[:, nb + 3:nb + 4], waits=ew, inc=scl.h)
        sc_free[i] = SCL()
        bld.emit("s", "activation", e_pw[p4][:, 0:384], pbf[2 + tp][:, 0:384],
                 ACT.Exp, bias=e_nm[:, nb + 2:nb + 3], scale=1.0,
                 accum_out=e_nm[:, nb + 4:nb + 5], inc=scl.h)
        sw_free[i] = SCL()

    def e_sm3(i):
        tp = i % 2
        nb = 8 * tp
        bld.emit("v", "tensor_tensor", e_nm[:, nb + 5:nb + 6],
                 e_nm[:, nb + 3:nb + 4], e_nm[:, nb + 4:nb + 5], ALU.add,
                 waits=[(scl.h, sw_free[i])], inc=vec.h)
        VEC()
        bld.emit("v", "reciprocal", e_nm[:, nb + 6:nb + 7],
                 e_nm[:, nb + 5:nb + 6], waits=[(vec.h, vec.n)], inc=vec.h)
        VEC()
        dw2 = [(vec.h, vec.n)]
        if i - 4 in pc_free:
            dw2.append((ten.h, pc_free[i - 4]))
        bld.emit("v", "tensor_scalar", e_D[i % 4], g_ident,
                 e_nm[:, nb + 6:nb + 7], None, ALU.mult, waits=dw2,
                 inc=vec.h)
        dbl_d[i] = VEC()

    def e_tr(i):
        m, t = divmod(i, QT)
        tp = i % 2
        wc = PADC + 8 * (t + 1)
        nwc = (wc + 127) // 128
        p4 = i % 4
        nt = nwc + 3
        for jj in range(3):
            src = e_pw[p4][:, 128 * jj:128 * (jj + 1)]
            dstp = pbf[4 + tp][:, 128 * (jj + 1):128 * (jj + 2)]
            wv = []
            if jj == 0:
                wv = [(vec.h, dbl_d[i]), (scl.h, sw_free[i])]
                if i - 2 in o_free:
                    wv.append((scl.h, o_free[i - 2]))
                prev = tr_ev.get((i - 2, "b"))
                if prev:
                    wv.append((vec.h, prev[1]))
            bld.emit("t", "matmul", dstp, src, e_D[p4],
                     start=True, stop=True, waits=wv, inc=ten.h)
            trd = TEN()
        bld.emit("v", "tensor_copy", e_pt[tp][:, 128 * nwc:128 * nt],
                 pbf[4 + tp][:, 128:512], waits=[(ten.h, trd)], inc=vec.h)
        tr_ev[(i, "b")] = ("v", VEC())
        for j in range(nwc):
            w0 = min(128, wc - 128 * j)
            src = e_pc[p4][:, 128 * j:128 * j + w0]
            dstp = pbf[6 + j][0:w0, 0:128]
            prev = tr_ev.get((i - 1, j))
            wv = []
            if prev:
                wv.append(((vec.h if prev[0] == "v" else scl.h), prev[1]))
            bld.emit("t", "matmul", dstp, src, e_D[p4],
                     start=True, stop=True, waits=wv, inc=ten.h)
            trd = TEN()
            dst = e_pt[tp][0:w0, 128 * j:128 * (j + 1)]
            if j == 0:
                bld.emit("v", "tensor_copy", dst, dstp,
                         waits=[(ten.h, trd)], inc=vec.h)
                tr_ev[(i, j)] = ("v", VEC())
            else:
                bld.emit("s", "copy", dst, dstp, waits=[(ten.h, trd)],
                         inc=scl.h)
                tr_ev[(i, j)] = ("s", SCL())
        pc_free[i] = ten.n

    def e_av(i):
        m, t = divmod(i, QT)
        p = m % 2
        tp = i % 2
        wc = PADC + 8 * (t + 1)
        nwc = (wc + 127) // 128
        wv = [(vec.h, tr_ev[(i, "b")][1])]
        if i - 2 in o_free:
            wv.append((scl.h, o_free[i - 2]))
        for j in range(3):
            bld.emit("t", "matmul", pbf[4 + tp][:, 0:128],
                     e_v[p][:, 128 * (t + j):128 * (t + j + 1)],
                     e_pt[tp][:, 128 * (nwc + j):128 * (nwc + j + 1)],
                     start=(j == 0), stop=False, waits=(wv if j == 0 else ()))
        for j in range(nwc):
            w0 = min(128, wc - 128 * j)
            ev = tr_ev[(i, j)]
            last = (j == nwc - 1)
            bld.emit("t", "matmul", pbf[4 + tp][:, 0:128],
                     g_cv[0:w0, (NH * 128) * j + 128 * m:(NH * 128) * j + 128 * (m + 1)],
                     e_pt[tp][0:w0, 128 * j:128 * (j + 1)],
                     start=False, stop=last,
                     waits=[((vec.h if ev[0] == "v" else scl.h), ev[1])],
                     inc=(ten.h if last else None))
        av_done[i] = TEN()
        bld.emit("s", "copy", g_o[:, OWN * m + 128 * t:OWN * m + 128 * (t + 1)],
                 pbf[4 + tp][:, 0:128], waits=[(ten.h, av_done[i])],
                 inc=scl.h)
        o_free[i] = SCL()
        if t == QT - 1 and m + 2 < NH:
            bld.dma("y", e_k[p], kwin_h[m + 2], dyp[p],
                    waits=[(ten.h, ten.n)])
            bld.dma("y", e_v[p], vwin_h[m + 2], dyp[p])
            kv_ld[m + 2] = bld.dma("y", e_q[p], q_hbm[m + 2], dyp[p])

    for step in range(N_IT + 3):
        if step < N_IT:
            e_scores(step)
        if 2 <= step <= N_IT + 1:
            e_sm3(step - 2)
        if 1 <= step <= N_IT:
            e_sm1(step - 1)
            e_sm2(step - 1)
        if step >= 3:
            e_tr(step - 3)
            e_av(step - 3)
    barrier()

    phase_reset()
    f_wo = alloc(KC * H, BF16)
    f_xr = [alloc(H, F32) for _ in range(2)]
    f_ln2 = alloc(H, BF16)
    f_ln = [f_ln2, f_ln2]
    f_tr = [alloc(KC * 128, BF16) for _ in range(2)]
    f_wr = alloc(KC * 8, BF16)
    f_stat = alloc(6 * NSQ, F32)
    f_ag = alloc(4, F32)
    f_eps = alloc(2, F32)
    f_istd = alloc(2, F32)
    f_nmu = alloc(2, F32)
    f_lg = alloc(8, F32)
    f_m8 = alloc(8, F32)
    f_nm1 = alloc(2, F32)
    f_den = alloc(2, F32)
    f_sel = alloc(8, F32)

    wold = bld.dma("s", f_wo.rearrange("p (kc c) -> p kc c", c=H),
                   wobf_h.rearrange("(kc p) c -> p kc c", p=128), ds1,
                   waits=[(dw.h, PRE_WO)])
    wrld = bld.dma("s", f_wr.rearrange("p (kc c) -> p kc c", c=8),
                   wrtbf_h.rearrange("(kc p) c -> p kc c", p=128), ds2,
                   waits=[(dw.h, PRE_WRT)])
    x_ld = [0, 0]
    for t in range(min(2, QT)):
        x_ld[t % 2] = bld.dma("y", f_xr[t % 2],
                              x_ctx[PAD + 128 * t:PAD + 128 * (t + 1), :],
                              dyp[t % 2])
    f_psfree = {}
    f_store = [0, 0]
    f_trlast = 0
    f_r1free = [0, 0]
    rt_evd = 0
    tpipeF = TrPipe((6, 7), "s")
    for t in range(QT):
        p = t % 2
        for jg in range(JG):
            bk = jg % 2
            wv = []
            if t == 0 and jg == 0:
                wv.append((ds1.h, wold))
            if bk in f_psfree:
                wv.append((vec.h, f_psfree[bk]))
            for kcI in range(KC):
                bld.emit("t", "matmul", pbf[bk][:, 0:512],
                         g_o[:, OWN * kcI + 128 * t:OWN * kcI + 128 * (t + 1)],
                         f_wo[:, H * kcI + 512 * jg:H * kcI + 512 * (jg + 1)],
                         start=(kcI == 0), stop=(kcI == KC - 1),
                         waits=(wv if kcI == 0 else ()),
                         inc=(ten.h if kcI == KC - 1 else None))
            opd = TEN()
            wv = [(ten.h, opd), (dyp[p].h, x_ld[p])]
            if jg == 0 and t >= 2:
                wv.append((sfp[p].h, f_store[p]))
                wv.append((vec.h, f_r1free[p]))
            bld.emit("v", "tensor_tensor", f_xr[p][:, 512 * jg:512 * (jg + 1)],
                     pbf[bk][:, 0:512], f_xr[p][:, 512 * jg:512 * (jg + 1)],
                     ALU.add, waits=wv, inc=vec.h)
            f_psfree[bk] = VEC()
        r1d = vec.n
        bld.dma("y", r1_h[128 * t:128 * (t + 1), :], f_xr[p], sfp[p],
                waits=[(vec.h, r1d)])
        f_store[p] = sfp[p].n
        for cchunk in range(NSQ):
            bld.emit("v", "bn_stats", f_stat[:, 6 * cchunk:6 * cchunk + 6],
                     f_xr[p][:, 512 * cchunk:512 * (cchunk + 1)],
                     waits=([(vec.h, vec.n)] if cchunk == 0 else ()),
                     inc=(vec.h if cchunk == NSQ - 1 else None))
        VEC()
        bld.emit("v", "bn_aggr", f_ag[:, 0:2], f_stat[:, 0:6 * NSQ],
                 waits=[(vec.h, vec.n)], inc=vec.h)
        VEC()
        bld.emit("v", "tensor_scalar", f_eps[:, 0:1], f_ag[:, 1:2], EPS, None,
                 ALU.add, waits=[(vec.h, vec.n)], inc=vec.h)
        vrec = VEC()
        bld.emit("s", "activation", f_istd[:, 1:2], f_eps[:, 0:1], ACT.Ln,
                 waits=[(vec.h, vrec)], inc=scl.h)
        SCL()
        bld.emit("s", "activation", f_istd[:, 0:1], f_istd[:, 1:2], ACT.Exp,
                 scale=-0.5, waits=[(scl.h, scl.n)], inc=scl.h)
        srec = SCL()
        bld.emit("v", "tensor_scalar", f_nmu[:, 0:1], f_ag[:, 0:1],
                 f_istd[:, 0:1], -1.0, ALU.mult, ALU.mult,
                 waits=[(scl.h, srec)], inc=vec.h)
        VEC()
        bld.emit("v", "tensor_scalar", f_ln[p], f_xr[p], f_istd[:, 0:1],
                 f_nmu[:, 0:1], ALU.mult, ALU.add,
                 waits=[(vec.h, vec.n)] +
                       ([(ten.h, f_trlast)] if t >= 1 else []), inc=vec.h)
        lnd = VEC()
        f_r1free[p] = lnd
        if t + 2 < QT:
            x_ld[p] = bld.dma("y", f_xr[p],
                              x_ctx[PAD + 128 * (t + 2):PAD + 128 * (t + 3), :],
                              dyp[p],
                              waits=[(vec.h, lnd), (sfp[p].h, f_store[p])])
        for kcI in range(KC):
            ew = []
            if kcI == 0 and t >= 2:
                ew.append((sfp[p].h, f_store[p]))
            tpipeF.go(f_ln[p][:, 128 * kcI:128 * (kcI + 1)],
                      f_tr[p][:, 128 * kcI:128 * (kcI + 1)],
                      extra_waits=([(vec.h, lnd)] if kcI == 0 else ()),
                      evac_waits=ew)
        f_trlast = ten.n
        evd = scl.n
        wv = [(scl.h, evd), (ds2.h, wrld)]
        if rt_evd:
            wv.append((vec.h, rt_evd))
        for kcI in range(KC):
            bld.emit("t", "matmul", pbf[4][:, 0:8],
                     f_tr[p][:, 128 * kcI:128 * (kcI + 1)],
                     f_wr[:, 8 * kcI:8 * (kcI + 1)],
                     start=(kcI == 0), stop=(kcI == KC - 1),
                     waits=(wv if kcI == 0 else ()),
                     inc=(ten.h if kcI == KC - 1 else None))
        rtd = TEN()
        bld.emit("v", "tensor_copy", f_lg[:, 0:8], pbf[4][:, 0:8],
                 waits=[(ten.h, rtd)], inc=vec.h)
        rt_evd = VEC()
        bld.emit("v", "max", f_m8[:, 0:8], f_lg[:, 0:8],
                 waits=[(vec.h, vec.n)], inc=vec.h)
        VEC()
        bld.emit("v", "tensor_scalar", f_nm1[:, 0:1], f_m8[:, 0:1], -1.0, None,
                 ALU.mult, waits=[(vec.h, vec.n)])
        bld.emit("v", "tensor_scalar", f_sel[:, 0:8], f_lg[:, 0:8],
                 f_m8[:, 1:2], None, ALU.is_ge, inc=vec.h)
        seld = VEC()
        bld.emit("s", "activation", f_den[:, 0:1], f_m8[:, 1:2], ACT.Exp,
                 bias=f_nm1[:, 0:1], scale=1.0, waits=[(vec.h, seld)])
        bld.emit("s", "activation", f_lg[:, 0:8], f_lg[:, 0:8], ACT.Exp,
                 bias=f_nm1[:, 0:1], scale=1.0, inc=scl.h)
        expd = SCL()
        bld.emit("v", "tensor_scalar", f_den[:, 1:2], f_den[:, 0:1], 1.0, None,
                 ALU.add, waits=[(scl.h, expd)], inc=vec.h)
        VEC()
        bld.emit("v", "reciprocal", f_den[:, 0:1], f_den[:, 1:2],
                 waits=[(vec.h, vec.n)], inc=vec.h)
        VEC()
        bld.emit("v", "tensor_tensor", f_sel[:, 0:8], f_sel[:, 0:8],
                 f_lg[:, 0:8], ALU.mult, waits=[(vec.h, vec.n)], inc=vec.h)
        VEC()
        bld.emit("v", "tensor_scalar", g_comb[:, 8 * t:8 * (t + 1)],
                 f_sel[:, 0:8], f_den[:, 0:1], None, ALU.mult,
                 waits=[(vec.h, vec.n)], inc=vec.h)
        VEC()
        bld.dma("y", lnx2_h[t].transpose([1, 0, 2]),
                f_tr[p].rearrange("p (kc c) -> p kc c", c=128),
                sfp[p], waits=[(scl.h, evd), (ten.h, rtd)])
        f_store[p] = sfp[p].n
    bld.wait("y", sf0.h, sf0.n)
    bld.wait("y", sf1.h, sf1.n)
    barrier()
    F_ST0, F_ST1 = sf0.n, sf1.n

    phase_reset(reuse_o=True)
    g_w1 = alloc(KC * I, BF16)
    g_w2 = alloc(ICH * H, BF16)
    g_h = alloc(ICH * 512, BF16)
    g_acc = alloc(4 * H, F32)
    g_lnx2_b = [alloc(KC * 512, BF16) for _ in range(2)]
    g_r1 = [alloc(H, F32) for _ in range(2)]

    w1_ld = bld.dma("s", g_w1.rearrange("p (kc c) -> p kc c", c=I),
                    w1bf_h[0].rearrange("(kc p) c -> p kc c", p=128), ds1,
                    waits=[(dw.h, PRECAST_DONE)])
    w2_ld = bld.dma("s", g_w2.rearrange("p (ic c) -> p ic c", c=H),
                    w2bf_h[0].rearrange("(ic p) c -> p ic c", p=128), ds2)
    h_done = {}
    yst = [0, 0, 0, 0]
    od_g = {}
    l2_tgt = {}
    pass_hall = []

    def emit_l2(ps2):
        l2v = g_lnx2_b[ps2 % 2].rearrange("p (kc g c) -> p kc g c",
                                          g=4, c=128)
        for j in range(4):
            l2_tgt[ps2] = bld.dma(
                "s", l2v[:, :, j, :],
                lnx2_h[4 * ps2 + j].transpose([1, 0, 2]), dl0,
                waits=([(scl.h, pass_hall[ps2 - 2])]
                       if (j == 0 and ps2 >= 2) else ()))

    emit_l2(0)
    for pss in range(NPASS):
        g_lnx2 = g_lnx2_b[pss % 2]
        l2 = l2_tgt[pss]
        acc_part = {}
        acc_done = {}
        for e in range(NE):
            for ic in range(ICH):
                bk = ic % 2
                wv = []
                if ic == 0:
                    wv.append((ds1.h, w1_ld))
                    wv.append((dl0.h, l2))
                    if e >= 1:
                        wv.append((vec.h, acc_done[e - 1]))
                if bk in h_done:
                    wv.append((scl.h, h_done[bk]))
                for kcI in range(KC):
                    bld.emit("t", "matmul", pbf[bk][:, 0:512],
                             g_w1[:, I * kcI + 128 * ic:I * kcI + 128 * (ic + 1)],
                             g_lnx2[:, 512 * kcI:512 * (kcI + 1)],
                             start=(kcI == 0), stop=(kcI == KC - 1),
                             waits=(wv if kcI == 0 else ()),
                             inc=(ten.h if kcI == KC - 1 else None))
                w1d = TEN()
                bld.emit("s", "activation", g_h[:, 512 * ic:512 * (ic + 1)],
                         pbf[bk][:, 0:512], ACT.Silu, waits=[(ten.h, w1d)],
                         inc=scl.h)
                h_done[bk] = SCL()
            hall = scl.n
            if not (pss == NPASS - 1 and e == NE - 1):
                w1_ld = bld.dma("s", g_w1.rearrange("p (kc c) -> p kc c", c=I),
                                w1bf_h[(e + 1) % NE].rearrange("(kc p) c -> p kc c", p=128),
                                ds1)
            for tt in range(4):
                gt = 4 * pss + tt
                for jg in range(JG):
                    bk = 2 + ((tt * JG + jg) % 2)
                    wv = [(scl.h, hall)]
                    if tt == 0 and jg == 0:
                        wv.append((ds2.h, w2_ld))
                    if bk in acc_part:
                        wv.append((vec.h, acc_part[bk]))
                    for ic in range(ICH):
                        bld.emit("t", "matmul", pbf[bk][:, 0:512],
                                 g_h[:, 512 * ic + 128 * tt:512 * ic + 128 * (tt + 1)],
                                 g_w2[:, H * ic + 512 * jg:H * ic + 512 * (jg + 1)],
                                 start=(ic == 0), stop=(ic == ICH - 1),
                                 waits=(wv if ic == 0 else ()),
                                 inc=(ten.h if ic == ICH - 1 else None))
                    w2d = TEN()
                    accsl = g_acc[:, H * tt + 512 * jg:H * tt + 512 * (jg + 1)]
                    if e == 0:
                        aw = [(ten.h, w2d)]
                        if pss > 0 and jg == 0:
                            aw.append((sy.h, yst[tt]))
                        bld.emit("v", "tensor_copy", accsl, pbf[bk][:, 0:512],
                                 waits=aw, inc=vec.h)
                    else:
                        bld.emit("v", "scalar_tensor_tensor", accsl,
                                 pbf[bk][:, 0:512],
                                 g_comb[:, 8 * gt + (e - 1):8 * gt + e], accsl,
                                 ALU.mult, ALU.add, waits=[(ten.h, w2d)],
                                 inc=vec.h)
                    acc_part[bk] = VEC()
                if e == NE - 1:
                    p2 = tt % 2
                    r1w = [(scl.h, hall)]
                    if od_g.get(p2):
                        r1w.append((vec.h, od_g[p2]))
                    r1v = bld.dma("y", g_r1[p2],
                                  r1_h[512 * pss + 128 * tt:
                                       512 * pss + 128 * (tt + 1), :],
                                  dyp[p2], waits=r1w)
                    ow = [(dyp[p2].h, r1v),
                          (vec.h, acc_part[2 + (tt * JG + JG - 1) % 2])]
                    bld.emit("v", "tensor_tensor", g_acc[:, H * tt:H * (tt + 1)],
                             g_acc[:, H * tt:H * (tt + 1)], g_r1[p2], ALU.add,
                             waits=ow, inc=vec.h)
                    od = VEC()
                    od_g[p2] = od
                    yst[tt] = bld.dma("g",
                                      y[512 * pss + 128 * tt:512 * pss + 128 * (tt + 1), :],
                                      g_acc[:, H * tt:H * (tt + 1)], sy,
                                      waits=[(vec.h, od)])
            acc_done[e] = vec.n
            if e == NE - 1:
                pass_hall.append(hall)
            if not (pss == NPASS - 1 and e == NE - 1):
                w2_ld = bld.dma("s", g_w2.rearrange("p (ic c) -> p ic c", c=H),
                                w2bf_h[(e + 1) % NE].rearrange("(ic p) c -> p ic c", p=128),
                                ds2, waits=[(ten.h, ten.n)])
            if e == 0 and pss + 1 < NPASS:
                emit_l2(pss + 1)
    bld.wait("g", sy.h, sy.n)
    bld.wait("y", sy.h, sy.n)

    import contextlib
    with contextlib.ExitStack() as cx:
        block = cx.enter_context(nc.Block())
        P = bld.P

        @block.tensor
        def _(t):
            for f in P["t"]:
                f(t)

        @block.vector
        def _(v):
            for f in P["v"]:
                f(v)

        @block.scalar
        def _(s):
            for f in P["s"]:
                f(s)

        @block.gpsimd
        def _(g):
            for f in P["g"]:
                f(g)

        @block.sync
        def _(yq):
            for f in P["y"]:
                f(yq)

    for cm in reversed(semctx):
        cm.__exit__(None, None, None)
    for cm in reversed(pb_ctx):
        cm.__exit__(None, None, None)
    return nc


def rope_tables(pos):
    inv = 1.0 / (10000.0 ** (np.arange(0, 128, 2, dtype=np.float32) / 128))
    inv = inv.astype(np.float32)
    f = (pos.astype(np.float32)[:, None] * inv[None, :]).astype(np.float32)
    fc = np.exp(1j * f).astype(np.complex64)
    c = fc.real.T
    sn = fc.imag.T
    C = np.repeat(c, 2, axis=0)
    S3 = np.empty((128, len(pos)), dtype=np.float32)
    S3[0::2] = sn
    S3[1::2] = -sn
    return np.ascontiguousarray(C), np.ascontiguousarray(S3)


def make_core_inputs(cfg, core, full):
    d = derive(cfg)
    H = cfg["H"]
    OWN, WINDOW = cfg["OWN"], cfg["WINDOW"]
    CTX, PAD, NCH, QT = d["CTX"], d["PAD"], d["NCH"], d["QT"]
    b = core // 2
    half = core % 2
    OS = half * OWN
    x = np.asarray(full["L0_attn_x"][b], dtype=np.float32)
    xc = np.zeros((CTX, H), dtype=np.float32)
    g0 = OS - PAD
    lo = max(0, -g0)
    xc[lo:] = x[g0 + lo:g0 + CTX]
    pos = np.arange(CTX, dtype=np.int64) + g0
    posc = np.where(pos < 0, 0, pos)
    kc_, ks_ = rope_tables(posc)
    qc_, qs_ = rope_tables(pos[PAD:])
    lq = PAD + np.arange(OWN)
    cm = np.empty((QT, 128, NCH), dtype=np.float32)
    wm = np.empty((QT, 128, 384), dtype=np.float32)
    delta = (PAD - OS) // 16
    for t in range(QT):
        lqt = lq[128 * t:128 * (t + 1)][:, None]
        ch = np.arange(NCH)[None, :]
        vis = (ch < (lqt + 1) // 16) & (ch >= delta)
        cm[t] = np.where(vis, 0.0, NEG)
        lk = (PAD - WINDOW + 128 * t) + np.arange(384)[None, :]
        vis = (lk <= lqt) & (lk > lqt - WINDOW) & (lk >= PAD - OS)
        wm[t] = np.where(vis, 0.0, NEG)
    return {
        "x_ctx": xc,
        "wqT": np.ascontiguousarray(np.asarray(full["L0_attn_wq"]).T),
        "wkT": np.ascontiguousarray(np.asarray(full["L0_attn_wk"]).T),
        "wvT": np.ascontiguousarray(np.asarray(full["L0_attn_wv"]).T),
        "woT": np.ascontiguousarray(np.asarray(full["L0_attn_wo"]).T),
        "ws1": np.asarray(full["L0_ffn_w_shared1"], dtype=np.float32),
        "ws2": np.asarray(full["L0_ffn_w_shared2"], dtype=np.float32),
        "w1": np.asarray(full["L0_ffn_w1"], dtype=np.float32),
        "w2": np.asarray(full["L0_ffn_w2"], dtype=np.float32),
        "wrt": np.asarray(full["L0_ffn_w_router"], dtype=np.float32),
        "kcos": kc_, "ks3": ks_, "qcos": qc_, "qs3": qs_,
        "cmask": cm, "wmask": wm,
        "identin": np.eye(128, dtype=np.float32),
        "pswapin": np.ascontiguousarray(
            np.eye(128, dtype=np.float32)[np.arange(128) ^ 1]),
    }


def kernel(**inputs):
    cfg = cfg_full()
    nc = build_program(cfg)
    in_maps = [make_core_inputs(cfg, c, inputs) for c in range(cfg["NCORES"])]
    res = run_bass_kernel_spmd(nc, in_maps, core_ids=list(range(cfg["NCORES"])))
    B, S, H = cfg["B"], cfg["S"], cfg["H"]
    out = np.empty((B, S, H), dtype=np.float32)
    for c in range(cfg["NCORES"]):
        b, half = c // 2, c % 2
        out[b, half * cfg["OWN"]:(half + 1) * cfg["OWN"]] = res.results[c]["y"]
    return out
